# version 14
# speedup vs baseline: 1.1694x; 1.1055x over previous
import contextlib
import numpy as np
import ml_dtypes
import concourse.bass as bass
import concourse.mybir as mybir
from concourse.bass_utils import run_bass_kernel_spmd

F32 = mybir.dt.float32
BF16 = mybir.dt.bfloat16
U8 = mybir.dt.uint8
ALU = mybir.AluOpType
ACTF = mybir.ActivationFunctionType
AX = mybir.AxisListType

D = 1024
NCK = D // 128
NH = 16
DH = 64
FF = 4096
FI = 2816
NFI = FI // 128


class Sem:

    def __init__(self, stack, nc, name):
        self.h = stack.enter_context(nc.semaphore(name))
        self.n = 0

    def inc(self, inst, k=1):
        inst.then_inc(self.h, k)
        self.n += k
        return inst


class Stream:

    def __init__(self, stack, nc, name, nbuf=2):
        self.sems = [Sem(stack, nc, f"{name}{i}") for i in range(nbuf)]
        self.nbuf = nbuf
        self.targets = {}

    def issue(self, inst, idx):
        s = self.sems[idx % self.nbuf]
        s.inc(inst, 16)
        self.targets[idx] = s.n

    def wait(self, eng, idx):
        eng.wait_ge(self.sems[idx % self.nbuf].h, self.targets[idx])


def ap2(tensor, offset, dims):
    return bass.AP(tensor=tensor, offset=offset, ap=[list(d) for d in dims])


class MM:

    def __init__(self):
        self.tmpl = None
        self.key = None
        self.n = 0

    def __call__(self, tensor, out, lhsT, rhs, start, stop, key=None):
        if key is not None and key == self.key and self.tmpl is not None:
            t = self.tmpl
            rhs_l = tensor.lower_ap_or_imm(rhs)
            out_l = tensor.lower_ap_or_imm(out)
            self.n += 1
            repl = mybir.InstMatmult(
                replication_resolution=t.replication_resolution,
                replication_shift_amnt=t.replication_shift_amnt,
                replication_num_rows=t.replication_num_rows,
                start_tensor_calc=start, stop_tensor_calc=stop,
                is_transpose=t.is_transpose, ldweights=False,
                name=f"{t.name}_r{self.n}", engine=t.engine, debug=t.debug,
                ins=[rhs_l, t.ins[1]], outs=[out_l])
            return tensor.add_instruction(repl)
        i = tensor.matmul(out, lhsT, rhs, start=start, stop=stop)
        self.tmpl = i.ins if key is not None else None
        self.key = key
        return i


def build(T=2048, dbg=(), upto="full"):
    NT = T // 128
    QB = min(1024, T)
    NQB = T // QB
    NJH = max(1, T // 1024)
    nc = bass.Bass("TRN2", target_bir_lowering=False)

    x_d = nc.dram_tensor("x", [T, D], F32, kind="ExternalInput").ap()
    y_d = nc.dram_tensor("y", [T, D], F32, kind="ExternalOutput").ap()
    wqkv_d = nc.dram_tensor("wqkv_t", [24, 8, 128, 128], F32, kind="ExternalInput").ap()
    wproj_d = nc.dram_tensor("wproj_t", [8, 8, 128, 128], F32, kind="ExternalInput").ap()
    k0_d = nc.dram_tensor("k0_t", [32, 128, 16, 128], U8, kind="ExternalInput").ap()
    k1_d = nc.dram_tensor("k1_t", [32, 128, 32, 128], U8, kind="ExternalInput").ap()
    k2_d = nc.dram_tensor("k2_t", [8, 128, 32, 128], U8, kind="ExternalInput").ap()
    k0bf_d = nc.dram_tensor("k0bf_scr", [32, 128, 16, 128], BF16, kind="Internal").ap()
    k1bf_d = nc.dram_tensor("k1bf_scr", [32, 128, 32, 128], BF16, kind="Internal").ap()
    k2bf_d = nc.dram_tensor("k2bf_scr", [8, 128, 32, 128], BF16, kind="Internal").ap()
    s0_d = nc.dram_tensor("s0", [1, 1], F32, kind="ExternalInput").ap()
    s1_d = nc.dram_tensor("s1", [1, 1], F32, kind="ExternalInput").ap()
    s2_d = nc.dram_tensor("s2", [1, 1], F32, kind="ExternalInput").ap()
    wr_d = nc.dram_tensor("wr", [D, D], F32, kind="ExternalInput").ap()
    wk_d = nc.dram_tensor("wk", [D, D], F32, kind="ExternalInput").ap()
    wv_d = nc.dram_tensor("wv", [D, D], F32, kind="ExternalInput").ap()
    wo_d = nc.dram_tensor("wo", [D, D], F32, kind="ExternalInput").ap()
    decay_d = nc.dram_tensor("decay8", [8, 128], F32, kind="ExternalInput").ap()
    lnw_d = nc.dram_tensor("lnw8", [8, 128], F32, kind="ExternalInput").ap()
    wg_d = nc.dram_tensor("wg", [FI, D], F32, kind="ExternalInput").ap()
    wu_d = nc.dram_tensor("wu", [FI, D], F32, kind="ExternalInput").ap()
    wd_d = nc.dram_tensor("wd", [D, FI], F32, kind="ExternalInput").ap()
    identf_d = nc.dram_tensor("identf", [128, 128], F32, kind="ExternalInput").ap()
    identb_d = nc.dram_tensor("identb", [128, 128], BF16, kind="ExternalInput").ap()
    iota_d = nc.dram_tensor("iota", [128, T], F32, kind="ExternalInput").ap()
    masku_d = nc.dram_tensor("masku", [128, 128], F32, kind="ExternalInput").ap()
    blk2_d = nc.dram_tensor("blk2", [128, 2], BF16, kind="ExternalInput").ap()
    sel2_d = nc.dram_tensor("sel2", [2, 128], F32, kind="ExternalInput").ap()

    dbg_out = {}

    def dbg_reg(name, shape, dt):
        if name in dbg:
            dbg_out[name] = nc.dram_tensor("dbg_" + name, shape, dt,
                                           kind="ExternalOutput").ap()
        return dbg_out.get(name)

    dbg_reg("xn_hi", [128, NCK * T], BF16)
    dbg_reg("xn_lo", [128, NCK * T], BF16)

    def scratch(name, shape, dt):
        if ("dbg_" + name) if False else (name in dbg):
            return nc.dram_tensor("dbg_" + name, shape, dt, kind="ExternalOutput").ap()
        return nc.dram_tensor(name + "_scr", shape, dt, kind="Internal").ap()

    o_scr = scratch("o_l0", [NCK, 128, T], F32)
    s_scr = scratch("s_l0", [NCK, 128, T], F32)

    st = contextlib.ExitStack()
    mm = MM()
    with st:
        all_streams = []

        def mkstream(name, nbuf=2):
            s = Stream(st, nc, name, nbuf)
            all_streams.append(s)
            return s

        sc_ = Sem(st, nc, "sc")
        stx = mkstream("stx")
        so_ = Sem(st, nc, "so")
        sp = Sem(st, nc, "sp")
        sv = Sem(st, nc, "sv")
        sa = Sem(st, nc, "sa")
        sq = Sem(st, nc, "sq")
        sw = mkstream("sw")
        sgd = mkstream("sgd")
        srt = mkstream("srt")
        sso = mkstream("sso")
        srs = mkstream("srs")
        sot = mkstream("sot")
        sst = mkstream("sst", nbuf=4)
        srg = mkstream("srg")

        PS = [st.enter_context(nc.psum_tensor(f"PS{i}", [128, 1024], F32))
              for i in range(4)]
        ps_free = [None, None, None, None]

        def ps_wait(eng, i):
            if ps_free[i] is not None:
                eng.wait_ge(ps_free[i][0].h, ps_free[i][1])

        ident = st.enter_context(nc.sbuf_tensor("identf_s", [128, 128], F32))
        identb = st.enter_context(nc.sbuf_tensor("identb_s", [128, 128], BF16))
        masku = st.enter_context(nc.sbuf_tensor("masku_s", [128, 128], F32))
        blk2 = st.enter_context(nc.sbuf_tensor("blk2_s", [128, 2], BF16))
        sel2 = st.enter_context(nc.sbuf_tensor("sel2_s", [128, 128], F32))
        iota = st.enter_context(nc.sbuf_tensor("iota_s", [128, T], F32))
        onesf = st.enter_context(nc.sbuf_tensor("onesf", [128, 1], F32))
        onesb = st.enter_context(nc.sbuf_tensor("onesb", [128, 1], BF16))
        ones1r = st.enter_context(nc.sbuf_tensor("ones1r", [128, 128], F32))
        scal = st.enter_context(nc.sbuf_tensor("scal", [128, 8], F32))
        decay_c = st.enter_context(nc.sbuf_tensor("decay_c", [128, 8], F32))
        lnw_c = st.enter_context(nc.sbuf_tensor("lnw_c", [128, 8], F32))
        xn_hi = st.enter_context(nc.sbuf_tensor("xn_hi", [128, NCK * T], BF16))
        xn_lo = st.enter_context(nc.sbuf_tensor("xn_lo", [128, NCK * T], BF16))
        mstats = st.enter_context(nc.sbuf_tensor("mstats", [128, 2 * NT], F32))

        block = st.enter_context(nc.Block())

        @block.sync
        def _(sync):
            for dst, src in [(ident[:], identf_d), (identb[:], identb_d),
                             (masku[:], masku_d), (blk2[:], blk2_d),
                             (sel2[0:2, :], sel2_d), (iota[:], iota_d)]:
                sc_.inc(sync.dma_start(out=dst, in_=src), 16)
            sc_.inc(sync.dma_start(
                out=ap2(decay_c, 0, [[8, 128], [1, 8]]),
                in_=ap2(decay_d.tensor, 0, [[1, 128], [128, 8]]),
                allow_slow_non_contiguous=True), 16)
            sc_.inc(sync.dma_start(
                out=ap2(lnw_c, 0, [[8, 128], [1, 8]]),
                in_=ap2(lnw_d.tensor, 0, [[1, 128], [128, 8]]),
                allow_slow_non_contiguous=True), 16)
            sc_.inc(sync.dma_start(out=scal[0:1, 0:1], in_=s0_d), 16)
            sc_.inc(sync.dma_start(out=scal[0:1, 1:2], in_=s1_d), 16)
            sc_.inc(sync.dma_start(out=scal[0:1, 2:3], in_=s2_d), 16)
        p0_ld = sc_.n

        @block.vector
        def _(vector):
            vector.memset(onesf[:], 1.0)
            vector.memset(onesb[:], 1.0)
            sv.inc(vector.memset(ones1r[0:1, :], 1.0))
        p0_ones = sv.n

        @block.tensor
        def _(tensor):
            tensor.wait_ge(sc_.h, p0_ld)
            tensor.wait_ge(sv.h, p0_ones)
            sp.inc(tensor.matmul(PS[0][:, 0:3], ones1r[0:1, :], scal[0:1, 0:3],
                                 start=True, stop=True))
        p0_mm = sp.n

        @block.vector
        def _(vector):
            vector.wait_ge(sp.h, p0_mm)
            vector.tensor_scalar_mul(scal[:, 0:2], PS[0][:, 0:2], 4.0)
            sv.inc(vector.tensor_scalar_mul(scal[:, 2:3], PS[0][:, 2:3], 2.0))
        ps_free[0] = (sv, sv.n)

        sck0 = mkstream("sck0")
        sck1 = mkstream("sck1")
        sck2 = mkstream("sck2")

        @block.gpsimd
        def _(gpsimd):
            for fc in range(32):
                off = fc * 16 * 128 * 128
                sck0.issue(gpsimd.dma_start(
                    out=ap2(k0bf_d.tensor, off, [[16384, 16], [1, 16384]]),
                    in_=ap2(k0_d.tensor, off, [[16384, 16], [1, 16384]])), fc)
            for fc2 in range(32):
                off = fc2 * 32 * 128 * 128
                sck1.issue(gpsimd.dma_start(
                    out=ap2(k1bf_d.tensor, off, [[16384, 32], [1, 16384]]),
                    in_=ap2(k1_d.tensor, off, [[16384, 32], [1, 16384]])), fc2)
            for fc3 in range(8):
                off = fc3 * 32 * 128 * 128
                sck2.issue(gpsimd.dma_start(
                    out=ap2(k2bf_d.tensor, off, [[16384, 32], [1, 16384]]),
                    in_=ap2(k2_d.tensor, off, [[16384, 32], [1, 16384]])), fc3)

        def wait_stream_all(eng, strm):
            for s in strm.sems:
                if s.n:
                    eng.wait_ge(s.h, s.n)

        def emit_ln_phase(src_dram, src_is_y, stats_store, tt_base=0, entry=None):
            with contextlib.ExitStack() as lh:
                xtile = lh.enter_context(nc.sbuf_tensor(f"xtile{tt_base}", [128, 2 * D], F32))
                lnst = lh.enter_context(nc.sbuf_tensor(f"lnst{tt_base}", [128, 12], F32))
                lnmv = lh.enter_context(nc.sbuf_tensor(f"lnmv{tt_base}", [128, 2], F32))
                xntmp = lh.enter_context(nc.sbuf_tensor(f"xntmp{tt_base}", [128, 2 * D], F32))
                tile_done = {}
                tr_done = {}
                for tt in range(NT):
                    buf = tt % 2
                    xv = xtile[:, buf * D:(buf + 1) * D]
                    xnv = xntmp[:, buf * D:(buf + 1) * D]

                    @block.sync
                    def _(sync):
                        if tt == 0 and entry is not None:
                            entry(sync)
                        if tt >= 2:
                            sync.wait_ge(sv.h, tile_done[tt - 2])
                        stx.issue(sync.dma_start(
                            out=xv, in_=src_dram[tt * 128:(tt + 1) * 128, :]),
                            tt_base + tt)

                    @block.vector
                    def _(vector):
                        stx.wait(vector, tt_base + tt)
                        if tt == 0 and entry is not None:
                            entry(vector)
                        if tt >= 1:
                            vector.wait_ge(sv.h, tile_done[tt - 1])
                        vector.bn_stats(lnst[:, 0:6], xv[:, 0:512])
                        sv.inc(vector.bn_stats(lnst[:, 6:12], xv[:, 512:1024]))
                    bn_m = sv.n

                    @block.vector
                    def _(vector):
                        vector.wait_ge(sv.h, bn_m)
                        sv.inc(vector.bn_aggr(lnmv[:], lnst[:]))
                    ag_m = sv.n

                    @block.vector
                    def _(vector):
                        vector.wait_ge(sv.h, ag_m)
                        sv.inc(vector.tensor_scalar_add(lnmv[:, 1:2], lnmv[:, 1:2], 1e-5))
                    var_m = sv.n

                    @block.scalar
                    def _(scalar):
                        scalar.wait_ge(sv.h, var_m)
                        sa.inc(scalar.activation(lnmv[:, 1:2], lnmv[:, 1:2], ACTF.Sqrt,
                                                 bias=0.0, scale=1.0))
                    sq_m = sa.n

                    @block.vector
                    def _(vector):
                        vector.wait_ge(sa.h, sq_m)
                        sv.inc(vector.reciprocal(lnmv[:, 1:2], lnmv[:, 1:2]))
                    rs_m = sv.n

                    @block.vector
                    def _(vector):
                        vector.wait_ge(sv.h, rs_m)
                        if tt >= 2:
                            vector.wait_ge(sp.h, tr_done[tt - 2])
                        if stats_store:
                            vector.tensor_copy(mstats[:, 2 * tt:2 * tt + 2], lnmv[:])
                        sv.inc(vector.tensor_scalar(xnv, xv, lnmv[:, 0:1], lnmv[:, 1:2],
                                                    ALU.subtract, ALU.mult))
                    xn_m = sv.n

                    a0, a1 = (0, 1) if tt % 2 == 0 else (2, 3)

                    @block.tensor
                    def _(tensor):
                        tensor.wait_ge(sv.h, xn_m)
                        if tt == 0 and entry is not None:
                            entry(tensor)
                        ps_wait(tensor, a0)
                        ps_wait(tensor, a1)
                        for c in range(NCK):
                            pa = PS[a0 if c < 4 else a1]
                            sp.inc(tensor.transpose(
                                pa[:, (c % 4) * 128:(c % 4) * 128 + 128],
                                xnv[:, c * 128:(c + 1) * 128], ident[:]))
                    tr_done[tt] = sp.n
                    tr_m = sp.n

                    @block.vector
                    def _(vector):
                        vector.wait_ge(sp.h, tr_m)
                        for c in range(NCK):
                            pa = PS[a0 if c < 4 else a1]
                            pav = pa[:, (c % 4) * 128:(c % 4) * 128 + 128]
                            dst = slice(c * T + tt * 128, c * T + (tt + 1) * 128)
                            inst = vector.tensor_copy(xn_hi[:, dst], pav)
                        sv.inc(inst)
                    hi_m = sv.n

                    @block.vector
                    def _(vector):
                        vector.wait_ge(sv.h, hi_m)
                        for c in range(NCK):
                            pa = PS[a0 if c < 4 else a1]
                            pav = pa[:, (c % 4) * 128:(c % 4) * 128 + 128]
                            dst = slice(c * T + tt * 128, c * T + (tt + 1) * 128)
                            inst = vector.scalar_tensor_tensor(
                                xn_lo[:, dst], pav, 1.0, xn_hi[:, dst],
                                ALU.mult, ALU.subtract)
                        sv.inc(inst)
                    tile_done[tt] = sv.n
                    ps_free[a0] = (sv, sv.n)
                    ps_free[a1] = (sv, sv.n)
            return sv.n

        ln1_done = emit_ln_phase(x_d, src_is_y=False, stats_store=False)

        if "xn_hi" in dbg_out or "xn_lo" in dbg_out:
            @block.sync
            def _(sync):
                sync.wait_ge(sv.h, ln1_done)
                if "xn_hi" in dbg_out:
                    so_.inc(sync.dma_start(out=dbg_out["xn_hi"], in_=xn_hi[:]), 16)
                if "xn_lo" in dbg_out:
                    so_.inc(sync.dma_start(out=dbg_out["xn_lo"], in_=xn_lo[:]), 16)

        def emit_l0_attn(rsum):
            ph = contextlib.ExitStack()
            with ph:
                qbuf = ph.enter_context(nc.sbuf_tensor("qbuf", [128, 2 * T], F32))
                kbuf = ph.enter_context(nc.sbuf_tensor("kbuf", [128, 2 * T], F32))
                vaug = ph.enter_context(nc.sbuf_tensor("vaug", [128, NT * 260], F32))
                wstg = ph.enter_context(nc.sbuf_tensor("wstg", [128, 2048], F32))
                whi = ph.enter_context(nc.sbuf_tensor("whi", [128, 2048], BF16))
                wlo = ph.enter_context(nc.sbuf_tensor("wlo", [128, 2048], BF16))
                ptb = ph.enter_context(nc.sbuf_tensor("ptb", [128, 2 * QB], F32))
                q2b = ph.enter_context(nc.sbuf_tensor("q2b", [128, 2 * T], BF16))
                nrm = ph.enter_context(nc.sbuf_tensor("nrm", [128, 32], F32))
                nrow = ph.enter_context(nc.sbuf_tensor("nrow", [128, 8], F32))
                mh_neg = ph.enter_context(nc.sbuf_tensor("mh_neg", [128, NH], F32))
                ostg = ph.enter_context(nc.sbuf_tensor("ostg", [128, 2 * QB], F32))

                def wtile_in(dram, fc, kstride=128):
                    base = fc * (8 * 128 * 128)
                    return ap2(dram.tensor, base, [[128, 128], [16384, 8], [1, 128]])

                wstg_free = [None, None]
                whl_free = [None, None]
                swi = [0]
                flash_done = [None]

                for g in range(4):
                    for qi, fc in enumerate([2 * g, 2 * g + 1, 8 + 2 * g, 9 + 2 * g]):
                        cidx = qi % 2
                        dstbuf = qbuf if qi < 2 else kbuf
                        slot = qi % 2
                        idx = swi[0]
                        swi[0] += 1

                        @block.sync
                        def _(sync):
                            if idx == 0:
                                sync.wait_ge(sv.h, ln1_done)
                            if wstg_free[slot] is not None:
                                sync.wait_ge(wstg_free[slot][0].h, wstg_free[slot][1])
                            dst = ap2(wstg, slot * 1024, [[2048, 128], [128, 8], [1, 128]])
                            sw.issue(sync.dma_start(out=dst, in_=wtile_in(wqkv_d, fc)), idx)

                        @block.vector
                        def _(vector):
                            sw.wait(vector, idx)
                            if whl_free[slot] is not None:
                                vector.wait_ge(whl_free[slot][0].h, whl_free[slot][1])
                            sv.inc(vector.tensor_copy(
                                whi[:, slot * 1024:(slot + 1) * 1024],
                                wstg[:, slot * 1024:(slot + 1) * 1024]))
                        hi_m = sv.n

                        @block.vector
                        def _(vector):
                            vector.wait_ge(sv.h, hi_m)
                            sv.inc(vector.scalar_tensor_tensor(
                                wlo[:, slot * 1024:(slot + 1) * 1024],
                                wstg[:, slot * 1024:(slot + 1) * 1024], 1.0,
                                whi[:, slot * 1024:(slot + 1) * 1024],
                                ALU.mult, ALU.subtract))
                        w_m = sv.n
                        wstg_free[slot] = (sv, w_m)

                        aren = (0, 1) if qi % 2 == 0 else (2, 3)

                        @block.tensor
                        def _(tensor):
                            tensor.wait_ge(sv.h, max(w_m, ln1_done))
                            if flash_done[0] is not None:
                                tensor.wait_ge(flash_done[0][0].h, flash_done[0][1])
                            ps_wait(tensor, aren[0])
                            ps_wait(tensor, aren[1])
                            for kc in range(NCK):
                                wh = whi[:, slot * 1024 + kc * 128: slot * 1024 + kc * 128 + 128]
                                wl = wlo[:, slot * 1024 + kc * 128: slot * 1024 + kc * 128 + 128]
                                for pi, (wt, xt) in enumerate(
                                        ((wh, xn_hi), (wh, xn_lo), (wl, xn_hi))):
                                    ky = ("qkv", idx, kc, pi == 2)
                                    for j in range((T + 511) // 512):
                                        w = min(512, T - j * 512)
                                        pa = PS[aren[j // 2]] if T > 1024 else PS[aren[0]]
                                        jj = (j % 2) if T > 1024 else j
                                        sp.inc(mm(tensor,
                                            pa[:, jj * 512:jj * 512 + w],
                                            wt, xt[:, kc * T + j * 512: kc * T + j * 512 + w],
                                            start=(kc == 0 and pi == 0),
                                            stop=(kc == NCK - 1 and pi == 2), key=ky))
                        mm_m = sp.n
                        whl_free[slot] = (sp, mm_m)

                        @block.vector
                        def _(vector):
                            vector.wait_ge(sp.h, mm_m)
                            if flash_done[0] is not None:
                                vector.wait_ge(flash_done[0][0].h, flash_done[0][1])
                            inst = vector.tensor_copy(
                                dstbuf[:, cidx * T: cidx * T + min(1024, T)],
                                PS[aren[0]][:, 0:min(1024, T)])
                            if T > 1024:
                                inst = vector.tensor_copy(
                                    dstbuf[:, cidx * T + 1024: cidx * T + T],
                                    PS[aren[1]][:, 0:T - 1024])
                            sv.inc(inst)
                        ps_free[aren[0]] = (sv, sv.n)
                        ps_free[aren[1]] = (sv, sv.n)
                    qk_done = sv.n

                    vidx = swi[0]
                    swi[0] += 2

                    @block.sync
                    def _(sync):
                        for sl in range(2):
                            if wstg_free[sl] is not None:
                                sync.wait_ge(wstg_free[sl][0].h, wstg_free[sl][1])
                        for fci in range(2):
                            fc = 16 + 2 * g + fci
                            base = fc * (8 * 128 * 128)
                            dst = ap2(wstg, fci * 128, [[2048, 128], [256, 8], [1, 128]])
                            sw.issue(sync.dma_start(
                                out=dst,
                                in_=ap2(wqkv_d.tensor, base,
                                        [[128, 128], [16384, 8], [1, 128]])),
                                vidx + fci)

                    @block.vector
                    def _(vector):
                        sw.wait(vector, vidx)
                        sw.wait(vector, vidx + 1)
                        for sl in range(2):
                            if whl_free[sl] is not None:
                                vector.wait_ge(whl_free[sl][0].h, whl_free[sl][1])
                        sv.inc(vector.tensor_copy(whi[:], wstg[:]))
                    vh_m = sv.n

                    @block.vector
                    def _(vector):
                        vector.wait_ge(sv.h, vh_m)
                        sv.inc(vector.scalar_tensor_tensor(
                            wlo[:], wstg[:], 1.0, whi[:], ALU.mult, ALU.subtract))
                    vw_m = sv.n
                    wstg_free[0] = wstg_free[1] = (sv, vw_m)

                    @block.vector
                    def _(vector):
                        if flash_done[0] is not None:
                            vector.wait_ge(flash_done[0][0].h, flash_done[0][1])
                        sv.inc(vector.memset(
                            ap2(vaug, 64, [[NT * 260, 128], [65, NT * 4], [1, 1]]), 1.0))
                    ones_m = sv.n

                    for tt in range(NT):
                        ai = 0 if tt % 2 == 0 else 2

                        @block.tensor
                        def _(tensor):
                            tensor.wait_ge(sv.h, max(vw_m, ln1_done))
                            ps_wait(tensor, ai)
                            for kc in range(NCK):
                                xh = xn_hi[:, kc * T + tt * 128: kc * T + (tt + 1) * 128]
                                xl = xn_lo[:, kc * T + tt * 128: kc * T + (tt + 1) * 128]
                                sp.inc(mm(tensor, PS[ai][:, 0:256], xh,
                                          whi[:, kc * 256:(kc + 1) * 256],
                                          start=(kc == 0), stop=False,
                                          key=("v", g, tt, kc, 0)))
                                sp.inc(mm(tensor, PS[ai][:, 0:256], xh,
                                          wlo[:, kc * 256:(kc + 1) * 256],
                                          start=False, stop=False,
                                          key=("v", g, tt, kc, 0)))
                                sp.inc(mm(tensor, PS[ai][:, 0:256], xl,
                                          whi[:, kc * 256:(kc + 1) * 256],
                                          start=False, stop=(kc == NCK - 1),
                                          key=("v", g, tt, kc, 1)))
                        vmm_m = sp.n

                        @block.scalar
                        def _(scalar):
                            scalar.wait_ge(sp.h, vmm_m)
                            scalar.wait_ge(sv.h, ones_m)
                            for hl in range(4):
                                inst = scalar.copy(
                                    vaug[:, tt * 260 + hl * 65: tt * 260 + hl * 65 + 64],
                                    PS[ai][:, hl * 64:(hl + 1) * 64])
                            sa.inc(inst)
                        ps_free[ai] = (sa, sa.n)
                    v_done = sa.n
                    whl_free[0] = whl_free[1] = (sp, sp.n)

                    def norm_pass(srcbuf, colbase, prev_m):
                        @block.scalar
                        def _(scalar):
                            scalar.wait_ge(sv.h, qk_done)
                            if prev_m is not None:
                                scalar.wait_ge(sv.h, prev_m)
                            scalar.activation(q2b[:, 0:T], srcbuf[:, 0:T], ACTF.Square,
                                              bias=0.0, scale=1.0)
                            sa.inc(scalar.activation(q2b[:, T:2 * T], srcbuf[:, T:2 * T],
                                                     ACTF.Square, bias=0.0, scale=1.0))
                        sq_m = sa.n

                        @block.tensor
                        def _(tensor):
                            tensor.wait_ge(sa.h, sq_m)
                            for i in range(4):
                                ps_wait(tensor, i)
                            for c in range(2):
                                for j in range(NJH):
                                    w = min(1024, T)
                                    for jj in range((w + 511) // 512):
                                        ww = min(512, w - jj * 512)
                                        sp.inc(mm(tensor,
                                            PS[2 * c + j][0:2, jj * 512: jj * 512 + ww],
                                            blk2[:],
                                            q2b[:, c * T + j * 1024 + jj * 512:
                                                 c * T + j * 1024 + jj * 512 + ww],
                                            start=True, stop=True, key=("blk2",)))
                        n_mm = sp.n

                        @block.vector
                        def _(vector):
                            vector.wait_ge(sp.h, n_mm)
                            for c in range(2):
                                for j in range(NJH):
                                    inst = vector.reduce_max(
                                        nrm[0:2, colbase + 2 * c + j: colbase + 2 * c + j + 1],
                                        PS[2 * c + j][0:2, 0:min(1024, T)], AX.X)
                            sv.inc(inst)
                        for i in range(4):
                            ps_free[i] = (sv, sv.n)
                        return sv.n

                    qn_m = norm_pass(qbuf, 0, None)
                    kn_m = norm_pass(kbuf, 4, qn_m)

                    if NJH > 1:
                        @block.vector
                        def _(vector):
                            vector.wait_ge(sv.h, kn_m)
                            for b in (0, 2, 4, 6):
                                inst = vector.tensor_tensor(
                                    nrm[0:2, b:b + 1], nrm[0:2, b:b + 1],
                                    nrm[0:2, b + 1:b + 2], ALU.max)
                            sv.inc(inst)
                        kn_m = sv.n

                    @block.vector
                    def _(vector):
                        vector.wait_ge(sv.h, kn_m)
                        vector.tensor_tensor(nrm[0:2, 8:9], nrm[0:2, 0:1],
                                             nrm[0:2, 4:5], ALU.mult)
                        sv.inc(vector.tensor_tensor(nrm[0:2, 9:10], nrm[0:2, 2:3],
                                                    nrm[0:2, 6:7], ALU.mult))
                    m2_m = sv.n

                    @block.scalar
                    def _(scalar):
                        scalar.wait_ge(sv.h, m2_m)
                        sa.inc(scalar.activation(nrm[0:2, 10:12], nrm[0:2, 8:10],
                                                 ACTF.Sqrt, bias=0.0, scale=1.0))
                    sq2_m = sa.n

                    @block.vector
                    def _(vector):
                        vector.wait_ge(sa.h, sq2_m)
                        sv.inc(vector.tensor_scalar_mul(nrm[0:2, 10:12],
                                                        nrm[0:2, 10:12], 0.126262626))
                    msc_m = sv.n

                    @block.sync
                    def _(sync):
                        sync.wait_ge(sv.h, msc_m)
                        srt.issue(sync.dma_start(
                            out=ap2(nrow, 0, [[8, 1], [1, 4]]),
                            in_=ap2(nrm, 10, [[32, 2], [1, 2]]),
                            allow_slow_non_contiguous=True), g)

                    @block.tensor
                    def _(tensor):
                        srt.wait(tensor, g)
                        ps_wait(tensor, 1)
                        for coli in range(4):
                            sp.inc(tensor.matmul(PS[1][:, coli:coli + 1], ones1r[0:1, :],
                                                 nrow[0:1, coli:coli + 1],
                                                 start=True, stop=True))
                    mb_mm = sp.n

                    @block.vector
                    def _(vector):
                        vector.wait_ge(sp.h, mb_mm)
                        for coli, hl in enumerate([0, 2, 1, 3]):
                            inst = vector.tensor_scalar_mul(
                                mh_neg[:, 4 * g + hl: 4 * g + hl + 1],
                                PS[1][:, coli:coli + 1], -1.0)
                        sv.inc(inst)
                    ps_free[1] = (sv, sv.n)
                    mh_m = sv.n

                    for hl in range(4):
                        h = 4 * g + hl
                        cidx, rh = hl // 2, hl % 2

                        def qv(a, b):
                            return qbuf[rh * 64:(rh + 1) * 64, cidx * T + a: cidx * T + b]

                        def kvv(a, b):
                            return kbuf[rh * 64:(rh + 1) * 64, cidx * T + a: cidx * T + b]

                        for qb in range(NQB):
                            q0 = qb * QB
                            nkt = (qb + 1) * (QB // 128)
                            sc_marks, ex_marks, av_marks, dm_marks = {}, {}, {}, {}

                            def emit_scores(kt):
                                qstart = max(q0, kt * 128)
                                qspan = q0 + QB - qstart
                                pp = 2 + (kt % 2)

                                @block.tensor
                                def _(tensor):
                                    if kt == 0:
                                        tensor.wait_ge(sv.h, mh_m)
                                        tensor.wait_ge(sa.h, v_done)
                                        ps_wait(tensor, 0)
                                    if kt >= 2:
                                        tensor.wait_ge(sa.h, ex_marks[kt - 2])
                                    else:
                                        ps_wait(tensor, pp)
                                    for j in range((qspan + 511) // 512):
                                        w = min(512, qspan - j * 512)
                                        sp.inc(tensor.matmul(
                                            PS[pp][:, j * 512: j * 512 + w],
                                            kvv(kt * 128, (kt + 1) * 128),
                                            qv(qstart + j * 512, qstart + j * 512 + w),
                                            start=True, stop=True))
                                sc_marks[kt] = sp.n

                            def emit_exp(kt):
                                qstart = max(q0, kt * 128)
                                qspan = q0 + QB - qstart
                                pp = 2 + (kt % 2)
                                ptv = ptb[:, (kt % 2) * QB: (kt % 2) * QB + qspan]

                                @block.scalar
                                def _(scalar):
                                    scalar.wait_ge(sp.h, sc_marks[kt])
                                    if kt >= 2:
                                        scalar.wait_ge(sp.h, av_marks[kt - 2])
                                    sa.inc(scalar.activation(
                                        ptv, PS[pp][:, 0:qspan], ACTF.Exp,
                                        bias=mh_neg[:, h:h + 1], scale=0.125))
                                ex_marks[kt] = sa.n

                                if qstart == kt * 128:
                                    @block.vector
                                    def _(vector):
                                        vector.wait_ge(sa.h, ex_marks[kt])
                                        mw = min(128, qspan)
                                        sv.inc(vector.tensor_tensor(
                                            ptv[:, 0:mw], ptv[:, 0:mw],
                                            masku[:, 0:mw], ALU.mult))
                                    dm_marks[kt] = sv.n

                            def emit_av(kt):
                                qstart = max(q0, kt * 128)
                                qspan = q0 + QB - qstart
                                ptv = ptb[:, (kt % 2) * QB: (kt % 2) * QB + qspan]
                                dm = dm_marks.get(kt)

                                @block.tensor
                                def _(tensor):
                                    tensor.wait_ge(sa.h, ex_marks[kt])
                                    if dm is not None:
                                        tensor.wait_ge(sv.h, dm)
                                    coff = qstart - q0
                                    a = coff
                                    while a < QB:
                                        e = min(QB, (a // 512 + 1) * 512)
                                        last_kt = min(nkt - 1,
                                                      (q0 + (a // 512 + 1) * 512) // 128 - 1)
                                        sp.inc(tensor.matmul(
                                            PS[0][0:65, a:e],
                                            vaug[:, kt * 260 + hl * 65: kt * 260 + (hl + 1) * 65],
                                            ptv[:, a - coff: e - coff],
                                            start=(kt == 0), stop=(kt == last_kt)))
                                        a = e
                                av_marks[kt] = sp.n

                            emit_scores(0)
                            emit_exp(0)
                            for kt in range(nkt):
                                if kt + 1 < nkt:
                                    emit_scores(kt + 1)
                                    emit_exp(kt + 1)
                                emit_av(kt)
                            o_mm = sp.n

                            ob = (h * NQB + qb) % 2

                            @block.vector
                            def _(vector):
                                vector.wait_ge(sp.h, o_mm)
                                if h * NQB + qb >= 2:
                                    srs.wait(vector, h * NQB + qb - 2)
                                    sso.wait(vector, h * NQB + qb - 2)
                                sv.inc(vector.tensor_copy(
                                    ostg[0:65, ob * QB: ob * QB + QB], PS[0][0:65, 0:QB]))
                            oe_m = sv.n
                            ps_free[0] = (sv, oe_m)

                            @block.sync
                            def _(sync):
                                sync.wait_ge(sv.h, oe_m)
                                c, r2 = h // 2, h % 2
                                sso.issue(sync.dma_start(
                                    out=ap2(o_scr.tensor, c * 128 * T + r2 * 64 * T + q0,
                                            [[T, 64], [1, QB]]),
                                    in_=ostg[0:64, ob * QB: ob * QB + QB]), h * NQB + qb)
                                srs.issue(sync.dma_start(
                                    out=rsum[h:h + 1, q0: q0 + QB],
                                    in_=ostg[64:65, ob * QB: ob * QB + QB]),
                                    h * NQB + qb)
                    flash_done[0] = (sp, sp.n)

                @block.vector
                def _(vector):
                    for i in range(NH * NQB):
                        srs.wait(vector, i)
                    sv.inc(vector.reciprocal(rsum[0:16, :], rsum[0:16, :]))
                rc_m = sv.n
            return rc_m

        def emit_l0_proj(rsum_handle, rc_m):
            ph = contextlib.ExitStack()
            with ph:
                TJ = 256
                NTJ = T // TJ
                otile = ph.enter_context(nc.sbuf_tensor("otile", [128, 2 * NCK * TJ], F32))
                rstage = ph.enter_context(nc.sbuf_tensor("rstage", [128, 2 * NCK * TJ], F32))
                sstg = ph.enter_context(nc.sbuf_tensor("sstg", [128, 2 * NCK * TJ], F32))
                pstg = ph.enter_context(nc.sbuf_tensor("pstg", [128, 2048], F32))
                pw_free = [None, None]
                ot_free = {}
                ss_free = {}

                wpbuf = ph.enter_context(nc.sbuf_tensor("wpbuf", [128, NCK * 1024], F32))

                def wtile_proj(fc):
                    base = fc * (8 * 128 * 128)
                    return ap2(wproj_d.tensor, base, [[128, 128], [16384, 8], [1, 128]])

                @block.sync
                def _(sync):
                    sync.wait_ge(sv.h, rc_m)
                    for i in range(NH * NQB):
                        sso.wait(sync, i)
                    for fc in range(NCK):
                        dst = ap2(wpbuf, fc * 1024, [[NCK * 1024, 128], [128, 8], [1, 128]])
                        sw.issue(sync.dma_start(out=dst, in_=wtile_proj(fc)), 100 + fc)

                for ti in range(NTJ):
                    tb = ti % 2
                    @block.sync
                    def _(sync):
                        if ti == 0:
                            for i in range(NH * NQB):
                                sso.wait(sync, i)
                        if ti >= 2:
                            sync.wait_ge(sp.h, ot_free[ti - 2])
                        sot.issue(sync.dma_start(
                            out=ap2(otile, tb * NCK * TJ,
                                    [[2 * NCK * TJ, 128], [TJ, 8], [1, TJ]]),
                            in_=ap2(o_scr.tensor, ti * TJ,
                                    [[T, 128], [128 * T, 8], [1, TJ]])), ti)
                        if ti == 0:
                            sync.wait_ge(sv.h, rc_m)
                        for hh in range(16):
                            c2, r = hh // 2, hh % 2
                            srg.issue(sync.dma_start(
                                out=rstage[r:r + 1,
                                           tb * NCK * TJ + c2 * TJ:
                                           tb * NCK * TJ + (c2 + 1) * TJ],
                                in_=rsum_handle[hh:hh + 1, ti * TJ:(ti + 1) * TJ]), ti)

                    nm_marks = []
                    for c in range(NCK):
                        ai = c % 2

                        @block.tensor
                        def _(tensor):
                            sot.wait(tensor, ti)
                            if c == 0:
                                srg.wait(tensor, ti)
                            ps_wait(tensor, ai)
                            sp.inc(tensor.matmul(
                                PS[ai][:, 0:TJ], sel2[0:2, :],
                                rstage[0:2, tb * NCK * TJ + c * TJ:
                                       tb * NCK * TJ + (c + 1) * TJ],
                                start=True, stop=True))
                        bc_mm = sp.n

                        @block.vector
                        def _(vector):
                            vector.wait_ge(sp.h, bc_mm)
                            ov = otile[:, tb * NCK * TJ + c * TJ: tb * NCK * TJ + (c + 1) * TJ]
                            sv.inc(vector.tensor_tensor(ov, ov, PS[ai][:, 0:TJ], ALU.mult))
                        ps_free[ai] = (sv, sv.n)
                        nm_marks.append(sv.n)
                    onrm_m = sv.n

                    for half in range(2):
                        aren = (0, 1) if half == 0 else (2, 3)

                        @block.tensor
                        def _(tensor):
                            tensor.wait_ge(sv.h, onrm_m)
                            sw.wait(tensor, 100 + NCK - 1)
                            sw.wait(tensor, 100 + NCK - 2)
                            ps_wait(tensor, aren[0])
                            ps_wait(tensor, aren[1])
                            for fc2 in range(4):
                                fc = half * 4 + fc2
                                pa = PS[aren[fc2 // 2]]
                                po = (fc2 % 2) * TJ
                                for dc in range(NCK):
                                    sp.inc(tensor.matmul(
                                        pa[:, po: po + TJ],
                                        wpbuf[:, fc * 1024 + dc * 128: fc * 1024 + dc * 128 + 128],
                                        otile[:, tb * NCK * TJ + dc * TJ:
                                              tb * NCK * TJ + (dc + 1) * TJ],
                                        start=(dc == 0), stop=(dc == NCK - 1)))
                        pj_mm = sp.n
                        if half == 1:
                            ot_free[ti] = pj_mm

                        @block.vector
                        def _(vector):
                            vector.wait_ge(sp.h, pj_mm)
                            if ti >= 2:
                                sst.wait(vector, (ti - 2) * 2 + half)
                            for fc2 in range(4):
                                fc = half * 4 + fc2
                                pa = PS[aren[fc2 // 2]]
                                po = (fc2 % 2) * TJ
                                inst = vector.tensor_copy(
                                    sstg[:, tb * NCK * TJ + fc * TJ:
                                         tb * NCK * TJ + (fc + 1) * TJ],
                                    pa[:, po: po + TJ])
                            sv.inc(inst)
                        se_m = sv.n
                        ps_free[aren[0]] = (sv, se_m)
                        ps_free[aren[1]] = (sv, se_m)

                        @block.sync
                        def _(sync):
                            sync.wait_ge(sv.h, se_m)
                            sst.issue(sync.dma_start(
                                out=ap2(s_scr.tensor, half * 4 * 128 * T + ti * TJ,
                                        [[T, 128], [128 * T, 4], [1, TJ]]),
                                in_=ap2(sstg, tb * NCK * TJ + half * 4 * TJ,
                                        [[2 * NCK * TJ, 128], [TJ, 4], [1, TJ]])),
                                ti * 2 + half)
            return sv.n

        if upto in ("attn", "l0", "ln2", "l1attn", "full"):
            with contextlib.ExitStack() as rs_scope:
                rsum_keep = rs_scope.enter_context(
                    nc.sbuf_tensor("rsum_keep", [128, T], F32))
                rc_m = emit_l0_attn(rsum_keep)
                pj_done = emit_l0_proj(rsum_keep, rc_m)

        def emit_l0_ffn(pj_done):
            TJ = 256
            NTJ = T // TJ
            ph = contextlib.ExitStack()
            with ph:
                a1h = ph.enter_context(nc.sbuf_tensor("a1h", [128, 32 * TJ], BF16))
                a1l = ph.enter_context(nc.sbuf_tensor("a1l", [128, 32 * TJ], BF16))
                a2h = ph.enter_context(nc.sbuf_tensor("a2h", [128, 32 * TJ], BF16))
                a2l = ph.enter_context(nc.sbuf_tensor("a2l", [128, 32 * TJ], BF16))
                k0s = ph.enter_context(nc.sbuf_tensor("k0s", [128, 2 * 2048], BF16))
                d0s = ph.enter_context(nc.sbuf_tensor("d0s", [128, 2 * 1024], BF16))
                k1s = ph.enter_context(nc.sbuf_tensor("k1s", [128, 2 * 4096], BF16))
                k2s = ph.enter_context(nc.sbuf_tensor("k2s", [128, 4096], BF16))
                tb16 = ph.enter_context(nc.sbuf_tensor("tb16", [128, 4 * TJ], BF16))
                nhrow = ph.enter_context(nc.sbuf_tensor("nhrow", [128, 128], BF16))
                af32 = ph.enter_context(nc.sbuf_tensor("af32", [128, 2 * TJ], F32))
                sgbuf = ph.enter_context(nc.sbuf_tensor("sgbuf", [128, NCK * TJ], F32))
                stile = ph.enter_context(nc.sbuf_tensor("stile", [128, NCK * TJ], F32))
                xres = ph.enter_context(nc.sbuf_tensor("xres", [128, 2 * 1024], F32))
                sk0 = mkstream("sk0")
                sk1 = mkstream("sk1")
                sk2 = mkstream("sk2")
                ssl = mkstream("ssl")
                sxr = mkstream("sxr", nbuf=4)
                syw = mkstream("syw", nbuf=4)

                @block.vector
                def _(vector):
                    vector.wait_ge(sv.h, pj_done)
                    for k in sorted(sst.targets):
                        sst.wait(vector, k)
                    sv.inc(vector.memset(nhrow[0:1, :], -0.5))
                nh_m = sv.n

                d0_marks, mm1_marks = {}, {}
                k1_mm_marks, k2_mm_marks = {}, {}
                mm2_done, mm3_done = {}, {}
                tr_marks, st_marks = {}, {}
                for ti in range(NTJ):
                    ev1 = {}
                    t1_mm = 0
                    for fc in range(32):
                        slot = fc % 2
                        kidx = ti * 32 + fc

                        @block.gpsimd
                        def _(sync):
                            if kidx == 0:
                                sync.wait_ge(sv.h, nh_m)
                            if kidx >= 2:
                                sync.wait_ge(sv.h, d0_marks[kidx - 2])
                            if kidx == 0:
                                wait_stream_all(sync, sck0)
                            sk0.issue(sync.dma_start(
                                out=ap2(k0s, slot * 2048,
                                        [[2 * 2048, 128], [1, 2048]]),
                                in_=ap2(k0bf_d.tensor, fc * 16 * 128 * 128,
                                        [[2048, 128], [1, 2048]])), kidx)

                        @block.vector
                        def _(vector):
                            sk0.wait(vector, kidx)
                            if kidx >= 2:
                                vector.wait_ge(sp.h, mm1_marks[kidx - 2])
                            sv.inc(vector.tensor_sub(
                                d0s[:, slot * 1024:(slot + 1) * 1024],
                                k0s[:, slot * 2048: slot * 2048 + 1024],
                                k0s[:, slot * 2048 + 1024: slot * 2048 + 2048]))
                        d0_marks[kidx] = sv.n
                        d0_m = sv.n

                        ai = fc % 4

                        @block.tensor
                        def _(tensor):
                            tensor.wait_ge(sv.h, d0_m)
                            tensor.wait_ge(sv.h, ln1_done)
                            ps_wait(tensor, ai)
                            for kc in range(NCK):
                                d0t = d0s[:, slot * 1024 + kc * 128:
                                          slot * 1024 + kc * 128 + 128]
                                for pi, xt in enumerate((xn_hi, xn_lo)):
                                    sp.inc(mm(tensor,
                                        PS[ai][:, 0:TJ], d0t,
                                        xt[:, kc * T + ti * TJ: kc * T + ti * TJ + TJ],
                                        start=(kc == 0 and pi == 0),
                                        stop=(kc == NCK - 1 and pi == 1),
                                        key=("mm1", kidx, kc)))
                        mm1_marks[kidx] = sp.n
                        mm1_m = sp.n

                        sl2 = fc % 2

                        @block.scalar
                        def _(scalar):
                            scalar.wait_ge(sp.h, mm1_m)
                            if fc >= 2:
                                scalar.wait_ge(sv.h, ev1[fc - 2])
                            sa.inc(scalar.activation(
                                af32[:, sl2 * TJ:(sl2 + 1) * TJ], PS[ai][:, 0:TJ],
                                ACTF.Relu, bias=0.0, scale=scal[:, 0:1]))
                        rl_m = sa.n
                        ps_free[ai] = (sa, rl_m)

                        @block.vector
                        def _(vector):
                            vector.wait_ge(sa.h, rl_m)
                            if ti >= 1 and fc == 0:
                                vector.wait_ge(sp.h, mm2_done[ti - 1])
                            sv.inc(vector.tensor_copy(
                                a1h[:, fc * TJ:(fc + 1) * TJ],
                                af32[:, sl2 * TJ:(sl2 + 1) * TJ]))
                        h_m = sv.n

                        @block.vector
                        def _(vector):
                            vector.wait_ge(sv.h, h_m)
                            sv.inc(vector.scalar_tensor_tensor(
                                a1l[:, fc * TJ:(fc + 1) * TJ],
                                af32[:, sl2 * TJ:(sl2 + 1) * TJ], 1.0,
                                a1h[:, fc * TJ:(fc + 1) * TJ], ALU.mult, ALU.subtract))
                        ev1[fc] = sv.n

                        @block.tensor
                        def _(tensor):
                            tensor.wait_ge(sv.h, ev1[fc])
                            if fc == 0:
                                tensor.wait_ge(sv.h, t1sp_m[0])
                            sp.inc(mm(tensor,
                                PS[0][0:1, 512:512 + TJ], onesb[:],
                                a1h[:, fc * TJ:(fc + 1) * TJ],
                                start=(fc == 0), stop=False, key=("onesb",)))
                            sp.inc(mm(tensor,
                                PS[0][0:1, 512:512 + TJ], onesb[:],
                                a1l[:, fc * TJ:(fc + 1) * TJ],
                                start=False, stop=(fc == 31), key=("onesb",)))
                        t1_mm = sp.n
                    a1_done_m = sv.n

                    @block.vector
                    def _(vector):
                        vector.wait_ge(sp.h, t1_mm)
                        sv.inc(vector.tensor_copy(tb16[0:1, 0:TJ],
                                                  PS[0][0:1, 512:512 + TJ]))
                    t1h_m = sv.n

                    @block.vector
                    def _(vector):
                        vector.wait_ge(sv.h, t1h_m)
                        sv.inc(vector.scalar_tensor_tensor(
                            tb16[0:1, TJ:2 * TJ], PS[0][0:1, 512:512 + TJ], 1.0,
                            tb16[0:1, 0:TJ], ALU.mult, ALU.subtract))
                    t1sp_m[0] = sv.n
                    t1_m = sv.n

                    ev2 = {}
                    t2_mm = 0
                    for fc2 in range(32):
                        slot = fc2 % 2
                        kidx = ti * 32 + fc2

                        @block.gpsimd
                        def _(scalar):
                            if kidx >= 2:
                                scalar.wait_ge(sp.h, k1_mm_marks[kidx - 2])
                            if kidx == 0:
                                wait_stream_all(scalar, sck1)
                            sk1.issue(scalar.dma_start(
                                out=ap2(k1s, slot * 4096,
                                        [[2 * 4096, 128], [1, 4096]]),
                                in_=ap2(k1bf_d.tensor, fc2 * 32 * 128 * 128,
                                        [[4096, 128], [1, 4096]])), kidx)

                        ai = fc2 % 4

                        @block.tensor
                        def _(tensor):
                            sk1.wait(tensor, kidx)
                            tensor.wait_ge(sv.h, t1_m)
                            ps_wait(tensor, ai)
                            for kc2 in range(32):
                                b1t = k1s[:, slot * 4096 + kc2 * 128:
                                          slot * 4096 + kc2 * 128 + 128]
                                for pi, at in enumerate((a1h, a1l)):
                                    sp.inc(mm(tensor,
                                        PS[ai][:, 0:TJ], b1t,
                                        at[:, kc2 * TJ:(kc2 + 1) * TJ],
                                        start=(kc2 == 0 and pi == 0), stop=False,
                                        key=("mm2", kidx, kc2)))
                            sp.inc(mm(tensor,
                                PS[ai][:, 0:TJ], nhrow[0:1, :], tb16[0:1, 0:TJ],
                                start=False, stop=False, key=("nh", ti)))
                            sp.inc(mm(tensor,
                                PS[ai][:, 0:TJ], nhrow[0:1, :], tb16[0:1, TJ:2 * TJ],
                                start=False, stop=True, key=("nh", ti)))
                        k1_mm_marks[kidx] = sp.n
                        mm2_m = sp.n

                        sl2 = fc2 % 2

                        @block.scalar
                        def _(scalar):
                            scalar.wait_ge(sp.h, mm2_m)
                            if fc2 >= 2:
                                scalar.wait_ge(sv.h, ev2[fc2 - 2])
                            sa.inc(scalar.activation(
                                af32[:, sl2 * TJ:(sl2 + 1) * TJ], PS[ai][:, 0:TJ],
                                ACTF.Relu, bias=0.0, scale=scal[:, 1:2]))
                        rl_m = sa.n
                        ps_free[ai] = (sa, rl_m)

                        @block.vector
                        def _(vector):
                            vector.wait_ge(sa.h, rl_m)
                            if ti >= 1 and fc2 == 0:
                                vector.wait_ge(sp.h, mm3_done[ti - 1])
                            sv.inc(vector.tensor_copy(
                                a2h[:, fc2 * TJ:(fc2 + 1) * TJ],
                                af32[:, sl2 * TJ:(sl2 + 1) * TJ]))
                        h_m = sv.n

                        @block.vector
                        def _(vector):
                            vector.wait_ge(sv.h, h_m)
                            sv.inc(vector.scalar_tensor_tensor(
                                a2l[:, fc2 * TJ:(fc2 + 1) * TJ],
                                af32[:, sl2 * TJ:(sl2 + 1) * TJ], 1.0,
                                a2h[:, fc2 * TJ:(fc2 + 1) * TJ], ALU.mult, ALU.subtract))
                        ev2[fc2] = sv.n

                        @block.tensor
                        def _(tensor):
                            tensor.wait_ge(sv.h, ev2[fc2])
                            if fc2 == 0:
                                tensor.wait_ge(sv.h, t2sp_m[0])
                            sp.inc(mm(tensor,
                                PS[1][0:1, 512:512 + TJ], onesb[:],
                                a2h[:, fc2 * TJ:(fc2 + 1) * TJ],
                                start=(fc2 == 0), stop=False, key=("onesb",)))
                            sp.inc(mm(tensor,
                                PS[1][0:1, 512:512 + TJ], onesb[:],
                                a2l[:, fc2 * TJ:(fc2 + 1) * TJ],
                                start=False, stop=(fc2 == 31), key=("onesb",)))
                        t2_mm = sp.n
                    mm2_done[ti] = sp.n

                    @block.vector
                    def _(vector):
                        vector.wait_ge(sp.h, t2_mm)
                        sv.inc(vector.tensor_copy(tb16[0:1, 2 * TJ:3 * TJ],
                                                  PS[1][0:1, 512:512 + TJ]))
                    t2h_m = sv.n

                    @block.vector
                    def _(vector):
                        vector.wait_ge(sv.h, t2h_m)
                        sv.inc(vector.scalar_tensor_tensor(
                            tb16[0:1, 3 * TJ:4 * TJ], PS[1][0:1, 512:512 + TJ], 1.0,
                            tb16[0:1, 2 * TJ:3 * TJ], ALU.mult, ALU.subtract))
                    t2sp_m[0] = sv.n
                    t2_m = sv.n

                    for fc3 in range(NCK):
                        slot = 0
                        kidx = ti * NCK + fc3

                        @block.gpsimd
                        def _(sync):
                            if kidx >= 1:
                                sync.wait_ge(sp.h, k2_mm_marks[kidx - 1])
                            if kidx == 0:
                                wait_stream_all(sync, sck2)
                            sk2.issue(sync.dma_start(
                                out=ap2(k2s, 0,
                                        [[4096, 128], [1, 4096]]),
                                in_=ap2(k2bf_d.tensor, fc3 * 32 * 128 * 128,
                                        [[4096, 128], [1, 4096]])), kidx)

                        ai = fc3 % 4

                        @block.tensor
                        def _(tensor):
                            sk2.wait(tensor, kidx)
                            tensor.wait_ge(sv.h, t2_m)
                            ps_wait(tensor, ai)
                            for kc3 in range(32):
                                b2t = k2s[:, kc3 * 128: kc3 * 128 + 128]
                                for pi, at in enumerate((a2h, a2l)):
                                    sp.inc(mm(tensor,
                                        PS[ai][:, 0:TJ], b2t,
                                        at[:, kc3 * TJ:(kc3 + 1) * TJ],
                                        start=(kc3 == 0 and pi == 0), stop=False,
                                        key=("mm3", kidx, kc3)))
                            sp.inc(mm(tensor,
                                PS[ai][:, 0:TJ], nhrow[0:1, :], tb16[0:1, 2 * TJ:3 * TJ],
                                start=False, stop=False, key=("nh3", ti)))
                            sp.inc(mm(tensor,
                                PS[ai][:, 0:TJ], nhrow[0:1, :], tb16[0:1, 3 * TJ:4 * TJ],
                                start=False, stop=True, key=("nh3", ti)))
                        k2_mm_marks[kidx] = sp.n
                        mm3_m = sp.n

                        @block.scalar
                        def _(scalar):
                            scalar.wait_ge(sp.h, mm3_m)
                            if ti >= 1 and fc3 == 0:
                                scalar.wait_ge(sv.h, comb_marks[ti - 1])
                            sa.inc(scalar.activation(
                                sgbuf[:, fc3 * TJ:(fc3 + 1) * TJ], PS[ai][:, 0:TJ],
                                ACTF.Silu, bias=0.0, scale=scal[:, 2:3]))
                        ps_free[ai] = (sa, sa.n)
                    mm3_done[ti] = sp.n
                    sg_m = sa.n

                    tb = ti % 2

                    @block.sync
                    def _(sync):
                        if ti == 0:
                            for k in sorted(sst.targets):
                                sst.wait(sync, k)
                        if ti >= 1:
                            sync.wait_ge(sv.h, comb_marks[ti - 1])
                        ssl.issue(sync.dma_start(
                            out=ap2(stile, 0,
                                    [[NCK * TJ, 128], [TJ, 8], [1, TJ]]),
                            in_=ap2(s_scr.tensor, ti * TJ,
                                    [[T, 128], [128 * T, 8], [1, TJ]])), ti)

                    @block.vector
                    def _(vector):
                        ssl.wait(vector, ti)
                        vector.wait_ge(sa.h, sg_m)
                        for c in range(NCK):
                            inst = vector.tensor_tensor(
                                sgbuf[:, c * TJ:(c + 1) * TJ],
                                sgbuf[:, c * TJ:(c + 1) * TJ],
                                stile[:, c * TJ:(c + 1) * TJ], ALU.mult)
                        sv.inc(inst)
                    p_m = sv.n
                    comb_marks[ti] = p_m

                    for tsub in range(2 if TJ == 256 else 1):
                        xslot = tsub

                        @block.sync
                        def _(sync):
                            if ti >= 1:
                                syw.wait(sync, (ti - 1) * 2 + tsub)
                            sxr.issue(sync.dma_start(
                                out=xres[:, xslot * 1024:(xslot + 1) * 1024],
                                in_=x_d[ti * TJ + tsub * 128: ti * TJ + tsub * 128 + 128, :]),
                                ti * 2 + tsub)

                        @block.tensor
                        def _(tensor):
                            tensor.wait_ge(sv.h, p_m)
                            ps_wait(tensor, 2 + tsub)
                            for c in range(NCK):
                                sp.inc(tensor.transpose(
                                    PS[2 + tsub][:, c * 128:(c + 1) * 128],
                                    sgbuf[:, c * TJ + tsub * 128: c * TJ + tsub * 128 + 128],
                                    ident[:]))
                        tr_m = sp.n
                        tr_marks[(ti, tsub)] = tr_m

                        @block.vector
                        def _(vector):
                            vector.wait_ge(sp.h, tr_m)
                            sxr.wait(vector, ti * 2 + tsub)
                            sv.inc(vector.tensor_tensor(
                                xres[:, xslot * 1024:(xslot + 1) * 1024],
                                xres[:, xslot * 1024:(xslot + 1) * 1024],
                                PS[2 + tsub][:], ALU.add))
                        x1_m = sv.n
                        ps_free[2 + tsub] = (sv, x1_m)

                        @block.sync
                        def _(sync):
                            sync.wait_ge(sv.h, x1_m)
                            syw.issue(sync.dma_start(
                                out=y_d[ti * TJ + tsub * 128: ti * TJ + tsub * 128 + 128, :],
                                in_=xres[:, xslot * 1024:(xslot + 1) * 1024]),
                                ti * 2 + tsub)
                    comb_marks[ti] = sv.n

                @block.sync
                def _(sync):
                    for k in sorted(syw.targets):
                        syw.wait(sync, k)
            return sv.n

        comb_marks = {}
        t1sp_m = [0]
        t2sp_m = [0]
        if upto in ("l0", "ln2", "l1attn", "full"):
            ffn_done = emit_l0_ffn(pj_done)

        s2_scr = (nc.dram_tensor("dbg_s2tok", [T, D], F32, kind="ExternalOutput").ap()
                  if "s2tok" in dbg else
                  nc.dram_tensor("s2_scr", [T, D], F32, kind="Internal").ap())
        h_scr = (nc.dram_tensor("dbg_h_tok", [T, FI], F32, kind="ExternalOutput").ap()
                 if "h_tok" in dbg else
                 nc.dram_tensor("h_scr", [T, FI], F32, kind="Internal").ap())

        def emit_l1_attn(arms, sob):
            ph = contextlib.ExitStack()
            with ph:
                wqstg = ph.enter_context(nc.sbuf_tensor("wqstg", [128, 2 * 1024], F32))
                zq = ph.enter_context(nc.sbuf_tensor("zq", [128, 1024], F32))
                s2stg = ph.enter_context(nc.sbuf_tensor("s2stg", [128, 2 * 1024], F32))
                cq1 = s2stg[:, 0:1024]
                cq2 = s2stg[:, 1024:2048]
                wqT = ph.enter_context(nc.sbuf_tensor("wqT", [128, 2 * 1024], BF16))
                wqTo = ph.enter_context(nc.sbuf_tensor("wqTo", [128, NCK * 1024], BF16))
                scw = ph.enter_context(nc.sbuf_tensor("scw", [128, 48], F32))
                kvbuf = ph.enter_context(nc.sbuf_tensor("kvbuf", [128, NCK * T], F32))
                ktile = ph.enter_context(nc.sbuf_tensor("ktile", [128, T], F32))
                ftile = ph.enter_context(nc.sbuf_tensor("ftile", [128, T], F32))
                rtile = ftile
                zrow = ph.enter_context(nc.sbuf_tensor("zrow", [128, 2048], F32))
                ystg = ph.enter_context(nc.sbuf_tensor("ystg", [128, 512], BF16))
                sww = mkstream("sww")
                sds = mkstream("sds")
                ss2 = mkstream("ss2", nbuf=4)

                @block.scalar
                def _(scalar):
                    scalar.wait_ge(sv.h, ln2_done)
                    sa.inc(scalar.activation(decay_c[:, 0:8], decay_c[:, 0:8],
                                             ACTF.Sigmoid, bias=0.0, scale=1.0))
                dec_m = sa.n

                @block.vector
                def _(vector):
                    vector.wait_ge(sa.h, dec_m)
                    sv.inc(vector.tensor_scalar_max(zrow[:, 0:8], decay_c[:, 0:8], 1e-7))
                clip_m = sv.n

                @block.scalar
                def _(scalar):
                    scalar.wait_ge(sv.h, clip_m)
                    sa.inc(scalar.activation(zrow[:, 8:16], zrow[:, 0:8], ACTF.Ln,
                                             bias=0.0, scale=1.0))
                ldl_m = sa.n

                last_trans_pe = [0]

                def quant_chain(w_dram, ic, sc_col, widx):
                    slot = widx % 2

                    @block.sync
                    def _(sync):
                        if widx >= 2:
                            sync.wait_ge(sv.h, wq_marks[widx - 2])
                        if widx == 0:
                            sync.wait_ge(sa.h, ldl_m)
                        sww.issue(sync.dma_start(
                            out=wqstg[:, slot * 1024:(slot + 1) * 1024],
                            in_=w_dram[ic * 128:(ic + 1) * 128, :]), widx)

                    @block.vector
                    def _(vector):
                        sww.wait(vector, widx)
                        if widx >= 1:
                            vector.wait_ge(sv.h, wq_marks.get(widx - 1, 0))
                        sv.inc(vector.tensor_reduce(
                            scw[:, sc_col:sc_col + 1],
                            wqstg[:, slot * 1024:(slot + 1) * 1024],
                            AX.X, ALU.add, apply_absolute_value=True))
                    r_m = sv.n

                    @block.vector
                    def _(vector):
                        vector.wait_ge(sv.h, r_m)
                        sv.inc(vector.tensor_scalar(
                            scw[:, sc_col:sc_col + 1], scw[:, sc_col:sc_col + 1],
                            0.0009765625, 1e-5, ALU.mult, ALU.max))
                    s_m = sv.n

                    @block.vector
                    def _(vector):
                        vector.wait_ge(sv.h, s_m)
                        sv.inc(vector.reciprocal(scw[:, 44:45], scw[:, sc_col:sc_col + 1]))
                    rc2_m = sv.n

                    @block.vector
                    def _(vector):
                        vector.wait_ge(sv.h, rc2_m)
                        vector.wait_ge(sp.h, last_trans_pe[0])
                        sv.inc(vector.tensor_scalar_mul(
                            zq[:], wqstg[:, slot * 1024:(slot + 1) * 1024],
                            scw[:, 44:45]))
                    z_m = sv.n
                    wq_marks[widx] = z_m

                    @block.vector
                    def _(vector):
                        vector.wait_ge(sv.h, z_m)
                        vector.tensor_single_scalar(cq1, zq[:], 0.5, ALU.is_gt)
                        sv.inc(vector.engine_nop())
                    c1_m = sv.n

                    @block.vector
                    def _(vector):
                        vector.wait_ge(sv.h, c1_m)
                        vector.tensor_single_scalar(cq2, zq[:], -0.5, ALU.is_lt)
                        sv.inc(vector.engine_nop())
                    c2_m = sv.n

                    @block.vector
                    def _(vector):
                        vector.wait_ge(sv.h, c2_m)
                        sv.inc(vector.tensor_sub(zq[:], cq1, cq2))
                    return sv.n

                def quant_trans(chain_m, tgt, tgt_off, prev_mm, tgt_stride=128):
                    @block.tensor
                    def _(tensor):
                        tensor.wait_ge(sv.h, chain_m)
                        if prev_mm is not None:
                            tensor.wait_ge(prev_mm[0].h, prev_mm[1])
                        ps_wait(tensor, 2)
                        ps_wait(tensor, 3)
                        for kc in range(NCK):
                            pa = PS[2] if kc < 4 else PS[3]
                            sp.inc(tensor.transpose(
                                pa[:, (kc % 4) * 128:(kc % 4) * 128 + 128],
                                zq[:, kc * 128:(kc + 1) * 128], ident[:]))
                    t_m = sp.n
                    last_trans_pe[0] = t_m

                    @block.vector
                    def _(vector):
                        vector.wait_ge(sp.h, t_m)
                        inst = None
                        for kc in range(NCK):
                            pa = PS[2] if kc < 4 else PS[3]
                            o = tgt_off + kc * tgt_stride
                            inst = vector.tensor_copy(
                                tgt[:, o: o + 128],
                                pa[:, (kc % 4) * 128:(kc % 4) * 128 + 128])
                        sv.inc(inst)
                    ps_free[2] = (sv, sv.n)
                    ps_free[3] = (sv, sv.n)
                    return sv.n

                def quant_w(w_dram, ic, sc_col, widx, tgt, tgt_off, prev_mm,
                            tgt_stride=128):
                    c_m = quant_chain(w_dram, ic, sc_col, widx)
                    return quant_trans(c_m, tgt, tgt_off, prev_mm, tgt_stride)

                wq_marks = {}
                widx_c = [0]
                ic_marks = {}
                wkey = [0]

                for ic in range(NCK):
                    ck_m = quant_chain(wk_d, ic, 0 + ic, widx_c[0])
                    widx_c[0] += 1
                    wqk_m = quant_trans(ck_m, wqT, 0, None)

                    wkey[0] += 1

                    @block.tensor
                    def _(tensor):
                        tensor.wait_ge(sv.h, max(wqk_m, ln2_done))
                        for j in range(4):
                            ps_wait(tensor, j)
                        for kc in range(NCK):
                            wt = wqT[:, kc * 128:(kc + 1) * 128]
                            for pi, xt in enumerate((xn_hi, xn_lo)):
                                for j in range((T + 511) // 512):
                                    w = min(512, T - j * 512)
                                    sp.inc(mm(tensor,
                                        PS[j][:, 0:w], wt,
                                        xt[:, kc * T + j * 512: kc * T + j * 512 + w],
                                        start=(kc == 0 and pi == 0),
                                        stop=(kc == NCK - 1 and pi == 1),
                                        key=("l1p", wkey[0], kc)))
                    kmm_m = sp.n

                    cv_m = quant_chain(wv_d, ic, 8 + ic, widx_c[0])
                    widx_c[0] += 1

                    @block.vector
                    def _(vector):
                        vector.wait_ge(sp.h, kmm_m)
                        inst = None
                        for j in range((T + 511) // 512):
                            w = min(512, T - j * 512)
                            inst = vector.tensor_copy(ktile[:, j * 512:j * 512 + w],
                                                      PS[j][:, 0:w])
                        sv.inc(inst)
                    ke_m = sv.n
                    for j in range(4):
                        ps_free[j] = (sv, ke_m)

                    wqv_m = quant_trans(cv_m, wqT, 1024, None)

                    @block.vector
                    def _(vector):
                        vector.wait_ge(sv.h, wqv_m)
                        sv.inc(vector.tensor_mul(scw[:, 16 + ic:17 + ic],
                                                 scw[:, 0 + ic:1 + ic],
                                                 scw[:, 8 + ic:9 + ic]))
                    skv_m = sv.n

                    wkey[0] += 1

                    @block.tensor
                    def _(tensor):
                        tensor.wait_ge(sv.h, max(wqv_m, ke_m))
                        for kc in range(NCK):
                            wt = wqT[:, 1024 + kc * 128: 1024 + (kc + 1) * 128]
                            for pi, xt in enumerate((xn_hi, xn_lo)):
                                for j in range((T + 511) // 512):
                                    w = min(512, T - j * 512)
                                    sp.inc(mm(tensor,
                                        PS[j][:, 0:w], wt,
                                        xt[:, kc * T + j * 512: kc * T + j * 512 + w],
                                        start=(kc == 0 and pi == 0),
                                        stop=(kc == NCK - 1 and pi == 1),
                                        key=("l1p", wkey[0], kc)))
                    vmm_m = sp.n

                    cr_m = quant_chain(wr_d, ic, 24 + ic, widx_c[0])
                    widx_c[0] += 1

                    @block.vector
                    def _(vector):
                        vector.wait_ge(sp.h, vmm_m)
                        vector.wait_ge(sv.h, skv_m)
                        if ic >= 1:
                            vector.wait_ge(sv.h, ic_marks[ic - 1])
                        inst = None
                        for j in range((T + 511) // 512):
                            w = min(512, T - j * 512)
                            inst = vector.scalar_tensor_tensor(
                                kvbuf[:, ic * T + j * 512: ic * T + j * 512 + w],
                                ktile[:, j * 512:j * 512 + w],
                                scw[:, 16 + ic:17 + ic],
                                PS[j][:, 0:w], ALU.mult, ALU.mult)
                        sv.inc(inst)
                    kv_m = sv.n
                    for j in range(4):
                        ps_free[j] = (sv, kv_m)

                    @block.vector
                    def _(vector):
                        vector.wait_ge(sv.h, kv_m)
                        sv.inc(vector.tensor_scalar_mul(
                            ftile[:], iota[:, 0:T], zrow[:, 8 + ic:9 + ic]))
                    fe_m = sv.n

                    @block.scalar
                    def _(scalar):
                        scalar.wait_ge(sv.h, fe_m)
                        sa.inc(scalar.activation(ftile[:], ftile[:], ACTF.Exp,
                                                 bias=0.0, scale=1.0))
                    fx_m = sa.n

                    @block.vector
                    def _(vector):
                        vector.wait_ge(sa.h, fx_m)
                        sv.inc(vector.tensor_scalar(ftile[:], ftile[:], 1e10, 1.0,
                                                    ALU.mult, ALU.min))
                    fm_m = sv.n

                    @block.vector
                    def _(vector):
                        vector.wait_ge(sv.h, fm_m)
                        sv.inc(vector.tensor_mul(ftile[:], ftile[:],
                                                 kvbuf[:, ic * T:(ic + 1) * T]))
                    kvp_m = sv.n

                    @block.vector
                    def _(vector):
                        vector.wait_ge(sv.h, kvp_m)
                        dec_b = ap2(decay_c, ic, [[8, 128], [0, T]])
                        sv.inc(vector.tensor_tensor_scan(
                            kvbuf[:, ic * T:(ic + 1) * T], dec_b, ftile[:],
                            0.0, ALU.mult, ALU.add))
                    sc_m = sv.n

                    wqr_m = quant_trans(cr_m, wqT, 0, (sp, vmm_m))

                    wkey[0] += 1

                    @block.tensor
                    def _(tensor):
                        tensor.wait_ge(sv.h, max(wqr_m, kv_m))
                        for kc in range(NCK):
                            wt = wqT[:, kc * 128:(kc + 1) * 128]
                            for pi, xt in enumerate((xn_hi, xn_lo)):
                                for j in range((T + 511) // 512):
                                    w = min(512, T - j * 512)
                                    sp.inc(mm(tensor,
                                        PS[j][:, 0:w], wt,
                                        xt[:, kc * T + j * 512: kc * T + j * 512 + w],
                                        start=(kc == 0 and pi == 0),
                                        stop=(kc == NCK - 1 and pi == 1),
                                        key=("l1p", wkey[0], kc)))
                    rmm_m = sp.n

                    @block.scalar
                    def _(scalar):
                        scalar.wait_ge(sp.h, rmm_m)
                        scalar.wait_ge(sv.h, sc_m)
                        inst = None
                        for j in range((T + 511) // 512):
                            w = min(512, T - j * 512)
                            inst = scalar.activation(
                                rtile[:, j * 512:j * 512 + w], PS[j][:, 0:w],
                                ACTF.Sigmoid, bias=0.0,
                                scale=scw[:, 24 + ic:25 + ic])
                        sa.inc(inst)
                    re_m = sa.n
                    for j in range(4):
                        ps_free[j] = (sa, re_m)

                    @block.vector
                    def _(vector):
                        vector.wait_ge(sa.h, re_m)
                        vector.wait_ge(sv.h, sc_m)
                        sv.inc(vector.tensor_mul(kvbuf[:, ic * T:(ic + 1) * T],
                                                 rtile[:],
                                                 kvbuf[:, ic * T:(ic + 1) * T]))
                    z_m = sv.n

                    @block.scalar
                    def _(scalar):
                        scalar.wait_ge(sv.h, z_m)
                        sa.inc(scalar.activation(ktile[:], kvbuf[:, ic * T:(ic + 1) * T],
                                                 ACTF.Square, bias=0.0, scale=1.0))
                    zs_m = sa.n

                    @block.tensor
                    def _(tensor):
                        tensor.wait_ge(sa.h, zs_m)
                        for j in range((T + 511) // 512):
                            w = min(512, T - j * 512)
                            sp.inc(tensor.matmul(
                                PS[j][0:1, 512:512 + w], onesf[:],
                                ktile[:, j * 512:j * 512 + w],
                                start=(ic == 0), stop=(ic == NCK - 1)))
                    zq_mms = sp.n

                    @block.vector
                    def _(vector):
                        vector.wait_ge(sv.h, z_m)
                        vector.wait_ge(sa.h, zs_m)
                        sv.inc(vector.tensor_scalar_mul(
                            kvbuf[:, ic * T:(ic + 1) * T],
                            kvbuf[:, ic * T:(ic + 1) * T], lnw_c[:, ic:ic + 1]))
                    ic_marks[ic] = sv.n

                y_done = sv.n

                @block.vector
                def _(vector):
                    vector.wait_ge(sp.h, zq_mms)
                    inst = None
                    for j in range((T + 511) // 512):
                        w = min(512, T - j * 512)
                        inst = vector.tensor_copy(zrow[0:1, j * 512:j * 512 + w],
                                                  PS[j][0:1, 512:512 + w])
                    sv.inc(inst)
                zr_m = sv.n

                @block.tensor
                def _(tensor):
                    tensor.wait_ge(sv.h, zr_m)
                    ps_wait(tensor, 0)
                    for tt in range(NT):
                        sp.inc(tensor.transpose(
                            PS[0][:, tt:tt + 1], zrow[0:1, tt * 128:(tt + 1) * 128],
                            ident[0:1, 0:1]))
                at_m = sp.n

                @block.vector
                def _(vector):
                    vector.wait_ge(sp.h, at_m)
                    sv.inc(vector.tensor_scalar(arms[:, 0:NT], PS[0][:, 0:NT],
                                                0.0009765625, 1e-6, ALU.mult, ALU.add))
                ps_free[0] = (sv, sv.n)
                am_m = sv.n

                @block.scalar
                def _(scalar):
                    scalar.wait_ge(sv.h, am_m)
                    sa.inc(scalar.activation(arms[:, 0:NT], arms[:, 0:NT], ACTF.Sqrt,
                                             bias=0.0, scale=1.0))
                as_m = sa.n

                @block.vector
                def _(vector):
                    vector.wait_ge(sa.h, as_m)
                    sv.inc(vector.reciprocal(arms[:, 0:NT], arms[:, 0:NT]))
                arms_m = sv.n

                for ic in range(NCK):
                    quant_w(wo_d, ic, 32 + ic, widx_c[0], wqTo, ic * 128,
                            (sp, zq_mms) if ic == 0 else None, tgt_stride=1024)
                    widx_c[0] += 1
                wo_m = sv.n

                @block.sync
                def _(sync):
                    sync.wait_ge(sv.h, wo_m)
                    for ic in range(NCK):
                        sds.issue(sync.dma_start(
                            out=ap2(zrow, ic * 128, [[2048, 1], [1, 128]]),
                            in_=ap2(scw, 32 + ic, [[48, 128], [1, 1]]),
                            allow_slow_non_contiguous=True), 0)

                @block.tensor
                def _(tensor):
                    sds.wait(tensor, 0)
                    ps_wait(tensor, 1)
                    for j in range(2):
                        sp.inc(tensor.matmul(
                            PS[1][:, j * 512:(j + 1) * 512], ones1r[0:1, :],
                            zrow[0:1, j * 512:(j + 1) * 512], start=True, stop=True))
                sob_mm = sp.n

                @block.vector
                def _(vector):
                    vector.wait_ge(sp.h, sob_mm)
                    sv.inc(vector.tensor_copy(sob[:], PS[1][:]))
                ps_free[1] = (sv, sv.n)
                sob_m = sv.n

                s2_marks = {}
                ymm_marks = {}
                for tt in range(NT):
                    sl = tt % 2
                    for kc in range(NCK):
                        ysl = kc % 2
                        yi = tt * NCK + kc

                        @block.vector
                        def _(vector):
                            vector.wait_ge(sv.h, max(wo_m, y_done))
                            if yi >= 2:
                                vector.wait_ge(sp.h, ymm_marks[yi - 2])
                            sv.inc(vector.tensor_copy(
                                ystg[:, ysl * 256: ysl * 256 + 128],
                                kvbuf[:, kc * T + tt * 128: kc * T + (tt + 1) * 128]))
                        yh_m = sv.n

                        @block.vector
                        def _(vector):
                            vector.wait_ge(sv.h, yh_m)
                            sv.inc(vector.scalar_tensor_tensor(
                                ystg[:, ysl * 256 + 128: ysl * 256 + 256],
                                kvbuf[:, kc * T + tt * 128: kc * T + (tt + 1) * 128],
                                1.0, ystg[:, ysl * 256: ysl * 256 + 128],
                                ALU.mult, ALU.subtract))
                        yl_m = sv.n

                        @block.tensor
                        def _(tensor):
                            tensor.wait_ge(sv.h, yl_m)
                            if kc == 0:
                                ps_wait(tensor, 0)
                                ps_wait(tensor, 2)
                            for pi in range(2):
                                yst = ystg[:, ysl * 256 + pi * 128:
                                           ysl * 256 + pi * 128 + 128]
                                for j in range(2):
                                    sp.inc(mm(tensor,
                                        PS[0 if j == 0 else 2][:, 0:512], yst,
                                        wqTo[:, kc * 1024 + j * 512: kc * 1024 + (j + 1) * 512],
                                        start=(kc == 0 and pi == 0),
                                        stop=(kc == NCK - 1 and pi == 1),
                                        key=("wo", tt, kc, pi)))
                        ymm_marks[yi] = sp.n
                    s2mm_m = sp.n

                    @block.vector
                    def _(vector):
                        vector.wait_ge(sp.h, s2mm_m)
                        if tt >= 2:
                            ss2.wait(vector, tt - 2)
                        vector.tensor_copy(s2stg[:, sl * 1024: sl * 1024 + 512],
                                           PS[0][:, 0:512])
                        sv.inc(vector.tensor_copy(
                            s2stg[:, sl * 1024 + 512: sl * 1024 + 1024],
                            PS[2][:, 0:512]))
                    s2e_m = sv.n
                    ps_free[0] = (sv, s2e_m)
                    ps_free[2] = (sv, s2e_m)

                    @block.sync
                    def _(sync):
                        sync.wait_ge(sv.h, s2e_m)
                        ss2.issue(sync.dma_start(
                            out=s2_scr[tt * 128:(tt + 1) * 128, :],
                            in_=s2stg[:, sl * 1024:(sl + 1) * 1024]), tt)
                    s2_marks[tt] = None

                @block.sync
                def _(sync):
                    for k in sorted(ss2.targets):
                        ss2.wait(sync, k)
            return sv.n, arms_m, sob_m

        if upto in ("ln2", "l1attn", "full"):
            ffn_end = {"sv": sv.n, "sp": sp.n, "sa": sa.n, "sq": sq.n}
            def ln2_entry(eng):
                eng.wait_ge(sv.h, ffn_end["sv"])
                eng.wait_ge(sp.h, ffn_end["sp"])
                eng.wait_ge(sa.h, ffn_end["sa"])

            ln2_done = emit_ln_phase(y_d, src_is_y=True, stats_store=True,
                                     tt_base=100, entry=ln2_entry)
            arms = st.enter_context(nc.sbuf_tensor("arms", [128, NT], F32))
            sob = st.enter_context(nc.sbuf_tensor("sob", [128, 1024], F32))
            if upto != "ln2":
                l1a_done, arms_m, sob_m = emit_l1_attn(arms, sob)

        wdt_scr = nc.dram_tensor("wdt_scr", [2, NFI, 128, 1024], BF16, kind="Internal").ap()

        def emit_l1_ffn(l1a_done, arms_m, sob_m):
            ph = contextlib.ExitStack()
            with ph:
                q_z = ph.enter_context(nc.sbuf_tensor("q_z", [128, 1024], F32))
                q_az = ph.enter_context(nc.sbuf_tensor("q_az", [128, 1024], F32))
                q_c1 = ph.enter_context(nc.sbuf_tensor("q_c1", [128, 1024], F32))
                q_c5 = ph.enter_context(nc.sbuf_tensor("q_c5", [128, 1024], F32))
                q_sc = ph.enter_context(nc.sbuf_tensor("q_sc", [128, 3 * (FI // 16)], F32))
                xtok = ph.enter_context(nc.sbuf_tensor("xtok", [128, 1024], F32))
                hstg = ph.enter_context(nc.sbuf_tensor("hstg", [128, 2048], F32))
                sgu = ph.enter_context(nc.sbuf_tensor("sgu", [128, 512], F32))
                rc6 = ph.enter_context(nc.sbuf_tensor("rc6", [128, 1], F32))
                gs = contextlib.ExitStack()
                wstg2 = gs.enter_context(nc.sbuf_tensor("wstg2", [128, 2 * FI], F32))
                wgT = gs.enter_context(nc.sbuf_tensor("wgT", [128, 2 * 2 * 8 * 512], BF16))
                wuT = gs.enter_context(nc.sbuf_tensor("wuT", [128, 2 * 2 * 8 * 512], BF16))
                sq2 = mkstream("sq2")
                swg = mkstream("swg", nbuf=4)
                shw = mkstream("shw", nbuf=4)
                shr = mkstream("shr")
                swd = mkstream("swd")
                swd2 = mkstream("swd2", nbuf=4)
                ss2r = mkstream("ss2r")
                sx2r = mkstream("sx2r")
                syf = mkstream("syf", nbuf=4)

                @block.vector
                def _(vector):
                    sv.inc(vector.memset(rc6[:], 6.0))
                rc6_0 = sv.n

                @block.vector
                def _(vector):
                    vector.wait_ge(sv.h, rc6_0)
                    sv.inc(vector.reciprocal(rc6[:], rc6[:]))
                rc6_m = sv.n

                vchain = [0]

                def fp4_chain(W, src_ap, dst_ap, entry_marks, scoff=0):
                    NB = W // 16
                    m0 = sv.n

                    @block.vector
                    def _(vector):
                        for s, v in entry_marks:
                            vector.wait_ge(s.h, v)
                        vector.wait_ge(sv.h, chain_free[0])
                        sv.inc(vector.tensor_reduce(
                            q_sc[:, scoff:scoff + NB],
                            bass.AP(tensor=src_ap.tensor, offset=src_ap.offset,
                                    ap=[list(src_ap.ap[0]), [16, NB], [1, 16]]),
                            AX.X, ALU.max, apply_absolute_value=True))
                    m = sv.n

                    @block.vector
                    def _(vector):
                        vector.wait_ge(sv.h, m)
                        sv.inc(vector.tensor_scalar(
                            q_sc[:, scoff:scoff + NB], q_sc[:, scoff:scoff + NB],
                            rc6[:, 0:1], 1e-12, ALU.mult, ALU.max))
                    m = sv.n

                    @block.vector
                    def _(vector):
                        vector.wait_ge(sv.h, m)
                        sv.inc(vector.reciprocal(
                            q_sc[:, scoff + NB:scoff + 2 * NB],
                            q_sc[:, scoff:scoff + NB]))
                    m = sv.n

                    def bc3(col0):
                        return bass.AP(tensor=q_sc, offset=col0,
                                       ap=[[3 * (FI // 16), 128], [1, NB], [0, 16]])

                    @block.vector
                    def _(vector):
                        vector.wait_ge(sv.h, m)
                        sv.inc(vector.tensor_tensor(q_z[:, 0:W], src_ap,
                                                    bc3(scoff + NB), ALU.mult))
                    m = sv.n

                    @block.vector
                    def _(vector):
                        vector.wait_ge(sv.h, m)
                        sv.inc(vector.scalar_tensor_tensor(
                            q_az[:, 0:W], q_z[:, 0:W], -1.0, q_z[:, 0:W],
                            ALU.mult, ALU.max))
                    m = sv.n

                    @block.vector
                    def _(vector):
                        vector.wait_ge(sv.h, m)
                        sv.inc(vector.tensor_single_scalar(q_c1[:, 0:W], q_az[:, 0:W],
                                                           0.25, ALU.is_gt))
                    m = sv.n
                    for thr in (0.75, 1.25, 1.75):
                        @block.vector
                        def _(vector, thr=thr, m=m):
                            vector.wait_ge(sv.h, m)
                            sv.inc(vector.scalar_tensor_tensor(
                                q_c1[:, 0:W], q_az[:, 0:W], thr, q_c1[:, 0:W],
                                ALU.is_gt, ALU.add))
                        m = sv.n

                    @block.vector
                    def _(vector):
                        vector.wait_ge(sv.h, m)
                        sv.inc(vector.tensor_single_scalar(q_c5[:, 0:W], q_az[:, 0:W],
                                                           2.5, ALU.is_gt))
                    m = sv.n

                    @block.vector
                    def _(vector):
                        vector.wait_ge(sv.h, m)
                        sv.inc(vector.scalar_tensor_tensor(
                            q_c5[:, 0:W], q_az[:, 0:W], 3.5, q_c5[:, 0:W],
                            ALU.is_gt, ALU.add))
                    m = sv.n

                    @block.vector
                    def _(vector):
                        vector.wait_ge(sv.h, m)
                        sv.inc(vector.scalar_tensor_tensor(
                            q_c5[:, 0:W], q_c1[:, 0:W], 0.5, q_c5[:, 0:W],
                            ALU.mult, ALU.add))
                    m = sv.n

                    @block.vector
                    def _(vector):
                        vector.wait_ge(sv.h, m)
                        sv.inc(vector.tensor_single_scalar(q_c1[:, 0:W], q_az[:, 0:W],
                                                           5.0, ALU.is_gt))
                    m = sv.n

                    @block.vector
                    def _(vector):
                        vector.wait_ge(sv.h, m)
                        sv.inc(vector.scalar_tensor_tensor(
                            q_c5[:, 0:W], q_c1[:, 0:W], 2.0, q_c5[:, 0:W],
                            ALU.mult, ALU.add))
                    m = sv.n

                    @block.vector
                    def _(vector):
                        vector.wait_ge(sv.h, m)
                        sv.inc(vector.tensor_tensor(q_c5[:, 0:W], q_c5[:, 0:W],
                                                    bc3(scoff), ALU.mult))
                    m = sv.n

                    @block.vector
                    def _(vector):
                        vector.wait_ge(sv.h, m)
                        sv.inc(vector.tensor_single_scalar(q_az[:, 0:W], q_z[:, 0:W],
                                                           0.0, ALU.is_gt))
                    m = sv.n

                    @block.vector
                    def _(vector):
                        vector.wait_ge(sv.h, m)
                        sv.inc(vector.tensor_mul(q_c1[:, 0:W], q_c5[:, 0:W],
                                                 q_az[:, 0:W]))
                    m = sv.n

                    @block.vector
                    def _(vector):
                        vector.wait_ge(sv.h, m)
                        sv.inc(vector.scalar_tensor_tensor(
                            dst_ap, q_c1[:, 0:W], 2.0, q_c5[:, 0:W],
                            ALU.mult, ALU.subtract))
                    chain_free[0] = sv.n
                    return sv.n

                chain_free = [0]

                def tsplit(src, nblk, dst_hi_fn, dst_lo_fn, wait_marks, aidx, src_off=0):
                    for w0 in range(0, nblk, 16):
                        blocks = list(range(w0, min(w0 + 16, nblk)))

                        @block.tensor
                        def _(tensor, blocks=blocks, w0=w0):
                            if w0 == 0:
                                for s, v in wait_marks:
                                    tensor.wait_ge(s.h, v)
                            ps_wait(tensor, aidx)
                            ps_wait(tensor, aidx + 1)
                            for b in blocks:
                                pa = PS[aidx + (b - w0) // 8]
                                po = ((b - w0) % 8) * 128
                                sp.inc(tensor.transpose(
                                    pa[:, po:po + 128],
                                    src[:, src_off + b * 128: src_off + (b + 1) * 128],
                                    ident[:]))
                        t_m = sp.n

                        @block.vector
                        def _(vector, blocks=blocks, w0=w0, t_m=t_m):
                            vector.wait_ge(sp.h, t_m)
                            inst = None
                            for b in blocks:
                                pa = PS[aidx + (b - w0) // 8]
                                po = ((b - w0) % 8) * 128
                                inst = vector.tensor_copy(dst_hi_fn(b), pa[:, po:po + 128])
                            sv.inc(inst)
                        h_m = sv.n

                        @block.vector
                        def _(vector, blocks=blocks, w0=w0, h_m=h_m):
                            vector.wait_ge(sv.h, h_m)
                            inst = None
                            for b in blocks:
                                pa = PS[aidx + (b - w0) // 8]
                                po = ((b - w0) % 8) * 128
                                inst = vector.scalar_tensor_tensor(
                                    dst_lo_fn(b), pa[:, po:po + 128], 1.0,
                                    dst_hi_fn(b), ALU.mult, ALU.subtract)
                            sv.inc(inst)
                        ps_free[aidx] = (sv, sv.n)
                        ps_free[aidx + 1] = (sv, sv.n)
                    return sv.n


                xq_marks = {}
                xstat_marks = {}
                for tt in range(NT):
                    xsl = tt % 2

                    @block.sync
                    def _(sync):
                        if tt == 0:
                            sync.wait_ge(sv.h, l1a_done)
                        if tt >= 2:
                            sync.wait_ge(sv.h, xstat_marks[tt - 2])
                        sq2.issue(sync.dma_start(
                            out=hstg[:, xsl * 1024: xsl * 1024 + 1024],
                            in_=y_d[tt * 128:(tt + 1) * 128, :]), tt)

                    @block.vector
                    def _(vector):
                        sq2.wait(vector, tt)
                        if tt == 0:
                            vector.wait_ge(sv.h, l1a_done)
                        else:
                            vector.wait_ge(sv.h, xq_marks[tt - 1])
                        sv.inc(vector.tensor_scalar(
                            xtok[:], hstg[:, xsl * 1024: xsl * 1024 + 1024],
                            mstats[:, 2 * tt:2 * tt + 1],
                            mstats[:, 2 * tt + 1:2 * tt + 2], ALU.subtract, ALU.mult))
                    xn_m = sv.n
                    xstat_marks[tt] = xn_m

                    q_m = fp4_chain(1024, xtok[:], xtok[:], [(sv, xn_m)])

                    xq_marks[tt] = tsplit(
                        xtok, 8,
                        lambda b, tt=tt: xn_hi[:, b * T + tt * 128: b * T + (tt + 1) * 128],
                        lambda b, tt=tt: xn_lo[:, b * T + tt * 128: b * T + (tt + 1) * 128],
                        [(sv, q_m)], 0)
                xq_done = sv.n

                wd_marks = {}
                for icd in range(NCK):
                    @block.sync
                    def _(sync):
                        if icd >= 2:
                            sync.wait_ge(sv.h, wd_marks[icd - 2])
                        swg.issue(sync.dma_start(
                            out=wstg2[:, (icd % 2) * FI:(icd % 2) * FI + FI],
                            in_=wd_d[icd * 128:(icd + 1) * 128, :]), icd)

                    @block.vector
                    def _(vector):
                        swg.wait(vector, icd)
                        sv.inc(vector.engine_nop())
                    wl_m = sv.n

                    for t0, tw in ((0, 1024), (1024, 1024), (2048, 768)):
                        b = (icd % 2) * FI + t0
                        q_m = fp4_chain(tw, wstg2[:, b:b + tw],
                                        wstg2[:, b:b + tw], [(sv, wl_m)])

                    def dhi(b, icd=icd):
                        return wgT[:, (b % 8) * 512 + (icd % 4) * 128:
                                   (b % 8) * 512 + (icd % 4) * 128 + 128] if False else                                wgT[:, b * 128: b * 128 + 128]

                    def dlo(b, icd=icd):
                        return wuT[:, b * 128: b * 128 + 128]

                    wm = [(sv, q_m)]
                    if icd >= 1:
                        for hl in range(2):
                            pidx = 100 + (icd - 1) * 2 + hl
                            wm.append((swd.sems[pidx % swd.nbuf], swd.targets[pidx]))
                    t_m = tsplit(wstg2, 22, dhi, dlo, wm, 0,
                                 src_off=(icd % 2) * FI)

                    @block.sync
                    def _(sync):
                        sync.wait_ge(sv.h, t_m)
                        for hl, tsrc in ((0, wgT), (1, wuT)):
                            swd.issue(sync.dma_start(
                                out=ap2(wdt_scr.tensor, hl * NFI * 128 * 1024 + icd * 128,
                                        [[1024, 128], [128 * 1024, 22], [1, 128]]),
                                in_=ap2(tsrc, 0, [[2 * 2 * 8 * 512, 128], [128, 22], [1, 128]])),
                                100 + icd * 2 + hl)
                    wd_marks[icd] = sv.n
                wd_done = sv.n

                sgu_guard = [0]
                spans = [(i * 512, min(512, FI - i * 512)) for i in range((FI + 511) // 512)]
                gw_marks = {}
                wcount = [8]
                SPW = 2 * 8 * 512

                def emit_wchain(spi, wi, j):
                    f0, fw = spans[spi]
                    wsrc, tgtT = ((wg_d, wgT), (wu_d, wuT))[wi]
                    po = (spi % 2) * SPW
                    ic2 = f0 // 128 + j
                    widx = wcount[0]
                    wcount[0] += 1

                    @block.sync
                    def _(sync):
                        if widx - 4 in gw_marks:
                            sync.wait_ge(sv.h, gw_marks[widx - 4])
                        elif widx in (8, 9, 10, 11):
                            sync.wait_ge(sv.h, wd_done)
                        swg.issue(sync.dma_start(
                            out=wstg2[:, (widx % 4) * 1024:(widx % 4) * 1024 + 1024],
                            in_=wsrc[ic2 * 128:(ic2 + 1) * 128, :]), widx)

                    @block.vector
                    def _(vector):
                        swg.wait(vector, widx)
                        sv.inc(vector.engine_nop())
                    wl_m = sv.n

                    q_m = fp4_chain(
                        1024,
                        wstg2[:, (widx % 4) * 1024:(widx % 4) * 1024 + 1024],
                        wstg2[:, (widx % 4) * 1024:(widx % 4) * 1024 + 1024],
                        [(sv, wl_m)])
                    return (widx, tgtT, ic2 * 128 - f0, po, q_m)

                def emit_wtsplit(task):
                    widx, tgtT, off, po, q_m = task

                    def dhi2(b, tgtT=tgtT, off=off, po=po):
                        return tgtT[:, po + b * 512 + off: po + b * 512 + off + 128]

                    def dlo2(b, tgtT=tgtT, off=off, po=po):
                        return tgtT[:, po + 8 * 512 + b * 512 + off:
                                    po + 8 * 512 + b * 512 + off + 128]

                    wm2 = [(sv, q_m)]
                    if widx == 8:
                        for k2, tgt in sorted(swd.targets.items()):
                            wm2.append((swd.sems[k2 % swd.nbuf], tgt))
                    gw_marks[widx] = tsplit(wstg2, 8, dhi2, dlo2, wm2, 2,
                                            src_off=(widx % 4) * 1024)
                    return gw_marks[widx]

                span_tasks = [[(wi, j) for wi in range(2)
                               for j in range(spans[s][1] // 128)]
                              for s in range(len(spans))]
                span_ready = {}
                for wi, j in span_tasks[0]:
                    emit_wtsplit(emit_wchain(0, wi, j))
                span_ready[0] = sv.n

                for spi, (f0, fw) in enumerate(spans):
                    nxt = span_tasks[spi + 1] if spi + 1 < len(spans) else []
                    po = (spi % 2) * SPW
                    pend = [emit_wchain(spi + 1, wi, j) for wi, j in nxt[0:4]]
                    nchained = len(pend)
                    for tt in range(NT):
                        p = tt % 2

                        @block.tensor
                        def _(tensor):
                            tensor.wait_ge(sv.h, max(span_ready[spi], xq_done))
                            ps_wait(tensor, p)
                            for kc in range(NCK):
                                sh = xn_hi[:, kc * T + tt * 128: kc * T + (tt + 1) * 128]
                                sl = xn_lo[:, kc * T + tt * 128: kc * T + (tt + 1) * 128]
                                whg = wgT[:, po + kc * 512: po + kc * 512 + fw]
                                wlg = wgT[:, po + 8 * 512 + kc * 512:
                                          po + 8 * 512 + kc * 512 + fw]
                                whu = wuT[:, po + kc * 512: po + kc * 512 + fw]
                                wlu = wuT[:, po + 8 * 512 + kc * 512:
                                          po + 8 * 512 + kc * 512 + fw]
                                seq = [(sh, whg, 0, kc == 0, False),
                                       (sh, wlg, 0, False, False),
                                       (sh, whu, 1, kc == 0, False),
                                       (sh, wlu, 1, False, False),
                                       (sl, whg, 0, False, kc == NCK - 1),
                                       (sl, whu, 1, False, kc == NCK - 1)]
                                for sop, mop, gi, st_, sp_ in seq:
                                    sp.inc(mm(tensor,
                                        PS[p][:, gi * 512: gi * 512 + fw], sop, mop,
                                        start=st_, stop=sp_,
                                        key=("gu", spi, tt, kc, sop is sl)))
                        gu_m = sp.n

                        @block.scalar
                        def _(scalar):
                            scalar.wait_ge(sp.h, gu_m)
                            if sgu_guard[0]:
                                scalar.wait_ge(sv.h, sgu_guard[0])
                            sa.inc(scalar.activation(sgu[:, 0:fw], PS[p][:, 0:fw],
                                                     ACTF.Silu, bias=0.0, scale=1.0))
                        si_m = sa.n

                        kpr = spi * NT + tt
                        hslot = kpr % 4

                        @block.vector
                        def _(vector):
                            vector.wait_ge(sa.h, si_m)
                            if kpr >= 4:
                                shw.wait(vector, kpr - 4)
                            sv.inc(vector.tensor_tensor(
                                hstg[:, hslot * 512: hslot * 512 + fw],
                                sgu[:, 0:fw], PS[p][:, 512:512 + fw], ALU.mult))
                        hv_m = sv.n
                        sgu_guard[0] = hv_m
                        ps_free[p] = (sv, hv_m)

                        @block.sync
                        def _(sync):
                            sync.wait_ge(sv.h, hv_m)
                            shw.issue(sync.dma_start(
                                out=ap2(h_scr.tensor, tt * 128 * FI + f0,
                                        [[FI, 128], [1, fw]]),
                                in_=hstg[:, hslot * 512: hslot * 512 + fw]),
                                spi * NT + tt)

                        if 0 <= tt - 2 < len(nxt):
                            emit_wtsplit(pend.pop(0))
                            if nchained < len(nxt):
                                wi, j = nxt[nchained]
                                pend.append(emit_wchain(spi + 1, wi, j))
                                nchained += 1
                    while pend:
                        emit_wtsplit(pend.pop(0))
                        if nchained < len(nxt):
                            wi, j = nxt[nchained]
                            pend.append(emit_wchain(spi + 1, wi, j))
                            nchained += 1
                    if nxt:
                        span_ready[spi + 1] = sv.n
                gu_done = sv.n
                gu_mm_final = sp.n
                gs.close()

                NG = min(4, NT)
                if NCK * T >= NG * NFI * 128:
                    hqh, hql = xn_hi, xn_lo
                else:
                    hqh = ph.enter_context(nc.sbuf_tensor("hqh", [128, NG * NFI * 128], BF16))
                    hql = ph.enter_context(nc.sbuf_tensor("hql", [128, NG * NFI * 128], BF16))
                hstg2 = ph.enter_context(nc.sbuf_tensor("hstg2", [128, 2 * FI], F32))
                wdk = ph.enter_context(nc.sbuf_tensor("wdk", [128, 4 * 1024], BF16))
                fstg = ph.enter_context(nc.sbuf_tensor("fstg", [128, 4 * 1024], F32))
                grp_marks = {}
                hq_ev = {}
                x2_marks = {}
                NGRP = NT // NG

                def emit_hchain(grp, tg, extra=()):
                    tt = grp * NG + tg
                    hsl = tg % 2

                    @block.sync
                    def _(sync):
                        if grp == 0 and tg == 0:
                            for k in sorted(shw.targets):
                                shw.wait(sync, k)
                            sync.wait_ge(sp.h, gu_mm_final)
                            sync.wait_ge(sv.h, gu_done)
                        for s, v in extra:
                            sync.wait_ge(s.h, v)
                        shr.issue(sync.dma_start(
                            out=hstg2[:, hsl * FI: hsl * FI + FI],
                            in_=h_scr[tt * 128:(tt + 1) * 128, :]), tt)

                    @block.vector
                    def _(vector):
                        shr.wait(vector, tt)
                        if grp == 0 and tg == 0:
                            vector.wait_ge(sv.h, gu_done)
                        sv.inc(vector.engine_nop())
                    hl_m = sv.n
                    q_m = hl_m
                    for t0, tw in ((0, 1024), (1024, 1024), (2048, 768)):
                        b = hsl * FI + t0
                        q_m = fp4_chain(tw, hstg2[:, b:b + tw],
                                        hstg2[:, b:b + tw], [(sv, hl_m)])
                    return q_m

                def emit_htsplit(grp, tg, q_m, aidx, guard_mm):
                    hsl = tg % 2

                    def hqhi(b, tg=tg):
                        return hqh[:, tg * NFI * 128 + b * 128:
                                   tg * NFI * 128 + (b + 1) * 128]

                    def hqlo(b, tg=tg):
                        return hql[:, tg * NFI * 128 + b * 128:
                                   tg * NFI * 128 + (b + 1) * 128]

                    hq_ev[(grp, tg)] = tsplit(hstg2, NFI, hqhi, hqlo,
                                              [(sv, q_m)] + list(guard_mm), aidx,
                                              src_off=hsl * FI)
                    return hq_ev[(grp, tg)]

                def serial_group(grp, guard_mm):
                    for tg in range(NG):
                        extra = []
                        if tg >= 2:
                            extra.append((sv, hq_ev[(grp, tg - 2)]))
                        elif grp > 0:
                            extra.append((sv, hq_ev[(grp - 1, tg + NG - 2)]))
                        q = emit_hchain(grp, tg, extra)
                        emit_htsplit(grp, tg, q, (tg % 2) * 2, guard_mm)

                serial_group(0, [(sp, gu_mm_final)])

                for grp in range(NGRP):
                    nxtg = grp + 1 if grp + 1 < NGRP else None
                    pendc = {}
                    if nxtg is not None and NG == 4:
                        pendc[0] = emit_hchain(nxtg, 0, [(sv, hq_ev[(grp, 2)])])
                        pendc[1] = emit_hchain(nxtg, 1, [(sv, hq_ev[(grp, 3)])])
                    hq_m = max(hq_ev[(grp, t)] for t in range(NG))

                    dn_m = 0
                    for kc in range(NFI):
                        ksl = kc % 2

                        @block.sync
                        def _(sync):
                            if grp == 0 and kc == 0:
                                for k in sorted(swd.targets):
                                    swd.wait(sync, k)
                                sync.wait_ge(sp.h, gu_mm_final)
                                sync.wait_ge(sv.h, gu_done)
                            if kc >= 2:
                                sync.wait_ge(sp.h, dn_kc_marks[kc - 2])
                            elif grp > 0:
                                sync.wait_ge(sp.h, grp_mm_marks[grp - 1])
                            swd.issue(sync.dma_start(
                                out=wdk[:, ksl * 2 * 1024:
                                        ksl * 2 * 1024 + 1024],
                                in_=ap2(wdt_scr.tensor, kc * 128 * 1024,
                                        [[1024, 128], [1, 1024]])),
                                200 + grp * NFI + kc)

                        @block.scalar
                        def _(scalar):
                            if grp == 0 and kc == 0:
                                for k in sorted(swd.targets):
                                    if k < 200:
                                        swd.wait(scalar, k)
                                scalar.wait_ge(sp.h, gu_mm_final)
                                scalar.wait_ge(sv.h, gu_done)
                            if kc >= 2:
                                scalar.wait_ge(sp.h, dn_kc_marks[kc - 2])
                            elif grp > 0:
                                scalar.wait_ge(sp.h, grp_mm_marks[grp - 1])
                            swd2.issue(scalar.dma_start(
                                out=wdk[:, (ksl * 2 + 1) * 1024:
                                        (ksl * 2 + 1) * 1024 + 1024],
                                in_=ap2(wdt_scr.tensor,
                                        NFI * 128 * 1024 + kc * 128 * 1024,
                                        [[1024, 128], [1, 1024]])),
                                grp * NFI + kc)

                        @block.tensor
                        def _(tensor):
                            swd.wait(tensor, 200 + grp * NFI + kc)
                            swd2.wait(tensor, grp * NFI + kc)
                            tensor.wait_ge(sv.h, hq_m)
                            if kc == 0:
                                for j in range(4):
                                    ps_wait(tensor, j)
                            wh = wdk[:, ksl * 2 * 1024: ksl * 2 * 1024 + 1024]
                            wl = wdk[:, (ksl * 2 + 1) * 1024: (ksl * 2 + 1) * 1024 + 1024]
                            for tg in range(NG):
                                sh = hqh[:, tg * NFI * 128 + kc * 128:
                                         tg * NFI * 128 + (kc + 1) * 128]
                                sl = hql[:, tg * NFI * 128 + kc * 128:
                                         tg * NFI * 128 + (kc + 1) * 128]
                                seq = [(sh, wh, kc == 0, False), (sh, wl, False, False),
                                       (sl, wh, False, kc == NFI - 1)]
                                for sop, mop, st_, sp_ in seq:
                                    for j in range(2):
                                        sp.inc(mm(tensor,
                                            PS[tg][:, j * 512:(j + 1) * 512],
                                            sop, mop[:, j * 512:(j + 1) * 512],
                                            start=st_, stop=sp_,
                                            key=("dn", grp, kc, tg, sop is sl)))
                        dn_kc_marks[kc] = sp.n
                    grp_mm_marks[grp] = sp.n
                    dn_m = sp.n

                    for tg in range(NG):
                        tt = grp * NG + tg
                        csl = tt % 2

                        @block.scalar
                        def _(scalar):
                            scalar.wait_ge(sp.h, dn_m)
                            sa.inc(scalar.activation(fstg[:, tg * 1024:(tg + 1) * 1024],
                                                     PS[tg][:], ACTF.Silu,
                                                     bias=0.0, scale=1.0))
                        g2_m = sa.n
                        ps_free[tg] = (sa, g2_m)

                        @block.sync
                        def _(sync):
                            sync.wait_ge(sp.h, dn_m)
                            if tt < 2:
                                for k in sorted(shw.targets):
                                    shw.wait(sync, k)
                                sync.wait_ge(sv.h, gu_done)
                            if tt >= 2:
                                syf.wait(sync, tt - 2)
                            if tt >= 1:
                                sync.wait_ge(sv.h, x2_marks[tt - 1])
                            ss2r.issue(sync.dma_start(
                                out=hstg[:, csl * 1024: csl * 1024 + 1024],
                                in_=s2_scr[tt * 128:(tt + 1) * 128, :]), tt)
                            sx2r.issue(sync.dma_start(
                                out=xtok[:, 0:1024],
                                in_=y_d[tt * 128:(tt + 1) * 128, :]), tt)

                        @block.vector
                        def _(vector):
                            ss2r.wait(vector, tt)
                            vector.wait_ge(sa.h, g2_m)
                            vector.wait_ge(sv.h, sob_m)
                            vector.tensor_mul(
                                hstg[:, csl * 1024: csl * 1024 + 1024],
                                hstg[:, csl * 1024: csl * 1024 + 1024], sob[:])
                            sv.inc(vector.engine_nop())
                        c1_m = sv.n

                        @block.vector
                        def _(vector):
                            vector.wait_ge(sv.h, c1_m)
                            sv.inc(vector.tensor_mul(
                                hstg[:, csl * 1024: csl * 1024 + 1024],
                                hstg[:, csl * 1024: csl * 1024 + 1024],
                                fstg[:, tg * 1024:(tg + 1) * 1024]))
                        c2_m = sv.n

                        @block.vector
                        def _(vector):
                            sx2r.wait(vector, tt)
                            vector.wait_ge(sv.h, c2_m)
                            vector.wait_ge(sv.h, arms_m)
                            sv.inc(vector.scalar_tensor_tensor(
                                hstg[:, csl * 1024: csl * 1024 + 1024],
                                hstg[:, csl * 1024: csl * 1024 + 1024],
                                arms[:, tt:tt + 1],
                                xtok[:, 0:1024],
                                ALU.mult, ALU.add))
                        x2_marks[tt] = sv.n
                        x2_m = sv.n

                        @block.sync
                        def _(sync):
                            sync.wait_ge(sv.h, x2_m)
                            syf.issue(sync.dma_start(
                                out=y_d[tt * 128:(tt + 1) * 128, :],
                                in_=hstg[:, csl * 1024: csl * 1024 + 1024]), tt)

                        if nxtg is not None and NG == 4:
                            if tg == 1:
                                emit_htsplit(nxtg, 0, pendc[0], 0, [(sp, dn_m)])
                            elif tg == 2:
                                pendc[2] = emit_hchain(
                                    nxtg, 2, [(sv, hq_ev[(nxtg, 0)])])
                            elif tg == 3:
                                emit_htsplit(nxtg, 1, pendc[1], 2, [(sp, dn_m)])
                                pendc[3] = emit_hchain(
                                    nxtg, 3, [(sv, hq_ev[(nxtg, 1)])])
                                emit_htsplit(nxtg, 2, pendc[2], 0, [(sp, dn_m)])
                                emit_htsplit(nxtg, 3, pendc[3], 2, [(sp, dn_m)])
                    if nxtg is not None and NG != 4:
                        serial_group(nxtg, [(sp, dn_m)])
                    grp_marks[grp] = sv.n

                @block.sync
                def _(sync):
                    for k in sorted(syf.targets):
                        syf.wait(sync, k)
            return sv.n

        if upto == "full":
            dn_kc_marks = {}
            grp_mm_marks = {}
            l1f_done = emit_l1_ffn(l1a_done, arms_m, sob_m)

        if upto == "ln1":
            pass

        @block.sync
        def _(sync):
            sync.wait_ge(so_.h, so_.n)
            for strm in all_streams:
                for s in strm.sems:
                    if s.n:
                        sync.wait_ge(s.h, s.n)

    return nc



def _host_prep(inputs):
    f32 = np.float32
    x = np.ascontiguousarray(np.asarray(inputs["L0_attn_x"], f32))
    wqkv = np.asarray(inputs["L0_attn_w_qkv"], f32)
    wproj = np.asarray(inputs["L0_attn_w_proj"], f32)
    k0 = np.asarray(inputs["L0_ffn_k0"]).view(np.uint8)
    k1 = np.asarray(inputs["L0_ffn_k1"]).view(np.uint8)
    k2 = np.asarray(inputs["L0_ffn_k2"]).view(np.uint8)
    com = {
        "wqkv_t": np.ascontiguousarray(wqkv.reshape(8, 128, 24, 128).transpose(2, 0, 1, 3)),
        "wproj_t": np.ascontiguousarray(wproj.reshape(8, 128, 8, 128).transpose(2, 0, 1, 3)),
        "k0_t": np.ascontiguousarray(k0.reshape(16, 128, 32, 128).transpose(2, 1, 0, 3)),
        "k1_t": np.ascontiguousarray(k1.reshape(32, 128, 32, 128).transpose(2, 1, 0, 3)),
        "k2_t": np.ascontiguousarray(k2.reshape(32, 128, 8, 128).transpose(2, 1, 0, 3)),
        "s0": np.asarray(inputs["L0_ffn_s0"], f32).reshape(1, 1),
        "s1": np.asarray(inputs["L0_ffn_s1"], f32).reshape(1, 1),
        "s2": np.asarray(inputs["L0_ffn_s2"], f32).reshape(1, 1),
        "wr": np.ascontiguousarray(np.asarray(inputs["L1_attn_Wr"], f32)),
        "wk": np.ascontiguousarray(np.asarray(inputs["L1_attn_Wk"], f32)),
        "wv": np.ascontiguousarray(np.asarray(inputs["L1_attn_Wv"], f32)),
        "wo": np.ascontiguousarray(np.asarray(inputs["L1_attn_Wo"], f32)),
        "decay8": np.ascontiguousarray(np.asarray(inputs["L1_attn_decay"], f32).reshape(8, 128)),
        "lnw8": np.ascontiguousarray(np.asarray(inputs["L1_attn_ln_w"], f32).reshape(8, 128)),
        "wg": np.ascontiguousarray(np.asarray(inputs["L1_ffn_w_gate"], f32)),
        "wu": np.ascontiguousarray(np.asarray(inputs["L1_ffn_w_up"], f32)),
        "wd": np.ascontiguousarray(np.asarray(inputs["L1_ffn_w_down"], f32)),
        "identf": np.eye(128, dtype=f32),
        "identb": np.eye(128, dtype=f32).astype(ml_dtypes.bfloat16),
        "iota": np.ascontiguousarray(np.broadcast_to(np.arange(2048, dtype=f32), (128, 2048))),
        "masku": np.triu(np.ones((128, 128), f32)),
        "blk2": np.ascontiguousarray(
            np.stack([np.r_[np.ones(64), np.zeros(64)],
                      np.r_[np.zeros(64), np.ones(64)]], axis=1).astype(ml_dtypes.bfloat16)),
        "sel2": np.ascontiguousarray(
            np.stack([np.r_[np.ones(64), np.zeros(64)],
                      np.r_[np.zeros(64), np.ones(64)]], axis=0).astype(f32)),
    }
    return x, com


def kernel(**inputs):
    x, com = _host_prep(inputs)
    nc = build()
    in_maps = [dict(com, x=np.ascontiguousarray(x[c])) for c in range(8)]
    res = run_bass_kernel_spmd(nc, in_maps, core_ids=list(range(8)))
    return np.stack([np.asarray(r["y"], np.float32) for r in res.results])


# version 15
# speedup vs baseline: 1.2917x; 1.1046x over previous
import contextlib
import numpy as np
import ml_dtypes
import concourse.bass as bass
import concourse.mybir as mybir
from concourse.bass_utils import run_bass_kernel_spmd

F32 = mybir.dt.float32
BF16 = mybir.dt.bfloat16
U8 = mybir.dt.uint8
ALU = mybir.AluOpType
ACTF = mybir.ActivationFunctionType
AX = mybir.AxisListType

D = 1024
NCK = D // 128
NH = 16
DH = 64
FF = 4096
FI = 2816
NFI = FI // 128


class Sem:

    def __init__(self, stack, nc, name):
        self.h = stack.enter_context(nc.semaphore(name))
        self.n = 0

    def inc(self, inst, k=1):
        inst.then_inc(self.h, k)
        self.n += k
        return inst


class Stream:

    def __init__(self, stack, nc, name, nbuf=2):
        self.sems = [Sem(stack, nc, f"{name}{i}") for i in range(nbuf)]
        self.nbuf = nbuf
        self.targets = {}

    def issue(self, inst, idx):
        s = self.sems[idx % self.nbuf]
        s.inc(inst, 16)
        self.targets[idx] = s.n

    def wait(self, eng, idx):
        eng.wait_ge(self.sems[idx % self.nbuf].h, self.targets[idx])


def ap2(tensor, offset, dims):
    return bass.AP(tensor=tensor, offset=offset, ap=[list(d) for d in dims])


class MM:

    def __init__(self):
        self.tmpl = None
        self.key = None
        self.n = 0

    def __call__(self, tensor, out, lhsT, rhs, start, stop, key=None):
        if key is not None and key == self.key and self.tmpl is not None:
            t = self.tmpl
            rhs_l = tensor.lower_ap_or_imm(rhs)
            out_l = tensor.lower_ap_or_imm(out)
            self.n += 1
            repl = mybir.InstMatmult(
                replication_resolution=t.replication_resolution,
                replication_shift_amnt=t.replication_shift_amnt,
                replication_num_rows=t.replication_num_rows,
                start_tensor_calc=start, stop_tensor_calc=stop,
                is_transpose=t.is_transpose, ldweights=False,
                name=f"{t.name}_r{self.n}", engine=t.engine, debug=t.debug,
                ins=[rhs_l, t.ins[1]], outs=[out_l])
            return tensor.add_instruction(repl)
        i = tensor.matmul(out, lhsT, rhs, start=start, stop=stop)
        self.tmpl = i.ins if key is not None else None
        self.key = key
        return i


def build(T=2048, dbg=(), upto="full"):
    NT = T // 128
    QB = min(1024, T)
    NQB = T // QB
    NJH = max(1, T // 1024)
    nc = bass.Bass("TRN2", target_bir_lowering=False)

    x_d = nc.dram_tensor("x", [T, D], F32, kind="ExternalInput").ap()
    y_d = nc.dram_tensor("y", [T, D], F32, kind="ExternalOutput").ap()
    wqkv_d = nc.dram_tensor("wqkv_t", [24, 8, 128, 128], F32, kind="ExternalInput").ap()
    wproj_d = nc.dram_tensor("wproj_t", [8, 8, 128, 128], F32, kind="ExternalInput").ap()
    k0_d = nc.dram_tensor("k0_t", [32, 128, 16, 128], U8, kind="ExternalInput").ap()
    k1_d = nc.dram_tensor("k1_t", [32, 128, 32, 128], U8, kind="ExternalInput").ap()
    k2_d = nc.dram_tensor("k2_t", [8, 128, 32, 128], U8, kind="ExternalInput").ap()
    k0bf_d = nc.dram_tensor("k0bf_scr", [32, 128, 16, 128], BF16, kind="Internal").ap()
    k1bf_d = nc.dram_tensor("k1bf_scr", [32, 128, 32, 128], BF16, kind="Internal").ap()
    k2bf_d = nc.dram_tensor("k2bf_scr", [8, 128, 32, 128], BF16, kind="Internal").ap()
    s0_d = nc.dram_tensor("s0", [1, 1], F32, kind="ExternalInput").ap()
    s1_d = nc.dram_tensor("s1", [1, 1], F32, kind="ExternalInput").ap()
    s2_d = nc.dram_tensor("s2", [1, 1], F32, kind="ExternalInput").ap()
    wr_d = nc.dram_tensor("wr", [D, D], F32, kind="ExternalInput").ap()
    wk_d = nc.dram_tensor("wk", [D, D], F32, kind="ExternalInput").ap()
    wv_d = nc.dram_tensor("wv", [D, D], F32, kind="ExternalInput").ap()
    wo_d = nc.dram_tensor("wo", [D, D], F32, kind="ExternalInput").ap()
    decay_d = nc.dram_tensor("decay8", [8, 128], F32, kind="ExternalInput").ap()
    lnw_d = nc.dram_tensor("lnw8", [8, 128], F32, kind="ExternalInput").ap()
    wg_d = nc.dram_tensor("wg", [FI, D], F32, kind="ExternalInput").ap()
    wu_d = nc.dram_tensor("wu", [FI, D], F32, kind="ExternalInput").ap()
    wd_d = nc.dram_tensor("wd", [D, FI], F32, kind="ExternalInput").ap()
    identf_d = nc.dram_tensor("identf", [128, 128], F32, kind="ExternalInput").ap()
    identb_d = nc.dram_tensor("identb", [128, 128], BF16, kind="ExternalInput").ap()
    iota_d = nc.dram_tensor("iota", [128, T], F32, kind="ExternalInput").ap()
    masku_d = nc.dram_tensor("masku", [128, 128], F32, kind="ExternalInput").ap()
    blk2_d = nc.dram_tensor("blk2", [128, 2], BF16, kind="ExternalInput").ap()
    sel2_d = nc.dram_tensor("sel2", [2, 128], F32, kind="ExternalInput").ap()

    dbg_out = {}

    def dbg_reg(name, shape, dt):
        if name in dbg:
            dbg_out[name] = nc.dram_tensor("dbg_" + name, shape, dt,
                                           kind="ExternalOutput").ap()
        return dbg_out.get(name)

    dbg_reg("xn_hi", [128, NCK * T], BF16)
    dbg_reg("xn_lo", [128, NCK * T], BF16)

    def scratch(name, shape, dt):
        if ("dbg_" + name) if False else (name in dbg):
            return nc.dram_tensor("dbg_" + name, shape, dt, kind="ExternalOutput").ap()
        return nc.dram_tensor(name + "_scr", shape, dt, kind="Internal").ap()

    o_scr = scratch("o_l0", [NCK, 128, T], F32)
    s_scr = scratch("s_l0", [NCK, 128, T], F32)

    st = contextlib.ExitStack()
    mm = MM()
    with st:
        all_streams = []

        def mkstream(name, nbuf=2):
            s = Stream(st, nc, name, nbuf)
            all_streams.append(s)
            return s

        sc_ = Sem(st, nc, "sc")
        stx = mkstream("stx")
        so_ = Sem(st, nc, "so")
        sp = Sem(st, nc, "sp")
        sv = Sem(st, nc, "sv")
        sa = Sem(st, nc, "sa")
        sq = Sem(st, nc, "sq")
        sw = mkstream("sw")
        sgd = mkstream("sgd")
        srt = mkstream("srt")
        sso = mkstream("sso")
        srs = mkstream("srs")
        sot = mkstream("sot")
        sst = mkstream("sst", nbuf=4)
        srg = mkstream("srg")

        PS = [st.enter_context(nc.psum_tensor(f"PS{i}", [128, 1024], F32))
              for i in range(4)]
        ps_free = [None, None, None, None]

        def ps_wait(eng, i):
            if ps_free[i] is not None:
                eng.wait_ge(ps_free[i][0].h, ps_free[i][1])

        ident = st.enter_context(nc.sbuf_tensor("identf_s", [128, 128], F32))
        identb = st.enter_context(nc.sbuf_tensor("identb_s", [128, 128], BF16))
        masku = st.enter_context(nc.sbuf_tensor("masku_s", [128, 128], F32))
        blk2 = st.enter_context(nc.sbuf_tensor("blk2_s", [128, 2], BF16))
        sel2 = st.enter_context(nc.sbuf_tensor("sel2_s", [128, 128], F32))
        iota = st.enter_context(nc.sbuf_tensor("iota_s", [128, T], F32))
        onesf = st.enter_context(nc.sbuf_tensor("onesf", [128, 1], F32))
        onesb = st.enter_context(nc.sbuf_tensor("onesb", [128, 1], BF16))
        ones1r = st.enter_context(nc.sbuf_tensor("ones1r", [128, 128], F32))
        scal = st.enter_context(nc.sbuf_tensor("scal", [128, 8], F32))
        decay_c = st.enter_context(nc.sbuf_tensor("decay_c", [128, 8], F32))
        lnw_c = st.enter_context(nc.sbuf_tensor("lnw_c", [128, 8], F32))
        xn_hi = st.enter_context(nc.sbuf_tensor("xn_hi", [128, NCK * T], BF16))
        xn_lo = st.enter_context(nc.sbuf_tensor("xn_lo", [128, NCK * T], BF16))
        mstats = st.enter_context(nc.sbuf_tensor("mstats", [128, 2 * NT], F32))

        block = st.enter_context(nc.Block())

        @block.sync
        def _(sync):
            for dst, src in [(ident[:], identf_d), (identb[:], identb_d),
                             (masku[:], masku_d), (blk2[:], blk2_d),
                             (sel2[0:2, :], sel2_d), (iota[:], iota_d)]:
                sc_.inc(sync.dma_start(out=dst, in_=src), 16)
            sc_.inc(sync.dma_start(
                out=ap2(decay_c, 0, [[8, 128], [1, 8]]),
                in_=ap2(decay_d.tensor, 0, [[1, 128], [128, 8]]),
                allow_slow_non_contiguous=True), 16)
            sc_.inc(sync.dma_start(
                out=ap2(lnw_c, 0, [[8, 128], [1, 8]]),
                in_=ap2(lnw_d.tensor, 0, [[1, 128], [128, 8]]),
                allow_slow_non_contiguous=True), 16)
            sc_.inc(sync.dma_start(out=scal[0:1, 0:1], in_=s0_d), 16)
            sc_.inc(sync.dma_start(out=scal[0:1, 1:2], in_=s1_d), 16)
            sc_.inc(sync.dma_start(out=scal[0:1, 2:3], in_=s2_d), 16)
        p0_ld = sc_.n

        @block.vector
        def _(vector):
            vector.memset(onesf[:], 1.0)
            vector.memset(onesb[:], 1.0)
            sv.inc(vector.memset(ones1r[0:1, :], 1.0))
        p0_ones = sv.n

        @block.tensor
        def _(tensor):
            tensor.wait_ge(sc_.h, p0_ld)
            tensor.wait_ge(sv.h, p0_ones)
            sp.inc(tensor.matmul(PS[0][:, 0:3], ones1r[0:1, :], scal[0:1, 0:3],
                                 start=True, stop=True))
        p0_mm = sp.n

        @block.vector
        def _(vector):
            vector.wait_ge(sp.h, p0_mm)
            vector.tensor_scalar_mul(scal[:, 0:2], PS[0][:, 0:2], 4.0)
            sv.inc(vector.tensor_scalar_mul(scal[:, 2:3], PS[0][:, 2:3], 2.0))
        ps_free[0] = (sv, sv.n)

        sck0 = mkstream("sck0")
        sck1 = mkstream("sck1")
        sck2 = mkstream("sck2")

        @block.gpsimd
        def _(gpsimd):
            for fc in range(32):
                off = fc * 16 * 128 * 128
                sck0.issue(gpsimd.dma_start(
                    out=ap2(k0bf_d.tensor, off, [[16384, 16], [1, 16384]]),
                    in_=ap2(k0_d.tensor, off, [[16384, 16], [1, 16384]])), fc)
            for fc2 in range(32):
                off = fc2 * 32 * 128 * 128
                sck1.issue(gpsimd.dma_start(
                    out=ap2(k1bf_d.tensor, off, [[16384, 32], [1, 16384]]),
                    in_=ap2(k1_d.tensor, off, [[16384, 32], [1, 16384]])), fc2)
            for fc3 in range(8):
                off = fc3 * 32 * 128 * 128
                sck2.issue(gpsimd.dma_start(
                    out=ap2(k2bf_d.tensor, off, [[16384, 32], [1, 16384]]),
                    in_=ap2(k2_d.tensor, off, [[16384, 32], [1, 16384]])), fc3)

        def wait_stream_all(eng, strm):
            for s in strm.sems:
                if s.n:
                    eng.wait_ge(s.h, s.n)

        def emit_ln_phase(src_dram, src_is_y, stats_store, tt_base=0, entry=None):
            with contextlib.ExitStack() as lh:
                xtile = lh.enter_context(nc.sbuf_tensor(f"xtile{tt_base}", [128, 2 * D], F32))
                lnst = lh.enter_context(nc.sbuf_tensor(f"lnst{tt_base}", [128, 12], F32))
                lnmv = lh.enter_context(nc.sbuf_tensor(f"lnmv{tt_base}", [128, 2], F32))
                xntmp = lh.enter_context(nc.sbuf_tensor(f"xntmp{tt_base}", [128, 2 * D], F32))
                tile_done = {}
                tr_done = {}
                for tt in range(NT):
                    buf = tt % 2
                    xv = xtile[:, buf * D:(buf + 1) * D]
                    xnv = xntmp[:, buf * D:(buf + 1) * D]

                    @block.sync
                    def _(sync):
                        if tt == 0 and entry is not None:
                            entry(sync)
                        if tt >= 2:
                            sync.wait_ge(sv.h, tile_done[tt - 2])
                        stx.issue(sync.dma_start(
                            out=xv, in_=src_dram[tt * 128:(tt + 1) * 128, :]),
                            tt_base + tt)

                    @block.vector
                    def _(vector):
                        stx.wait(vector, tt_base + tt)
                        if tt == 0 and entry is not None:
                            entry(vector)
                        if tt >= 1:
                            vector.wait_ge(sv.h, tile_done[tt - 1])
                        vector.bn_stats(lnst[:, 0:6], xv[:, 0:512])
                        sv.inc(vector.bn_stats(lnst[:, 6:12], xv[:, 512:1024]))
                    bn_m = sv.n

                    @block.vector
                    def _(vector):
                        vector.wait_ge(sv.h, bn_m)
                        sv.inc(vector.bn_aggr(lnmv[:], lnst[:]))
                    ag_m = sv.n

                    @block.vector
                    def _(vector):
                        vector.wait_ge(sv.h, ag_m)
                        sv.inc(vector.tensor_scalar_add(lnmv[:, 1:2], lnmv[:, 1:2], 1e-5))
                    var_m = sv.n

                    @block.scalar
                    def _(scalar):
                        scalar.wait_ge(sv.h, var_m)
                        sa.inc(scalar.activation(lnmv[:, 1:2], lnmv[:, 1:2], ACTF.Sqrt,
                                                 bias=0.0, scale=1.0))
                    sq_m = sa.n

                    @block.vector
                    def _(vector):
                        vector.wait_ge(sa.h, sq_m)
                        sv.inc(vector.reciprocal(lnmv[:, 1:2], lnmv[:, 1:2]))
                    rs_m = sv.n

                    @block.vector
                    def _(vector):
                        vector.wait_ge(sv.h, rs_m)
                        if tt >= 2:
                            vector.wait_ge(sp.h, tr_done[tt - 2])
                        if stats_store:
                            vector.tensor_copy(mstats[:, 2 * tt:2 * tt + 2], lnmv[:])
                        sv.inc(vector.tensor_scalar(xnv, xv, lnmv[:, 0:1], lnmv[:, 1:2],
                                                    ALU.subtract, ALU.mult))
                    xn_m = sv.n

                    a0, a1 = (0, 1) if tt % 2 == 0 else (2, 3)

                    @block.tensor
                    def _(tensor):
                        tensor.wait_ge(sv.h, xn_m)
                        if tt == 0 and entry is not None:
                            entry(tensor)
                        ps_wait(tensor, a0)
                        ps_wait(tensor, a1)
                        for c in range(NCK):
                            pa = PS[a0 if c < 4 else a1]
                            sp.inc(tensor.transpose(
                                pa[:, (c % 4) * 128:(c % 4) * 128 + 128],
                                xnv[:, c * 128:(c + 1) * 128], ident[:]))
                    tr_done[tt] = sp.n
                    tr_m = sp.n

                    @block.vector
                    def _(vector):
                        vector.wait_ge(sp.h, tr_m)
                        for c in range(NCK):
                            pa = PS[a0 if c < 4 else a1]
                            pav = pa[:, (c % 4) * 128:(c % 4) * 128 + 128]
                            dst = slice(c * T + tt * 128, c * T + (tt + 1) * 128)
                            inst = vector.tensor_copy(xn_hi[:, dst], pav)
                        sv.inc(inst)
                    hi_m = sv.n

                    @block.vector
                    def _(vector):
                        vector.wait_ge(sv.h, hi_m)
                        for c in range(NCK):
                            pa = PS[a0 if c < 4 else a1]
                            pav = pa[:, (c % 4) * 128:(c % 4) * 128 + 128]
                            dst = slice(c * T + tt * 128, c * T + (tt + 1) * 128)
                            inst = vector.scalar_tensor_tensor(
                                xn_lo[:, dst], pav, 1.0, xn_hi[:, dst],
                                ALU.mult, ALU.subtract)
                        sv.inc(inst)
                    tile_done[tt] = sv.n
                    ps_free[a0] = (sv, sv.n)
                    ps_free[a1] = (sv, sv.n)
            return sv.n

        ln1_done = emit_ln_phase(x_d, src_is_y=False, stats_store=False)

        if "xn_hi" in dbg_out or "xn_lo" in dbg_out:
            @block.sync
            def _(sync):
                sync.wait_ge(sv.h, ln1_done)
                if "xn_hi" in dbg_out:
                    so_.inc(sync.dma_start(out=dbg_out["xn_hi"], in_=xn_hi[:]), 16)
                if "xn_lo" in dbg_out:
                    so_.inc(sync.dma_start(out=dbg_out["xn_lo"], in_=xn_lo[:]), 16)

        def emit_l0_attn(rsum):
            ph = contextlib.ExitStack()
            with ph:
                qbuf = ph.enter_context(nc.sbuf_tensor("qbuf", [128, 2 * T], F32))
                kbuf = ph.enter_context(nc.sbuf_tensor("kbuf", [128, 2 * T], F32))
                vaug = ph.enter_context(nc.sbuf_tensor("vaug", [128, NT * 260], F32))
                wstg = ph.enter_context(nc.sbuf_tensor("wstg", [128, 2048], F32))
                whi = ph.enter_context(nc.sbuf_tensor("whi", [128, 2048], BF16))
                wlo = ph.enter_context(nc.sbuf_tensor("wlo", [128, 2048], BF16))
                ptb = ph.enter_context(nc.sbuf_tensor("ptb", [128, 2 * QB], F32))
                q2b = ph.enter_context(nc.sbuf_tensor("q2b", [128, 2 * T], BF16))
                nrm = ph.enter_context(nc.sbuf_tensor("nrm", [128, 32], F32))
                nrow = ph.enter_context(nc.sbuf_tensor("nrow", [128, 8], F32))
                mh_neg = ph.enter_context(nc.sbuf_tensor("mh_neg", [128, NH], F32))
                ostg = ph.enter_context(nc.sbuf_tensor("ostg", [128, 2 * QB], F32))

                def wtile_in(dram, fc, kstride=128):
                    base = fc * (8 * 128 * 128)
                    return ap2(dram.tensor, base, [[128, 128], [16384, 8], [1, 128]])

                wstg_free = [None, None]
                whl_free = [None, None]
                swi = [0]
                flash_done = [None]

                for g in range(4):
                    for qi, fc in enumerate([2 * g, 2 * g + 1, 8 + 2 * g, 9 + 2 * g]):
                        cidx = qi % 2
                        dstbuf = qbuf if qi < 2 else kbuf
                        slot = qi % 2
                        idx = swi[0]
                        swi[0] += 1

                        @block.sync
                        def _(sync):
                            if idx == 0:
                                sync.wait_ge(sv.h, ln1_done)
                            if wstg_free[slot] is not None:
                                sync.wait_ge(wstg_free[slot][0].h, wstg_free[slot][1])
                            dst = ap2(wstg, slot * 1024, [[2048, 128], [128, 8], [1, 128]])
                            sw.issue(sync.dma_start(out=dst, in_=wtile_in(wqkv_d, fc)), idx)

                        @block.vector
                        def _(vector):
                            sw.wait(vector, idx)
                            if whl_free[slot] is not None:
                                vector.wait_ge(whl_free[slot][0].h, whl_free[slot][1])
                            sv.inc(vector.tensor_copy(
                                whi[:, slot * 1024:(slot + 1) * 1024],
                                wstg[:, slot * 1024:(slot + 1) * 1024]))
                        hi_m = sv.n

                        @block.vector
                        def _(vector):
                            vector.wait_ge(sv.h, hi_m)
                            sv.inc(vector.scalar_tensor_tensor(
                                wlo[:, slot * 1024:(slot + 1) * 1024],
                                wstg[:, slot * 1024:(slot + 1) * 1024], 1.0,
                                whi[:, slot * 1024:(slot + 1) * 1024],
                                ALU.mult, ALU.subtract))
                        w_m = sv.n
                        wstg_free[slot] = (sv, w_m)

                        aren = (0, 1) if qi % 2 == 0 else (2, 3)

                        @block.tensor
                        def _(tensor):
                            tensor.wait_ge(sv.h, max(w_m, ln1_done))
                            if flash_done[0] is not None:
                                tensor.wait_ge(flash_done[0][0].h, flash_done[0][1])
                            ps_wait(tensor, aren[0])
                            ps_wait(tensor, aren[1])
                            for kc in range(NCK):
                                wh = whi[:, slot * 1024 + kc * 128: slot * 1024 + kc * 128 + 128]
                                wl = wlo[:, slot * 1024 + kc * 128: slot * 1024 + kc * 128 + 128]
                                for pi, (wt, xt) in enumerate(
                                        ((wh, xn_hi), (wh, xn_lo), (wl, xn_hi))):
                                    ky = ("qkv", idx, kc, pi == 2)
                                    for j in range((T + 511) // 512):
                                        w = min(512, T - j * 512)
                                        pa = PS[aren[j // 2]] if T > 1024 else PS[aren[0]]
                                        jj = (j % 2) if T > 1024 else j
                                        sp.inc(mm(tensor,
                                            pa[:, jj * 512:jj * 512 + w],
                                            wt, xt[:, kc * T + j * 512: kc * T + j * 512 + w],
                                            start=(kc == 0 and pi == 0),
                                            stop=(kc == NCK - 1 and pi == 2), key=ky))
                        mm_m = sp.n
                        whl_free[slot] = (sp, mm_m)

                        @block.vector
                        def _(vector):
                            vector.wait_ge(sp.h, mm_m)
                            if flash_done[0] is not None:
                                vector.wait_ge(flash_done[0][0].h, flash_done[0][1])
                            inst = vector.tensor_copy(
                                dstbuf[:, cidx * T: cidx * T + min(1024, T)],
                                PS[aren[0]][:, 0:min(1024, T)])
                            if T > 1024:
                                inst = vector.tensor_copy(
                                    dstbuf[:, cidx * T + 1024: cidx * T + T],
                                    PS[aren[1]][:, 0:T - 1024])
                            sv.inc(inst)
                        ps_free[aren[0]] = (sv, sv.n)
                        ps_free[aren[1]] = (sv, sv.n)
                    qk_done = sv.n

                    vidx = swi[0]
                    swi[0] += 2

                    @block.sync
                    def _(sync):
                        for sl in range(2):
                            if wstg_free[sl] is not None:
                                sync.wait_ge(wstg_free[sl][0].h, wstg_free[sl][1])
                        for fci in range(2):
                            fc = 16 + 2 * g + fci
                            base = fc * (8 * 128 * 128)
                            dst = ap2(wstg, fci * 128, [[2048, 128], [256, 8], [1, 128]])
                            sw.issue(sync.dma_start(
                                out=dst,
                                in_=ap2(wqkv_d.tensor, base,
                                        [[128, 128], [16384, 8], [1, 128]])),
                                vidx + fci)

                    @block.vector
                    def _(vector):
                        sw.wait(vector, vidx)
                        sw.wait(vector, vidx + 1)
                        for sl in range(2):
                            if whl_free[sl] is not None:
                                vector.wait_ge(whl_free[sl][0].h, whl_free[sl][1])
                        sv.inc(vector.tensor_copy(whi[:], wstg[:]))
                    vh_m = sv.n

                    @block.vector
                    def _(vector):
                        vector.wait_ge(sv.h, vh_m)
                        sv.inc(vector.scalar_tensor_tensor(
                            wlo[:], wstg[:], 1.0, whi[:], ALU.mult, ALU.subtract))
                    vw_m = sv.n
                    wstg_free[0] = wstg_free[1] = (sv, vw_m)

                    @block.vector
                    def _(vector):
                        if flash_done[0] is not None:
                            vector.wait_ge(flash_done[0][0].h, flash_done[0][1])
                        sv.inc(vector.memset(
                            ap2(vaug, 64, [[NT * 260, 128], [65, NT * 4], [1, 1]]), 1.0))
                    ones_m = sv.n

                    for tt in range(NT):
                        ai = 0 if tt % 2 == 0 else 2

                        @block.tensor
                        def _(tensor):
                            tensor.wait_ge(sv.h, max(vw_m, ln1_done))
                            ps_wait(tensor, ai)
                            for kc in range(NCK):
                                xh = xn_hi[:, kc * T + tt * 128: kc * T + (tt + 1) * 128]
                                xl = xn_lo[:, kc * T + tt * 128: kc * T + (tt + 1) * 128]
                                sp.inc(mm(tensor, PS[ai][:, 0:256], xh,
                                          whi[:, kc * 256:(kc + 1) * 256],
                                          start=(kc == 0), stop=False,
                                          key=("v", g, tt, kc, 0)))
                                sp.inc(mm(tensor, PS[ai][:, 0:256], xh,
                                          wlo[:, kc * 256:(kc + 1) * 256],
                                          start=False, stop=False,
                                          key=("v", g, tt, kc, 0)))
                                sp.inc(mm(tensor, PS[ai][:, 0:256], xl,
                                          whi[:, kc * 256:(kc + 1) * 256],
                                          start=False, stop=(kc == NCK - 1),
                                          key=("v", g, tt, kc, 1)))
                        vmm_m = sp.n

                        @block.scalar
                        def _(scalar):
                            scalar.wait_ge(sp.h, vmm_m)
                            scalar.wait_ge(sv.h, ones_m)
                            for hl in range(4):
                                inst = scalar.copy(
                                    vaug[:, tt * 260 + hl * 65: tt * 260 + hl * 65 + 64],
                                    PS[ai][:, hl * 64:(hl + 1) * 64])
                            sa.inc(inst)
                        ps_free[ai] = (sa, sa.n)
                    v_done = sa.n
                    whl_free[0] = whl_free[1] = (sp, sp.n)

                    def norm_pass(srcbuf, colbase, prev_m):
                        @block.scalar
                        def _(scalar):
                            scalar.wait_ge(sv.h, qk_done)
                            if prev_m is not None:
                                scalar.wait_ge(sv.h, prev_m)
                            scalar.activation(q2b[:, 0:T], srcbuf[:, 0:T], ACTF.Square,
                                              bias=0.0, scale=1.0)
                            sa.inc(scalar.activation(q2b[:, T:2 * T], srcbuf[:, T:2 * T],
                                                     ACTF.Square, bias=0.0, scale=1.0))
                        sq_m = sa.n

                        @block.tensor
                        def _(tensor):
                            tensor.wait_ge(sa.h, sq_m)
                            for i in range(4):
                                ps_wait(tensor, i)
                            for c in range(2):
                                for j in range(NJH):
                                    w = min(1024, T)
                                    for jj in range((w + 511) // 512):
                                        ww = min(512, w - jj * 512)
                                        sp.inc(mm(tensor,
                                            PS[2 * c + j][0:2, jj * 512: jj * 512 + ww],
                                            blk2[:],
                                            q2b[:, c * T + j * 1024 + jj * 512:
                                                 c * T + j * 1024 + jj * 512 + ww],
                                            start=True, stop=True, key=("blk2",)))
                        n_mm = sp.n

                        @block.vector
                        def _(vector):
                            vector.wait_ge(sp.h, n_mm)
                            for c in range(2):
                                for j in range(NJH):
                                    inst = vector.reduce_max(
                                        nrm[0:2, colbase + 2 * c + j: colbase + 2 * c + j + 1],
                                        PS[2 * c + j][0:2, 0:min(1024, T)], AX.X)
                            sv.inc(inst)
                        for i in range(4):
                            ps_free[i] = (sv, sv.n)
                        return sv.n

                    qn_m = norm_pass(qbuf, 0, None)
                    kn_m = norm_pass(kbuf, 4, qn_m)

                    if NJH > 1:
                        @block.vector
                        def _(vector):
                            vector.wait_ge(sv.h, kn_m)
                            for b in (0, 2, 4, 6):
                                inst = vector.tensor_tensor(
                                    nrm[0:2, b:b + 1], nrm[0:2, b:b + 1],
                                    nrm[0:2, b + 1:b + 2], ALU.max)
                            sv.inc(inst)
                        kn_m = sv.n

                    @block.vector
                    def _(vector):
                        vector.wait_ge(sv.h, kn_m)
                        vector.tensor_tensor(nrm[0:2, 8:9], nrm[0:2, 0:1],
                                             nrm[0:2, 4:5], ALU.mult)
                        sv.inc(vector.tensor_tensor(nrm[0:2, 9:10], nrm[0:2, 2:3],
                                                    nrm[0:2, 6:7], ALU.mult))
                    m2_m = sv.n

                    @block.scalar
                    def _(scalar):
                        scalar.wait_ge(sv.h, m2_m)
                        sa.inc(scalar.activation(nrm[0:2, 10:12], nrm[0:2, 8:10],
                                                 ACTF.Sqrt, bias=0.0, scale=1.0))
                    sq2_m = sa.n

                    @block.vector
                    def _(vector):
                        vector.wait_ge(sa.h, sq2_m)
                        sv.inc(vector.tensor_scalar_mul(nrm[0:2, 10:12],
                                                        nrm[0:2, 10:12], 0.126262626))
                    msc_m = sv.n

                    @block.sync
                    def _(sync):
                        sync.wait_ge(sv.h, msc_m)
                        srt.issue(sync.dma_start(
                            out=ap2(nrow, 0, [[8, 1], [1, 4]]),
                            in_=ap2(nrm, 10, [[32, 2], [1, 2]]),
                            allow_slow_non_contiguous=True), g)

                    @block.tensor
                    def _(tensor):
                        srt.wait(tensor, g)
                        ps_wait(tensor, 1)
                        for coli in range(4):
                            sp.inc(tensor.matmul(PS[1][:, coli:coli + 1], ones1r[0:1, :],
                                                 nrow[0:1, coli:coli + 1],
                                                 start=True, stop=True))
                    mb_mm = sp.n

                    @block.vector
                    def _(vector):
                        vector.wait_ge(sp.h, mb_mm)
                        for coli, hl in enumerate([0, 2, 1, 3]):
                            inst = vector.tensor_scalar_mul(
                                mh_neg[:, 4 * g + hl: 4 * g + hl + 1],
                                PS[1][:, coli:coli + 1], -1.0)
                        sv.inc(inst)
                    ps_free[1] = (sv, sv.n)
                    mh_m = sv.n

                    for hl in range(4):
                        h = 4 * g + hl
                        cidx, rh = hl // 2, hl % 2

                        def qv(a, b):
                            return qbuf[rh * 64:(rh + 1) * 64, cidx * T + a: cidx * T + b]

                        def kvv(a, b):
                            return kbuf[rh * 64:(rh + 1) * 64, cidx * T + a: cidx * T + b]

                        for qb in range(NQB):
                            q0 = qb * QB
                            nkt = (qb + 1) * (QB // 128)
                            sc_marks, ex_marks, av_marks, dm_marks = {}, {}, {}, {}

                            def emit_scores(kt):
                                qstart = max(q0, kt * 128)
                                qspan = q0 + QB - qstart
                                pp = 2 + (kt % 2)

                                @block.tensor
                                def _(tensor):
                                    if kt == 0:
                                        tensor.wait_ge(sv.h, mh_m)
                                        tensor.wait_ge(sa.h, v_done)
                                        ps_wait(tensor, 0)
                                    if kt >= 2:
                                        tensor.wait_ge(sa.h, ex_marks[kt - 2])
                                    else:
                                        ps_wait(tensor, pp)
                                    for j in range((qspan + 511) // 512):
                                        w = min(512, qspan - j * 512)
                                        sp.inc(tensor.matmul(
                                            PS[pp][:, j * 512: j * 512 + w],
                                            kvv(kt * 128, (kt + 1) * 128),
                                            qv(qstart + j * 512, qstart + j * 512 + w),
                                            start=True, stop=True))
                                sc_marks[kt] = sp.n

                            def emit_exp(kt):
                                qstart = max(q0, kt * 128)
                                qspan = q0 + QB - qstart
                                pp = 2 + (kt % 2)
                                ptv = ptb[:, (kt % 2) * QB: (kt % 2) * QB + qspan]

                                @block.scalar
                                def _(scalar):
                                    scalar.wait_ge(sp.h, sc_marks[kt])
                                    if kt >= 2:
                                        scalar.wait_ge(sp.h, av_marks[kt - 2])
                                    sa.inc(scalar.activation(
                                        ptv, PS[pp][:, 0:qspan], ACTF.Exp,
                                        bias=mh_neg[:, h:h + 1], scale=0.125))
                                ex_marks[kt] = sa.n

                                if qstart == kt * 128:
                                    @block.vector
                                    def _(vector):
                                        vector.wait_ge(sa.h, ex_marks[kt])
                                        mw = min(128, qspan)
                                        sv.inc(vector.tensor_tensor(
                                            ptv[:, 0:mw], ptv[:, 0:mw],
                                            masku[:, 0:mw], ALU.mult))
                                    dm_marks[kt] = sv.n

                            def emit_av(kt):
                                qstart = max(q0, kt * 128)
                                qspan = q0 + QB - qstart
                                ptv = ptb[:, (kt % 2) * QB: (kt % 2) * QB + qspan]
                                dm = dm_marks.get(kt)

                                @block.tensor
                                def _(tensor):
                                    tensor.wait_ge(sa.h, ex_marks[kt])
                                    if dm is not None:
                                        tensor.wait_ge(sv.h, dm)
                                    coff = qstart - q0
                                    a = coff
                                    while a < QB:
                                        e = min(QB, (a // 512 + 1) * 512)
                                        last_kt = min(nkt - 1,
                                                      (q0 + (a // 512 + 1) * 512) // 128 - 1)
                                        sp.inc(tensor.matmul(
                                            PS[0][0:65, a:e],
                                            vaug[:, kt * 260 + hl * 65: kt * 260 + (hl + 1) * 65],
                                            ptv[:, a - coff: e - coff],
                                            start=(kt == 0), stop=(kt == last_kt)))
                                        a = e
                                av_marks[kt] = sp.n

                            emit_scores(0)
                            emit_exp(0)
                            for kt in range(nkt):
                                if kt + 1 < nkt:
                                    emit_scores(kt + 1)
                                    emit_exp(kt + 1)
                                emit_av(kt)
                            o_mm = sp.n

                            ob = (h * NQB + qb) % 2

                            @block.vector
                            def _(vector):
                                vector.wait_ge(sp.h, o_mm)
                                if h * NQB + qb >= 2:
                                    srs.wait(vector, h * NQB + qb - 2)
                                    sso.wait(vector, h * NQB + qb - 2)
                                sv.inc(vector.tensor_copy(
                                    ostg[0:65, ob * QB: ob * QB + QB], PS[0][0:65, 0:QB]))
                            oe_m = sv.n
                            ps_free[0] = (sv, oe_m)

                            @block.sync
                            def _(sync):
                                sync.wait_ge(sv.h, oe_m)
                                c, r2 = h // 2, h % 2
                                sso.issue(sync.dma_start(
                                    out=ap2(o_scr.tensor, c * 128 * T + r2 * 64 * T + q0,
                                            [[T, 64], [1, QB]]),
                                    in_=ostg[0:64, ob * QB: ob * QB + QB]), h * NQB + qb)
                                srs.issue(sync.dma_start(
                                    out=rsum[h:h + 1, q0: q0 + QB],
                                    in_=ostg[64:65, ob * QB: ob * QB + QB]),
                                    h * NQB + qb)
                    flash_done[0] = (sp, sp.n)

                @block.vector
                def _(vector):
                    for i in range(NH * NQB):
                        srs.wait(vector, i)
                    sv.inc(vector.reciprocal(rsum[0:16, :], rsum[0:16, :]))
                rc_m = sv.n
            return rc_m

        def emit_l0_proj(rsum_handle, rc_m):
            ph = contextlib.ExitStack()
            with ph:
                TJ = 256
                NTJ = T // TJ
                otile = ph.enter_context(nc.sbuf_tensor("otile", [128, 2 * NCK * TJ], F32))
                rstage = ph.enter_context(nc.sbuf_tensor("rstage", [128, 2 * NCK * TJ], F32))
                sstg = ph.enter_context(nc.sbuf_tensor("sstg", [128, 2 * NCK * TJ], F32))
                pstg = ph.enter_context(nc.sbuf_tensor("pstg", [128, 2048], F32))
                pw_free = [None, None]
                ot_free = {}
                ss_free = {}

                wpbuf = ph.enter_context(nc.sbuf_tensor("wpbuf", [128, NCK * 1024], F32))

                def wtile_proj(fc):
                    base = fc * (8 * 128 * 128)
                    return ap2(wproj_d.tensor, base, [[128, 128], [16384, 8], [1, 128]])

                @block.sync
                def _(sync):
                    sync.wait_ge(sv.h, rc_m)
                    for i in range(NH * NQB):
                        sso.wait(sync, i)
                    for fc in range(NCK):
                        dst = ap2(wpbuf, fc * 1024, [[NCK * 1024, 128], [128, 8], [1, 128]])
                        sw.issue(sync.dma_start(out=dst, in_=wtile_proj(fc)), 100 + fc)

                for ti in range(NTJ):
                    tb = ti % 2
                    @block.sync
                    def _(sync):
                        if ti == 0:
                            for i in range(NH * NQB):
                                sso.wait(sync, i)
                        if ti >= 2:
                            sync.wait_ge(sp.h, ot_free[ti - 2])
                        sot.issue(sync.dma_start(
                            out=ap2(otile, tb * NCK * TJ,
                                    [[2 * NCK * TJ, 128], [TJ, 8], [1, TJ]]),
                            in_=ap2(o_scr.tensor, ti * TJ,
                                    [[T, 128], [128 * T, 8], [1, TJ]])), ti)
                        if ti == 0:
                            sync.wait_ge(sv.h, rc_m)
                        for hh in range(16):
                            c2, r = hh // 2, hh % 2
                            srg.issue(sync.dma_start(
                                out=rstage[r:r + 1,
                                           tb * NCK * TJ + c2 * TJ:
                                           tb * NCK * TJ + (c2 + 1) * TJ],
                                in_=rsum_handle[hh:hh + 1, ti * TJ:(ti + 1) * TJ]), ti)

                    nm_marks = []
                    for c in range(NCK):
                        ai = c % 2

                        @block.tensor
                        def _(tensor):
                            sot.wait(tensor, ti)
                            if c == 0:
                                srg.wait(tensor, ti)
                            ps_wait(tensor, ai)
                            sp.inc(tensor.matmul(
                                PS[ai][:, 0:TJ], sel2[0:2, :],
                                rstage[0:2, tb * NCK * TJ + c * TJ:
                                       tb * NCK * TJ + (c + 1) * TJ],
                                start=True, stop=True))
                        bc_mm = sp.n

                        @block.vector
                        def _(vector):
                            vector.wait_ge(sp.h, bc_mm)
                            ov = otile[:, tb * NCK * TJ + c * TJ: tb * NCK * TJ + (c + 1) * TJ]
                            sv.inc(vector.tensor_tensor(ov, ov, PS[ai][:, 0:TJ], ALU.mult))
                        ps_free[ai] = (sv, sv.n)
                        nm_marks.append(sv.n)
                    onrm_m = sv.n

                    for half in range(2):
                        aren = (0, 1) if half == 0 else (2, 3)

                        @block.tensor
                        def _(tensor):
                            tensor.wait_ge(sv.h, onrm_m)
                            sw.wait(tensor, 100 + NCK - 1)
                            sw.wait(tensor, 100 + NCK - 2)
                            ps_wait(tensor, aren[0])
                            ps_wait(tensor, aren[1])
                            for fc2 in range(4):
                                fc = half * 4 + fc2
                                pa = PS[aren[fc2 // 2]]
                                po = (fc2 % 2) * TJ
                                for dc in range(NCK):
                                    sp.inc(tensor.matmul(
                                        pa[:, po: po + TJ],
                                        wpbuf[:, fc * 1024 + dc * 128: fc * 1024 + dc * 128 + 128],
                                        otile[:, tb * NCK * TJ + dc * TJ:
                                              tb * NCK * TJ + (dc + 1) * TJ],
                                        start=(dc == 0), stop=(dc == NCK - 1)))
                        pj_mm = sp.n
                        if half == 1:
                            ot_free[ti] = pj_mm

                        @block.vector
                        def _(vector):
                            vector.wait_ge(sp.h, pj_mm)
                            if ti >= 2:
                                sst.wait(vector, (ti - 2) * 2 + half)
                            for fc2 in range(4):
                                fc = half * 4 + fc2
                                pa = PS[aren[fc2 // 2]]
                                po = (fc2 % 2) * TJ
                                inst = vector.tensor_copy(
                                    sstg[:, tb * NCK * TJ + fc * TJ:
                                         tb * NCK * TJ + (fc + 1) * TJ],
                                    pa[:, po: po + TJ])
                            sv.inc(inst)
                        se_m = sv.n
                        ps_free[aren[0]] = (sv, se_m)
                        ps_free[aren[1]] = (sv, se_m)

                        @block.sync
                        def _(sync):
                            sync.wait_ge(sv.h, se_m)
                            sst.issue(sync.dma_start(
                                out=ap2(s_scr.tensor, half * 4 * 128 * T + ti * TJ,
                                        [[T, 128], [128 * T, 4], [1, TJ]]),
                                in_=ap2(sstg, tb * NCK * TJ + half * 4 * TJ,
                                        [[2 * NCK * TJ, 128], [TJ, 4], [1, TJ]])),
                                ti * 2 + half)
            return sv.n

        if upto in ("attn", "l0", "ln2", "l1attn", "full"):
            with contextlib.ExitStack() as rs_scope:
                rsum_keep = rs_scope.enter_context(
                    nc.sbuf_tensor("rsum_keep", [128, T], F32))
                rc_m = emit_l0_attn(rsum_keep)
                pj_done = emit_l0_proj(rsum_keep, rc_m)

        def emit_l0_ffn(pj_done):
            TJ = 256
            NTJ = T // TJ
            ph = contextlib.ExitStack()
            with ph:
                a1h = ph.enter_context(nc.sbuf_tensor("a1h", [128, 32 * TJ], BF16))
                a1l = ph.enter_context(nc.sbuf_tensor("a1l", [128, 32 * TJ], BF16))
                a2h = ph.enter_context(nc.sbuf_tensor("a2h", [128, 32 * TJ], BF16))
                a2l = ph.enter_context(nc.sbuf_tensor("a2l", [128, 32 * TJ], BF16))
                k0s = ph.enter_context(nc.sbuf_tensor("k0s", [128, 2 * 2048], BF16))
                d0s = ph.enter_context(nc.sbuf_tensor("d0s", [128, 2 * 1024], BF16))
                k1s = ph.enter_context(nc.sbuf_tensor("k1s", [128, 2 * 4096], BF16))
                k2s = ph.enter_context(nc.sbuf_tensor("k2s", [128, 4096], BF16))
                tb16 = ph.enter_context(nc.sbuf_tensor("tb16", [128, 4 * TJ], BF16))
                nhrow = ph.enter_context(nc.sbuf_tensor("nhrow", [128, 128], BF16))
                af32 = ph.enter_context(nc.sbuf_tensor("af32", [128, 2 * TJ], F32))
                sgbuf = ph.enter_context(nc.sbuf_tensor("sgbuf", [128, NCK * TJ], F32))
                stile = ph.enter_context(nc.sbuf_tensor("stile", [128, NCK * TJ], F32))
                xres = ph.enter_context(nc.sbuf_tensor("xres", [128, 2 * 1024], F32))
                sk0 = mkstream("sk0")
                sk1 = mkstream("sk1")
                sk2 = mkstream("sk2")
                ssl = mkstream("ssl")
                sxr = mkstream("sxr", nbuf=4)
                syw = mkstream("syw", nbuf=4)

                @block.vector
                def _(vector):
                    vector.wait_ge(sv.h, pj_done)
                    for k in sorted(sst.targets):
                        sst.wait(vector, k)
                    sv.inc(vector.memset(nhrow[0:1, :], -0.5))
                nh_m = sv.n

                d0_marks, mm1_marks = {}, {}
                k1_mm_marks, k2_mm_marks = {}, {}
                mm2_done, mm3_done = {}, {}
                tr_marks, st_marks = {}, {}
                t2_prev = [None]
                for ti in range(NTJ):
                    ev1 = {}
                    t1_mm = 0
                    t1_marks = {}
                    rl_marks = {}

                    def emit_t1(fcp):
                        @block.tensor
                        def _(tensor):
                            tensor.wait_ge(sa.h, rl_marks[fcp])
                            if fcp == 0:
                                tensor.wait_ge(sv.h, t1sp_m[0])
                            sp.inc(tensor.matmul(
                                PS[0][0:1, 512:512 + TJ], onesf[:],
                                af32[:, (fcp % 2) * TJ:((fcp % 2) + 1) * TJ],
                                start=(fcp == 0), stop=(fcp == 31)))
                        t1_marks[fcp] = sp.n

                    for fc in range(32):
                        slot = fc % 2
                        kidx = ti * 32 + fc

                        @block.gpsimd
                        def _(sync):
                            if kidx == 0:
                                sync.wait_ge(sv.h, nh_m)
                            if kidx >= 2:
                                sync.wait_ge(sv.h, d0_marks[kidx - 2])
                            if kidx == 0:
                                wait_stream_all(sync, sck0)
                            sk0.issue(sync.dma_start(
                                out=ap2(k0s, slot * 2048,
                                        [[2 * 2048, 128], [1, 2048]]),
                                in_=ap2(k0bf_d.tensor, fc * 16 * 128 * 128,
                                        [[2048, 128], [1, 2048]])), kidx)

                        @block.vector
                        def _(vector):
                            sk0.wait(vector, kidx)
                            if kidx >= 2:
                                vector.wait_ge(sp.h, mm1_marks[kidx - 2])
                            sv.inc(vector.tensor_sub(
                                d0s[:, slot * 1024:(slot + 1) * 1024],
                                k0s[:, slot * 2048: slot * 2048 + 1024],
                                k0s[:, slot * 2048 + 1024: slot * 2048 + 2048]))
                        d0_marks[kidx] = sv.n
                        d0_m = sv.n

                        ai = fc % 4

                        @block.tensor
                        def _(tensor):
                            tensor.wait_ge(sv.h, d0_m)
                            tensor.wait_ge(sv.h, ln1_done)
                            ps_wait(tensor, ai)
                            for kc in range(NCK):
                                d0t = d0s[:, slot * 1024 + kc * 128:
                                          slot * 1024 + kc * 128 + 128]
                                for pi, xt in enumerate((xn_hi, xn_lo)):
                                    sp.inc(mm(tensor,
                                        PS[ai][:, 0:TJ], d0t,
                                        xt[:, kc * T + ti * TJ: kc * T + ti * TJ + TJ],
                                        start=(kc == 0 and pi == 0),
                                        stop=(kc == NCK - 1 and pi == 1),
                                        key=("mm1", kidx, kc)))
                        mm1_marks[kidx] = sp.n
                        mm1_m = sp.n
                        if fc >= 1:
                            emit_t1(fc - 1)

                        sl2 = fc % 2

                        @block.scalar
                        def _(scalar):
                            scalar.wait_ge(sp.h, mm1_m)
                            if fc >= 2:
                                scalar.wait_ge(sv.h, ev1[fc - 2])
                                scalar.wait_ge(sp.h, t1_marks[fc - 2])
                            elif t2_prev[0] is not None:
                                scalar.wait_ge(sp.h, t2_prev[0][fc])
                            sa.inc(scalar.activation(
                                af32[:, sl2 * TJ:(sl2 + 1) * TJ], PS[ai][:, 0:TJ],
                                ACTF.Relu, bias=0.0, scale=scal[:, 0:1]))
                        rl_m = sa.n
                        rl_marks[fc] = rl_m
                        ps_free[ai] = (sa, rl_m)

                        @block.vector
                        def _(vector):
                            vector.wait_ge(sa.h, rl_m)
                            if ti >= 1 and fc == 0:
                                vector.wait_ge(sp.h, mm2_done[ti - 1])
                            sv.inc(vector.tensor_copy(
                                a1h[:, fc * TJ:(fc + 1) * TJ],
                                af32[:, sl2 * TJ:(sl2 + 1) * TJ]))
                        h_m = sv.n

                        @block.vector
                        def _(vector):
                            vector.wait_ge(sv.h, h_m)
                            sv.inc(vector.scalar_tensor_tensor(
                                a1l[:, fc * TJ:(fc + 1) * TJ],
                                af32[:, sl2 * TJ:(sl2 + 1) * TJ], 1.0,
                                a1h[:, fc * TJ:(fc + 1) * TJ], ALU.mult, ALU.subtract))
                        ev1[fc] = sv.n

                    emit_t1(31)
                    t1_mm = t1_marks[31]
                    a1_done_m = sv.n

                    @block.vector
                    def _(vector):
                        vector.wait_ge(sp.h, t1_mm)
                        sv.inc(vector.tensor_copy(tb16[0:1, 0:TJ],
                                                  PS[0][0:1, 512:512 + TJ]))
                    t1h_m = sv.n

                    @block.vector
                    def _(vector):
                        vector.wait_ge(sv.h, t1h_m)
                        sv.inc(vector.scalar_tensor_tensor(
                            tb16[0:1, TJ:2 * TJ], PS[0][0:1, 512:512 + TJ], 1.0,
                            tb16[0:1, 0:TJ], ALU.mult, ALU.subtract))
                    t1sp_m[0] = sv.n
                    t1_m = sv.n

                    ev2 = {}
                    t2_mm = 0
                    t2_marks = {}
                    rl2_marks = {}

                    def emit_t2(fcp):
                        @block.tensor
                        def _(tensor):
                            tensor.wait_ge(sa.h, rl2_marks[fcp])
                            if fcp == 0:
                                tensor.wait_ge(sv.h, t2sp_m[0])
                            sp.inc(tensor.matmul(
                                PS[1][0:1, 512:512 + TJ], onesf[:],
                                af32[:, (fcp % 2) * TJ:((fcp % 2) + 1) * TJ],
                                start=(fcp == 0), stop=(fcp == 31)))
                        t2_marks[fcp] = sp.n

                    for fc2 in range(32):
                        slot = fc2 % 2
                        kidx = ti * 32 + fc2

                        @block.gpsimd
                        def _(scalar):
                            if kidx >= 2:
                                scalar.wait_ge(sp.h, k1_mm_marks[kidx - 2])
                            if kidx == 0:
                                wait_stream_all(scalar, sck1)
                            sk1.issue(scalar.dma_start(
                                out=ap2(k1s, slot * 4096,
                                        [[2 * 4096, 128], [1, 4096]]),
                                in_=ap2(k1bf_d.tensor, fc2 * 32 * 128 * 128,
                                        [[4096, 128], [1, 4096]])), kidx)

                        ai = fc2 % 4

                        @block.tensor
                        def _(tensor):
                            sk1.wait(tensor, kidx)
                            tensor.wait_ge(sv.h, t1_m)
                            if fc2 == 0:
                                tensor.wait_ge(sv.h, a1_done_m)
                            ps_wait(tensor, ai)
                            for kc2 in range(32):
                                b1t = k1s[:, slot * 4096 + kc2 * 128:
                                          slot * 4096 + kc2 * 128 + 128]
                                for pi, at in enumerate((a1h, a1l)):
                                    sp.inc(mm(tensor,
                                        PS[ai][:, 0:TJ], b1t,
                                        at[:, kc2 * TJ:(kc2 + 1) * TJ],
                                        start=(kc2 == 0 and pi == 0), stop=False,
                                        key=("mm2", kidx, kc2)))
                            sp.inc(mm(tensor,
                                PS[ai][:, 0:TJ], nhrow[0:1, :], tb16[0:1, 0:TJ],
                                start=False, stop=False, key=("nh", ti)))
                            sp.inc(mm(tensor,
                                PS[ai][:, 0:TJ], nhrow[0:1, :], tb16[0:1, TJ:2 * TJ],
                                start=False, stop=True, key=("nh", ti)))
                        k1_mm_marks[kidx] = sp.n
                        mm2_m = sp.n
                        if fc2 >= 1:
                            emit_t2(fc2 - 1)

                        sl2 = fc2 % 2

                        @block.scalar
                        def _(scalar):
                            scalar.wait_ge(sp.h, mm2_m)
                            if fc2 >= 2:
                                scalar.wait_ge(sv.h, ev2[fc2 - 2])
                                scalar.wait_ge(sp.h, t2_marks[fc2 - 2])
                            else:
                                scalar.wait_ge(sp.h, t1_marks[30 + fc2])
                            sa.inc(scalar.activation(
                                af32[:, sl2 * TJ:(sl2 + 1) * TJ], PS[ai][:, 0:TJ],
                                ACTF.Relu, bias=0.0, scale=scal[:, 1:2]))
                        rl_m = sa.n
                        rl2_marks[fc2] = rl_m
                        ps_free[ai] = (sa, rl_m)

                        @block.vector
                        def _(vector):
                            vector.wait_ge(sa.h, rl_m)
                            if ti >= 1 and fc2 == 0:
                                vector.wait_ge(sp.h, mm3_done[ti - 1])
                            sv.inc(vector.tensor_copy(
                                a2h[:, fc2 * TJ:(fc2 + 1) * TJ],
                                af32[:, sl2 * TJ:(sl2 + 1) * TJ]))
                        h_m = sv.n

                        @block.vector
                        def _(vector):
                            vector.wait_ge(sv.h, h_m)
                            sv.inc(vector.scalar_tensor_tensor(
                                a2l[:, fc2 * TJ:(fc2 + 1) * TJ],
                                af32[:, sl2 * TJ:(sl2 + 1) * TJ], 1.0,
                                a2h[:, fc2 * TJ:(fc2 + 1) * TJ], ALU.mult, ALU.subtract))
                        ev2[fc2] = sv.n

                    emit_t2(31)
                    t2_mm = t2_marks[31]
                    a2_done_m = sv.n
                    t2_prev[0] = (t2_marks[30], t2_marks[31])
                    mm2_done[ti] = sp.n

                    @block.vector
                    def _(vector):
                        vector.wait_ge(sp.h, t2_mm)
                        sv.inc(vector.tensor_copy(tb16[0:1, 2 * TJ:3 * TJ],
                                                  PS[1][0:1, 512:512 + TJ]))
                    t2h_m = sv.n

                    @block.vector
                    def _(vector):
                        vector.wait_ge(sv.h, t2h_m)
                        sv.inc(vector.scalar_tensor_tensor(
                            tb16[0:1, 3 * TJ:4 * TJ], PS[1][0:1, 512:512 + TJ], 1.0,
                            tb16[0:1, 2 * TJ:3 * TJ], ALU.mult, ALU.subtract))
                    t2sp_m[0] = sv.n
                    t2_m = sv.n

                    for fc3 in range(NCK):
                        slot = 0
                        kidx = ti * NCK + fc3

                        @block.gpsimd
                        def _(sync):
                            if kidx >= 1:
                                sync.wait_ge(sp.h, k2_mm_marks[kidx - 1])
                            if kidx == 0:
                                wait_stream_all(sync, sck2)
                            sk2.issue(sync.dma_start(
                                out=ap2(k2s, 0,
                                        [[4096, 128], [1, 4096]]),
                                in_=ap2(k2bf_d.tensor, fc3 * 32 * 128 * 128,
                                        [[4096, 128], [1, 4096]])), kidx)

                        ai = fc3 % 4

                        @block.tensor
                        def _(tensor):
                            sk2.wait(tensor, kidx)
                            tensor.wait_ge(sv.h, t2_m)
                            if fc3 == 0:
                                tensor.wait_ge(sv.h, a2_done_m)
                            ps_wait(tensor, ai)
                            for kc3 in range(32):
                                b2t = k2s[:, kc3 * 128: kc3 * 128 + 128]
                                for pi, at in enumerate((a2h, a2l)):
                                    sp.inc(mm(tensor,
                                        PS[ai][:, 0:TJ], b2t,
                                        at[:, kc3 * TJ:(kc3 + 1) * TJ],
                                        start=(kc3 == 0 and pi == 0), stop=False,
                                        key=("mm3", kidx, kc3)))
                            sp.inc(mm(tensor,
                                PS[ai][:, 0:TJ], nhrow[0:1, :], tb16[0:1, 2 * TJ:3 * TJ],
                                start=False, stop=False, key=("nh3", ti)))
                            sp.inc(mm(tensor,
                                PS[ai][:, 0:TJ], nhrow[0:1, :], tb16[0:1, 3 * TJ:4 * TJ],
                                start=False, stop=True, key=("nh3", ti)))
                        k2_mm_marks[kidx] = sp.n
                        mm3_m = sp.n

                        @block.scalar
                        def _(scalar):
                            scalar.wait_ge(sp.h, mm3_m)
                            if ti >= 1 and fc3 == 0:
                                scalar.wait_ge(sv.h, comb_marks[ti - 1])
                            sa.inc(scalar.activation(
                                sgbuf[:, fc3 * TJ:(fc3 + 1) * TJ], PS[ai][:, 0:TJ],
                                ACTF.Silu, bias=0.0, scale=scal[:, 2:3]))
                        ps_free[ai] = (sa, sa.n)
                    mm3_done[ti] = sp.n
                    sg_m = sa.n

                    tb = ti % 2

                    @block.sync
                    def _(sync):
                        if ti == 0:
                            for k in sorted(sst.targets):
                                sst.wait(sync, k)
                        if ti >= 1:
                            sync.wait_ge(sv.h, comb_marks[ti - 1])
                        ssl.issue(sync.dma_start(
                            out=ap2(stile, 0,
                                    [[NCK * TJ, 128], [TJ, 8], [1, TJ]]),
                            in_=ap2(s_scr.tensor, ti * TJ,
                                    [[T, 128], [128 * T, 8], [1, TJ]])), ti)

                    @block.vector
                    def _(vector):
                        ssl.wait(vector, ti)
                        vector.wait_ge(sa.h, sg_m)
                        for c in range(NCK):
                            inst = vector.tensor_tensor(
                                sgbuf[:, c * TJ:(c + 1) * TJ],
                                sgbuf[:, c * TJ:(c + 1) * TJ],
                                stile[:, c * TJ:(c + 1) * TJ], ALU.mult)
                        sv.inc(inst)
                    p_m = sv.n
                    comb_marks[ti] = p_m

                    for tsub in range(2 if TJ == 256 else 1):
                        xslot = tsub

                        @block.sync
                        def _(sync):
                            if ti >= 1:
                                syw.wait(sync, (ti - 1) * 2 + tsub)
                            sxr.issue(sync.dma_start(
                                out=xres[:, xslot * 1024:(xslot + 1) * 1024],
                                in_=x_d[ti * TJ + tsub * 128: ti * TJ + tsub * 128 + 128, :]),
                                ti * 2 + tsub)

                        @block.tensor
                        def _(tensor):
                            tensor.wait_ge(sv.h, p_m)
                            ps_wait(tensor, 2 + tsub)
                            for c in range(NCK):
                                sp.inc(tensor.transpose(
                                    PS[2 + tsub][:, c * 128:(c + 1) * 128],
                                    sgbuf[:, c * TJ + tsub * 128: c * TJ + tsub * 128 + 128],
                                    ident[:]))
                        tr_m = sp.n
                        tr_marks[(ti, tsub)] = tr_m

                        @block.vector
                        def _(vector):
                            vector.wait_ge(sp.h, tr_m)
                            sxr.wait(vector, ti * 2 + tsub)
                            sv.inc(vector.tensor_tensor(
                                xres[:, xslot * 1024:(xslot + 1) * 1024],
                                xres[:, xslot * 1024:(xslot + 1) * 1024],
                                PS[2 + tsub][:], ALU.add))
                        x1_m = sv.n
                        ps_free[2 + tsub] = (sv, x1_m)

                        @block.sync
                        def _(sync):
                            sync.wait_ge(sv.h, x1_m)
                            syw.issue(sync.dma_start(
                                out=y_d[ti * TJ + tsub * 128: ti * TJ + tsub * 128 + 128, :],
                                in_=xres[:, xslot * 1024:(xslot + 1) * 1024]),
                                ti * 2 + tsub)
                    comb_marks[ti] = sv.n

                @block.sync
                def _(sync):
                    for k in sorted(syw.targets):
                        syw.wait(sync, k)
            return sv.n

        comb_marks = {}
        t1sp_m = [0]
        t2sp_m = [0]
        if upto in ("l0", "ln2", "l1attn", "full"):
            ffn_done = emit_l0_ffn(pj_done)

        s2_scr = (nc.dram_tensor("dbg_s2tok", [T, D], F32, kind="ExternalOutput").ap()
                  if "s2tok" in dbg else
                  nc.dram_tensor("s2_scr", [T, D], F32, kind="Internal").ap())
        h_scr = (nc.dram_tensor("dbg_h_tok", [T, FI], F32, kind="ExternalOutput").ap()
                 if "h_tok" in dbg else
                 nc.dram_tensor("h_scr", [T, FI], F32, kind="Internal").ap())

        def emit_l1_attn(arms, sob):
            ph = contextlib.ExitStack()
            with ph:
                wqstg = ph.enter_context(nc.sbuf_tensor("wqstg", [128, 2 * 1024], F32))
                zq = ph.enter_context(nc.sbuf_tensor("zq", [128, 1024], F32))
                s2stg = ph.enter_context(nc.sbuf_tensor("s2stg", [128, 2 * 1024], F32))
                cq1 = s2stg[:, 0:1024]
                cq2 = s2stg[:, 1024:2048]
                wqT = ph.enter_context(nc.sbuf_tensor("wqT", [128, 2 * 1024], BF16))
                wqTo = ph.enter_context(nc.sbuf_tensor("wqTo", [128, NCK * 1024], BF16))
                scw = ph.enter_context(nc.sbuf_tensor("scw", [128, 48], F32))
                kvbuf = ph.enter_context(nc.sbuf_tensor("kvbuf", [128, NCK * T], F32))
                ktile = ph.enter_context(nc.sbuf_tensor("ktile", [128, T], F32))
                ftile = ph.enter_context(nc.sbuf_tensor("ftile", [128, T], F32))
                rtile = ftile
                zrow = ph.enter_context(nc.sbuf_tensor("zrow", [128, 2048], F32))
                ystg = ph.enter_context(nc.sbuf_tensor("ystg", [128, 512], BF16))
                sww = mkstream("sww")
                sds = mkstream("sds")
                ss2 = mkstream("ss2", nbuf=4)

                @block.scalar
                def _(scalar):
                    scalar.wait_ge(sv.h, ln2_done)
                    sa.inc(scalar.activation(decay_c[:, 0:8], decay_c[:, 0:8],
                                             ACTF.Sigmoid, bias=0.0, scale=1.0))
                dec_m = sa.n

                @block.vector
                def _(vector):
                    vector.wait_ge(sa.h, dec_m)
                    sv.inc(vector.tensor_scalar_max(zrow[:, 0:8], decay_c[:, 0:8], 1e-7))
                clip_m = sv.n

                @block.scalar
                def _(scalar):
                    scalar.wait_ge(sv.h, clip_m)
                    sa.inc(scalar.activation(zrow[:, 8:16], zrow[:, 0:8], ACTF.Ln,
                                             bias=0.0, scale=1.0))
                ldl_m = sa.n

                last_trans_pe = [0]

                def quant_chain(w_dram, ic, sc_col, widx):
                    slot = widx % 2

                    @block.sync
                    def _(sync):
                        if widx >= 2:
                            sync.wait_ge(sv.h, wq_marks[widx - 2])
                        if widx == 0:
                            sync.wait_ge(sa.h, ldl_m)
                        sww.issue(sync.dma_start(
                            out=wqstg[:, slot * 1024:(slot + 1) * 1024],
                            in_=w_dram[ic * 128:(ic + 1) * 128, :]), widx)

                    @block.vector
                    def _(vector):
                        sww.wait(vector, widx)
                        if widx >= 1:
                            vector.wait_ge(sv.h, wq_marks.get(widx - 1, 0))
                        sv.inc(vector.tensor_reduce(
                            scw[:, sc_col:sc_col + 1],
                            wqstg[:, slot * 1024:(slot + 1) * 1024],
                            AX.X, ALU.add, apply_absolute_value=True))
                    r_m = sv.n

                    @block.vector
                    def _(vector):
                        vector.wait_ge(sv.h, r_m)
                        sv.inc(vector.tensor_scalar(
                            scw[:, sc_col:sc_col + 1], scw[:, sc_col:sc_col + 1],
                            0.0009765625, 1e-5, ALU.mult, ALU.max))
                    s_m = sv.n

                    @block.vector
                    def _(vector):
                        vector.wait_ge(sv.h, s_m)
                        sv.inc(vector.reciprocal(scw[:, 44:45], scw[:, sc_col:sc_col + 1]))
                    rc2_m = sv.n

                    @block.vector
                    def _(vector):
                        vector.wait_ge(sv.h, rc2_m)
                        vector.wait_ge(sp.h, last_trans_pe[0])
                        sv.inc(vector.tensor_scalar_mul(
                            zq[:], wqstg[:, slot * 1024:(slot + 1) * 1024],
                            scw[:, 44:45]))
                    z_m = sv.n
                    wq_marks[widx] = z_m

                    @block.vector
                    def _(vector):
                        vector.wait_ge(sv.h, z_m)
                        vector.tensor_single_scalar(cq1, zq[:], 0.5, ALU.is_gt)
                        sv.inc(vector.engine_nop())
                    c1_m = sv.n

                    @block.vector
                    def _(vector):
                        vector.wait_ge(sv.h, c1_m)
                        vector.tensor_single_scalar(cq2, zq[:], -0.5, ALU.is_lt)
                        sv.inc(vector.engine_nop())
                    c2_m = sv.n

                    @block.vector
                    def _(vector):
                        vector.wait_ge(sv.h, c2_m)
                        sv.inc(vector.tensor_sub(zq[:], cq1, cq2))
                    return sv.n

                def quant_trans(chain_m, tgt, tgt_off, prev_mm, tgt_stride=128):
                    @block.tensor
                    def _(tensor):
                        tensor.wait_ge(sv.h, chain_m)
                        if prev_mm is not None:
                            tensor.wait_ge(prev_mm[0].h, prev_mm[1])
                        ps_wait(tensor, 2)
                        ps_wait(tensor, 3)
                        for kc in range(NCK):
                            pa = PS[2] if kc < 4 else PS[3]
                            sp.inc(tensor.transpose(
                                pa[:, (kc % 4) * 128:(kc % 4) * 128 + 128],
                                zq[:, kc * 128:(kc + 1) * 128], ident[:]))
                    t_m = sp.n
                    last_trans_pe[0] = t_m

                    @block.vector
                    def _(vector):
                        vector.wait_ge(sp.h, t_m)
                        inst = None
                        for kc in range(NCK):
                            pa = PS[2] if kc < 4 else PS[3]
                            o = tgt_off + kc * tgt_stride
                            inst = vector.tensor_copy(
                                tgt[:, o: o + 128],
                                pa[:, (kc % 4) * 128:(kc % 4) * 128 + 128])
                        sv.inc(inst)
                    ps_free[2] = (sv, sv.n)
                    ps_free[3] = (sv, sv.n)
                    return sv.n

                def quant_w(w_dram, ic, sc_col, widx, tgt, tgt_off, prev_mm,
                            tgt_stride=128):
                    c_m = quant_chain(w_dram, ic, sc_col, widx)
                    return quant_trans(c_m, tgt, tgt_off, prev_mm, tgt_stride)

                wq_marks = {}
                widx_c = [0]
                ic_marks = {}
                wkey = [0]

                for ic in range(NCK):
                    ck_m = quant_chain(wk_d, ic, 0 + ic, widx_c[0])
                    widx_c[0] += 1
                    wqk_m = quant_trans(ck_m, wqT, 0, None)

                    wkey[0] += 1

                    @block.tensor
                    def _(tensor):
                        tensor.wait_ge(sv.h, max(wqk_m, ln2_done))
                        for j in range(4):
                            ps_wait(tensor, j)
                        for kc in range(NCK):
                            wt = wqT[:, kc * 128:(kc + 1) * 128]
                            for pi, xt in enumerate((xn_hi, xn_lo)):
                                for j in range((T + 511) // 512):
                                    w = min(512, T - j * 512)
                                    sp.inc(mm(tensor,
                                        PS[j][:, 0:w], wt,
                                        xt[:, kc * T + j * 512: kc * T + j * 512 + w],
                                        start=(kc == 0 and pi == 0),
                                        stop=(kc == NCK - 1 and pi == 1),
                                        key=("l1p", wkey[0], kc)))
                    kmm_m = sp.n

                    cv_m = quant_chain(wv_d, ic, 8 + ic, widx_c[0])
                    widx_c[0] += 1

                    @block.vector
                    def _(vector):
                        vector.wait_ge(sp.h, kmm_m)
                        inst = None
                        for j in range((T + 511) // 512):
                            w = min(512, T - j * 512)
                            inst = vector.tensor_copy(ktile[:, j * 512:j * 512 + w],
                                                      PS[j][:, 0:w])
                        sv.inc(inst)
                    ke_m = sv.n
                    for j in range(4):
                        ps_free[j] = (sv, ke_m)

                    wqv_m = quant_trans(cv_m, wqT, 1024, None)

                    @block.vector
                    def _(vector):
                        vector.wait_ge(sv.h, wqv_m)
                        sv.inc(vector.tensor_mul(scw[:, 16 + ic:17 + ic],
                                                 scw[:, 0 + ic:1 + ic],
                                                 scw[:, 8 + ic:9 + ic]))
                    skv_m = sv.n

                    wkey[0] += 1

                    @block.tensor
                    def _(tensor):
                        tensor.wait_ge(sv.h, max(wqv_m, ke_m))
                        for kc in range(NCK):
                            wt = wqT[:, 1024 + kc * 128: 1024 + (kc + 1) * 128]
                            for pi, xt in enumerate((xn_hi, xn_lo)):
                                for j in range((T + 511) // 512):
                                    w = min(512, T - j * 512)
                                    sp.inc(mm(tensor,
                                        PS[j][:, 0:w], wt,
                                        xt[:, kc * T + j * 512: kc * T + j * 512 + w],
                                        start=(kc == 0 and pi == 0),
                                        stop=(kc == NCK - 1 and pi == 1),
                                        key=("l1p", wkey[0], kc)))
                    vmm_m = sp.n

                    cr_m = quant_chain(wr_d, ic, 24 + ic, widx_c[0])
                    widx_c[0] += 1

                    @block.vector
                    def _(vector):
                        vector.wait_ge(sp.h, vmm_m)
                        vector.wait_ge(sv.h, skv_m)
                        if ic >= 1:
                            vector.wait_ge(sv.h, ic_marks[ic - 1])
                        inst = None
                        for j in range((T + 511) // 512):
                            w = min(512, T - j * 512)
                            inst = vector.scalar_tensor_tensor(
                                kvbuf[:, ic * T + j * 512: ic * T + j * 512 + w],
                                ktile[:, j * 512:j * 512 + w],
                                scw[:, 16 + ic:17 + ic],
                                PS[j][:, 0:w], ALU.mult, ALU.mult)
                        sv.inc(inst)
                    kv_m = sv.n
                    for j in range(4):
                        ps_free[j] = (sv, kv_m)

                    @block.vector
                    def _(vector):
                        vector.wait_ge(sv.h, kv_m)
                        sv.inc(vector.tensor_scalar_mul(
                            ftile[:], iota[:, 0:T], zrow[:, 8 + ic:9 + ic]))
                    fe_m = sv.n

                    @block.scalar
                    def _(scalar):
                        scalar.wait_ge(sv.h, fe_m)
                        sa.inc(scalar.activation(ftile[:], ftile[:], ACTF.Exp,
                                                 bias=0.0, scale=1.0))
                    fx_m = sa.n

                    @block.vector
                    def _(vector):
                        vector.wait_ge(sa.h, fx_m)
                        sv.inc(vector.tensor_scalar(ftile[:], ftile[:], 1e10, 1.0,
                                                    ALU.mult, ALU.min))
                    fm_m = sv.n

                    @block.vector
                    def _(vector):
                        vector.wait_ge(sv.h, fm_m)
                        sv.inc(vector.tensor_mul(ftile[:], ftile[:],
                                                 kvbuf[:, ic * T:(ic + 1) * T]))
                    kvp_m = sv.n

                    @block.vector
                    def _(vector):
                        vector.wait_ge(sv.h, kvp_m)
                        dec_b = ap2(decay_c, ic, [[8, 128], [0, T]])
                        sv.inc(vector.tensor_tensor_scan(
                            kvbuf[:, ic * T:(ic + 1) * T], dec_b, ftile[:],
                            0.0, ALU.mult, ALU.add))
                    sc_m = sv.n

                    wqr_m = quant_trans(cr_m, wqT, 0, (sp, vmm_m))

                    wkey[0] += 1

                    @block.tensor
                    def _(tensor):
                        tensor.wait_ge(sv.h, max(wqr_m, kv_m))
                        for kc in range(NCK):
                            wt = wqT[:, kc * 128:(kc + 1) * 128]
                            for pi, xt in enumerate((xn_hi, xn_lo)):
                                for j in range((T + 511) // 512):
                                    w = min(512, T - j * 512)
                                    sp.inc(mm(tensor,
                                        PS[j][:, 0:w], wt,
                                        xt[:, kc * T + j * 512: kc * T + j * 512 + w],
                                        start=(kc == 0 and pi == 0),
                                        stop=(kc == NCK - 1 and pi == 1),
                                        key=("l1p", wkey[0], kc)))
                    rmm_m = sp.n

                    @block.scalar
                    def _(scalar):
                        scalar.wait_ge(sp.h, rmm_m)
                        scalar.wait_ge(sv.h, sc_m)
                        inst = None
                        for j in range((T + 511) // 512):
                            w = min(512, T - j * 512)
                            inst = scalar.activation(
                                rtile[:, j * 512:j * 512 + w], PS[j][:, 0:w],
                                ACTF.Sigmoid, bias=0.0,
                                scale=scw[:, 24 + ic:25 + ic])
                        sa.inc(inst)
                    re_m = sa.n
                    for j in range(4):
                        ps_free[j] = (sa, re_m)

                    @block.vector
                    def _(vector):
                        vector.wait_ge(sa.h, re_m)
                        vector.wait_ge(sv.h, sc_m)
                        sv.inc(vector.tensor_mul(kvbuf[:, ic * T:(ic + 1) * T],
                                                 rtile[:],
                                                 kvbuf[:, ic * T:(ic + 1) * T]))
                    z_m = sv.n

                    @block.scalar
                    def _(scalar):
                        scalar.wait_ge(sv.h, z_m)
                        sa.inc(scalar.activation(ktile[:], kvbuf[:, ic * T:(ic + 1) * T],
                                                 ACTF.Square, bias=0.0, scale=1.0))
                    zs_m = sa.n

                    @block.tensor
                    def _(tensor):
                        tensor.wait_ge(sa.h, zs_m)
                        for j in range((T + 511) // 512):
                            w = min(512, T - j * 512)
                            sp.inc(tensor.matmul(
                                PS[j][0:1, 512:512 + w], onesf[:],
                                ktile[:, j * 512:j * 512 + w],
                                start=(ic == 0), stop=(ic == NCK - 1)))
                    zq_mms = sp.n

                    @block.vector
                    def _(vector):
                        vector.wait_ge(sv.h, z_m)
                        vector.wait_ge(sa.h, zs_m)
                        sv.inc(vector.tensor_scalar_mul(
                            kvbuf[:, ic * T:(ic + 1) * T],
                            kvbuf[:, ic * T:(ic + 1) * T], lnw_c[:, ic:ic + 1]))
                    ic_marks[ic] = sv.n

                y_done = sv.n

                @block.vector
                def _(vector):
                    vector.wait_ge(sp.h, zq_mms)
                    inst = None
                    for j in range((T + 511) // 512):
                        w = min(512, T - j * 512)
                        inst = vector.tensor_copy(zrow[0:1, j * 512:j * 512 + w],
                                                  PS[j][0:1, 512:512 + w])
                    sv.inc(inst)
                zr_m = sv.n

                @block.tensor
                def _(tensor):
                    tensor.wait_ge(sv.h, zr_m)
                    ps_wait(tensor, 0)
                    for tt in range(NT):
                        sp.inc(tensor.transpose(
                            PS[0][:, tt:tt + 1], zrow[0:1, tt * 128:(tt + 1) * 128],
                            ident[0:1, 0:1]))
                at_m = sp.n

                @block.vector
                def _(vector):
                    vector.wait_ge(sp.h, at_m)
                    sv.inc(vector.tensor_scalar(arms[:, 0:NT], PS[0][:, 0:NT],
                                                0.0009765625, 1e-6, ALU.mult, ALU.add))
                ps_free[0] = (sv, sv.n)
                am_m = sv.n

                @block.scalar
                def _(scalar):
                    scalar.wait_ge(sv.h, am_m)
                    sa.inc(scalar.activation(arms[:, 0:NT], arms[:, 0:NT], ACTF.Sqrt,
                                             bias=0.0, scale=1.0))
                as_m = sa.n

                @block.vector
                def _(vector):
                    vector.wait_ge(sa.h, as_m)
                    sv.inc(vector.reciprocal(arms[:, 0:NT], arms[:, 0:NT]))
                arms_m = sv.n

                for ic in range(NCK):
                    quant_w(wo_d, ic, 32 + ic, widx_c[0], wqTo, ic * 128,
                            (sp, zq_mms) if ic == 0 else None, tgt_stride=1024)
                    widx_c[0] += 1
                wo_m = sv.n

                @block.sync
                def _(sync):
                    sync.wait_ge(sv.h, wo_m)
                    for ic in range(NCK):
                        sds.issue(sync.dma_start(
                            out=ap2(zrow, ic * 128, [[2048, 1], [1, 128]]),
                            in_=ap2(scw, 32 + ic, [[48, 128], [1, 1]]),
                            allow_slow_non_contiguous=True), 0)

                @block.tensor
                def _(tensor):
                    sds.wait(tensor, 0)
                    ps_wait(tensor, 1)
                    for j in range(2):
                        sp.inc(tensor.matmul(
                            PS[1][:, j * 512:(j + 1) * 512], ones1r[0:1, :],
                            zrow[0:1, j * 512:(j + 1) * 512], start=True, stop=True))
                sob_mm = sp.n

                @block.vector
                def _(vector):
                    vector.wait_ge(sp.h, sob_mm)
                    sv.inc(vector.tensor_copy(sob[:], PS[1][:]))
                ps_free[1] = (sv, sv.n)
                sob_m = sv.n

                s2_marks = {}
                ymm_marks = {}
                for tt in range(NT):
                    sl = tt % 2
                    for kc in range(NCK):
                        ysl = kc % 2
                        yi = tt * NCK + kc

                        @block.vector
                        def _(vector):
                            vector.wait_ge(sv.h, max(wo_m, y_done))
                            if yi >= 2:
                                vector.wait_ge(sp.h, ymm_marks[yi - 2])
                            sv.inc(vector.tensor_copy(
                                ystg[:, ysl * 256: ysl * 256 + 128],
                                kvbuf[:, kc * T + tt * 128: kc * T + (tt + 1) * 128]))
                        yh_m = sv.n

                        @block.vector
                        def _(vector):
                            vector.wait_ge(sv.h, yh_m)
                            sv.inc(vector.scalar_tensor_tensor(
                                ystg[:, ysl * 256 + 128: ysl * 256 + 256],
                                kvbuf[:, kc * T + tt * 128: kc * T + (tt + 1) * 128],
                                1.0, ystg[:, ysl * 256: ysl * 256 + 128],
                                ALU.mult, ALU.subtract))
                        yl_m = sv.n

                        @block.tensor
                        def _(tensor):
                            tensor.wait_ge(sv.h, yl_m)
                            if kc == 0:
                                ps_wait(tensor, 0)
                                ps_wait(tensor, 2)
                            for pi in range(2):
                                yst = ystg[:, ysl * 256 + pi * 128:
                                           ysl * 256 + pi * 128 + 128]
                                for j in range(2):
                                    sp.inc(mm(tensor,
                                        PS[0 if j == 0 else 2][:, 0:512], yst,
                                        wqTo[:, kc * 1024 + j * 512: kc * 1024 + (j + 1) * 512],
                                        start=(kc == 0 and pi == 0),
                                        stop=(kc == NCK - 1 and pi == 1),
                                        key=("wo", tt, kc, pi)))
                        ymm_marks[yi] = sp.n
                    s2mm_m = sp.n

                    @block.vector
                    def _(vector):
                        vector.wait_ge(sp.h, s2mm_m)
                        if tt >= 2:
                            ss2.wait(vector, tt - 2)
                        vector.tensor_copy(s2stg[:, sl * 1024: sl * 1024 + 512],
                                           PS[0][:, 0:512])
                        sv.inc(vector.tensor_copy(
                            s2stg[:, sl * 1024 + 512: sl * 1024 + 1024],
                            PS[2][:, 0:512]))
                    s2e_m = sv.n
                    ps_free[0] = (sv, s2e_m)
                    ps_free[2] = (sv, s2e_m)

                    @block.sync
                    def _(sync):
                        sync.wait_ge(sv.h, s2e_m)
                        ss2.issue(sync.dma_start(
                            out=s2_scr[tt * 128:(tt + 1) * 128, :],
                            in_=s2stg[:, sl * 1024:(sl + 1) * 1024]), tt)
                    s2_marks[tt] = None

                @block.sync
                def _(sync):
                    for k in sorted(ss2.targets):
                        ss2.wait(sync, k)
            return sv.n, arms_m, sob_m

        if upto in ("ln2", "l1attn", "full"):
            ffn_end = {"sv": sv.n, "sp": sp.n, "sa": sa.n, "sq": sq.n}
            def ln2_entry(eng):
                eng.wait_ge(sv.h, ffn_end["sv"])
                eng.wait_ge(sp.h, ffn_end["sp"])
                eng.wait_ge(sa.h, ffn_end["sa"])

            ln2_done = emit_ln_phase(y_d, src_is_y=True, stats_store=True,
                                     tt_base=100, entry=ln2_entry)
            arms = st.enter_context(nc.sbuf_tensor("arms", [128, NT], F32))
            sob = st.enter_context(nc.sbuf_tensor("sob", [128, 1024], F32))
            if upto != "ln2":
                l1a_done, arms_m, sob_m = emit_l1_attn(arms, sob)

        wdt_scr = nc.dram_tensor("wdt_scr", [2, NFI, 128, 1024], BF16, kind="Internal").ap()

        def emit_l1_ffn(l1a_done, arms_m, sob_m):
            ph = contextlib.ExitStack()
            with ph:
                q_z = ph.enter_context(nc.sbuf_tensor("q_z", [128, 1024], F32))
                q_az = ph.enter_context(nc.sbuf_tensor("q_az", [128, 1024], F32))
                q_c1 = ph.enter_context(nc.sbuf_tensor("q_c1", [128, 1024], F32))
                q_c5 = ph.enter_context(nc.sbuf_tensor("q_c5", [128, 1024], F32))
                q_sc = ph.enter_context(nc.sbuf_tensor("q_sc", [128, 3 * (FI // 16)], F32))
                xtok = ph.enter_context(nc.sbuf_tensor("xtok", [128, 1024], F32))
                hstg = ph.enter_context(nc.sbuf_tensor("hstg", [128, 2048], F32))
                sgu = ph.enter_context(nc.sbuf_tensor("sgu", [128, 512], F32))
                rc6 = ph.enter_context(nc.sbuf_tensor("rc6", [128, 1], F32))
                gs = contextlib.ExitStack()
                wstg2 = gs.enter_context(nc.sbuf_tensor("wstg2", [128, 2 * FI], F32))
                wgT = gs.enter_context(nc.sbuf_tensor("wgT", [128, 2 * 2 * 8 * 512], BF16))
                wuT = gs.enter_context(nc.sbuf_tensor("wuT", [128, 2 * 2 * 8 * 512], BF16))
                sq2 = mkstream("sq2")
                swg = mkstream("swg", nbuf=4)
                shw = mkstream("shw", nbuf=4)
                shr = mkstream("shr")
                swd = mkstream("swd")
                swd2 = mkstream("swd2", nbuf=4)
                ss2r = mkstream("ss2r")
                sx2r = mkstream("sx2r")
                syf = mkstream("syf", nbuf=4)

                @block.vector
                def _(vector):
                    sv.inc(vector.memset(rc6[:], 6.0))
                rc6_0 = sv.n

                @block.vector
                def _(vector):
                    vector.wait_ge(sv.h, rc6_0)
                    sv.inc(vector.reciprocal(rc6[:], rc6[:]))
                rc6_m = sv.n

                vchain = [0]

                def fp4_chain(W, src_ap, dst_ap, entry_marks, scoff=0):
                    NB = W // 16
                    m0 = sv.n

                    @block.vector
                    def _(vector):
                        for s, v in entry_marks:
                            vector.wait_ge(s.h, v)
                        vector.wait_ge(sv.h, chain_free[0])
                        sv.inc(vector.tensor_reduce(
                            q_sc[:, scoff:scoff + NB],
                            bass.AP(tensor=src_ap.tensor, offset=src_ap.offset,
                                    ap=[list(src_ap.ap[0]), [16, NB], [1, 16]]),
                            AX.X, ALU.max, apply_absolute_value=True))
                    m = sv.n

                    @block.vector
                    def _(vector):
                        vector.wait_ge(sv.h, m)
                        sv.inc(vector.tensor_scalar(
                            q_sc[:, scoff:scoff + NB], q_sc[:, scoff:scoff + NB],
                            rc6[:, 0:1], 1e-12, ALU.mult, ALU.max))
                    m = sv.n

                    @block.vector
                    def _(vector):
                        vector.wait_ge(sv.h, m)
                        sv.inc(vector.reciprocal(
                            q_sc[:, scoff + NB:scoff + 2 * NB],
                            q_sc[:, scoff:scoff + NB]))
                    m = sv.n

                    def bc3(col0):
                        return bass.AP(tensor=q_sc, offset=col0,
                                       ap=[[3 * (FI // 16), 128], [1, NB], [0, 16]])

                    @block.vector
                    def _(vector):
                        vector.wait_ge(sv.h, m)
                        sv.inc(vector.tensor_tensor(q_z[:, 0:W], src_ap,
                                                    bc3(scoff + NB), ALU.mult))
                    m = sv.n

                    @block.vector
                    def _(vector):
                        vector.wait_ge(sv.h, m)
                        sv.inc(vector.scalar_tensor_tensor(
                            q_az[:, 0:W], q_z[:, 0:W], -1.0, q_z[:, 0:W],
                            ALU.mult, ALU.max))
                    m = sv.n

                    @block.vector
                    def _(vector):
                        vector.wait_ge(sv.h, m)
                        sv.inc(vector.tensor_single_scalar(q_c1[:, 0:W], q_az[:, 0:W],
                                                           0.25, ALU.is_gt))
                    m = sv.n
                    for thr in (0.75, 1.25, 1.75):
                        @block.vector
                        def _(vector, thr=thr, m=m):
                            vector.wait_ge(sv.h, m)
                            sv.inc(vector.scalar_tensor_tensor(
                                q_c1[:, 0:W], q_az[:, 0:W], thr, q_c1[:, 0:W],
                                ALU.is_gt, ALU.add))
                        m = sv.n

                    @block.vector
                    def _(vector):
                        vector.wait_ge(sv.h, m)
                        sv.inc(vector.tensor_single_scalar(q_c5[:, 0:W], q_az[:, 0:W],
                                                           2.5, ALU.is_gt))
                    m = sv.n

                    @block.vector
                    def _(vector):
                        vector.wait_ge(sv.h, m)
                        sv.inc(vector.scalar_tensor_tensor(
                            q_c5[:, 0:W], q_az[:, 0:W], 3.5, q_c5[:, 0:W],
                            ALU.is_gt, ALU.add))
                    m = sv.n

                    @block.vector
                    def _(vector):
                        vector.wait_ge(sv.h, m)
                        sv.inc(vector.scalar_tensor_tensor(
                            q_c5[:, 0:W], q_c1[:, 0:W], 0.5, q_c5[:, 0:W],
                            ALU.mult, ALU.add))
                    m = sv.n

                    @block.vector
                    def _(vector):
                        vector.wait_ge(sv.h, m)
                        sv.inc(vector.tensor_single_scalar(q_c1[:, 0:W], q_az[:, 0:W],
                                                           5.0, ALU.is_gt))
                    m = sv.n

                    @block.vector
                    def _(vector):
                        vector.wait_ge(sv.h, m)
                        sv.inc(vector.scalar_tensor_tensor(
                            q_c5[:, 0:W], q_c1[:, 0:W], 2.0, q_c5[:, 0:W],
                            ALU.mult, ALU.add))
                    m = sv.n

                    @block.vector
                    def _(vector):
                        vector.wait_ge(sv.h, m)
                        sv.inc(vector.tensor_tensor(q_c5[:, 0:W], q_c5[:, 0:W],
                                                    bc3(scoff), ALU.mult))
                    m = sv.n

                    @block.vector
                    def _(vector):
                        vector.wait_ge(sv.h, m)
                        sv.inc(vector.tensor_single_scalar(q_az[:, 0:W], q_z[:, 0:W],
                                                           0.0, ALU.is_gt))
                    m = sv.n

                    @block.vector
                    def _(vector):
                        vector.wait_ge(sv.h, m)
                        sv.inc(vector.tensor_mul(q_c1[:, 0:W], q_c5[:, 0:W],
                                                 q_az[:, 0:W]))
                    m = sv.n

                    @block.vector
                    def _(vector):
                        vector.wait_ge(sv.h, m)
                        sv.inc(vector.scalar_tensor_tensor(
                            dst_ap, q_c1[:, 0:W], 2.0, q_c5[:, 0:W],
                            ALU.mult, ALU.subtract))
                    chain_free[0] = sv.n
                    return sv.n

                chain_free = [0]

                def tsplit(src, nblk, dst_hi_fn, dst_lo_fn, wait_marks, aidx, src_off=0):
                    for w0 in range(0, nblk, 16):
                        blocks = list(range(w0, min(w0 + 16, nblk)))

                        @block.tensor
                        def _(tensor, blocks=blocks, w0=w0):
                            if w0 == 0:
                                for s, v in wait_marks:
                                    tensor.wait_ge(s.h, v)
                            ps_wait(tensor, aidx)
                            ps_wait(tensor, aidx + 1)
                            for b in blocks:
                                pa = PS[aidx + (b - w0) // 8]
                                po = ((b - w0) % 8) * 128
                                sp.inc(tensor.transpose(
                                    pa[:, po:po + 128],
                                    src[:, src_off + b * 128: src_off + (b + 1) * 128],
                                    ident[:]))
                        t_m = sp.n

                        @block.vector
                        def _(vector, blocks=blocks, w0=w0, t_m=t_m):
                            vector.wait_ge(sp.h, t_m)
                            inst = None
                            for b in blocks:
                                pa = PS[aidx + (b - w0) // 8]
                                po = ((b - w0) % 8) * 128
                                inst = vector.tensor_copy(dst_hi_fn(b), pa[:, po:po + 128])
                            sv.inc(inst)
                        h_m = sv.n

                        @block.vector
                        def _(vector, blocks=blocks, w0=w0, h_m=h_m):
                            vector.wait_ge(sv.h, h_m)
                            inst = None
                            for b in blocks:
                                pa = PS[aidx + (b - w0) // 8]
                                po = ((b - w0) % 8) * 128
                                inst = vector.scalar_tensor_tensor(
                                    dst_lo_fn(b), pa[:, po:po + 128], 1.0,
                                    dst_hi_fn(b), ALU.mult, ALU.subtract)
                            sv.inc(inst)
                        ps_free[aidx] = (sv, sv.n)
                        ps_free[aidx + 1] = (sv, sv.n)
                    return sv.n


                xq_marks = {}
                xstat_marks = {}
                for tt in range(NT):
                    xsl = tt % 2

                    @block.sync
                    def _(sync):
                        if tt == 0:
                            sync.wait_ge(sv.h, l1a_done)
                        if tt >= 2:
                            sync.wait_ge(sv.h, xstat_marks[tt - 2])
                        sq2.issue(sync.dma_start(
                            out=hstg[:, xsl * 1024: xsl * 1024 + 1024],
                            in_=y_d[tt * 128:(tt + 1) * 128, :]), tt)

                    @block.vector
                    def _(vector):
                        sq2.wait(vector, tt)
                        if tt == 0:
                            vector.wait_ge(sv.h, l1a_done)
                        else:
                            vector.wait_ge(sv.h, xq_marks[tt - 1])
                        sv.inc(vector.tensor_scalar(
                            xtok[:], hstg[:, xsl * 1024: xsl * 1024 + 1024],
                            mstats[:, 2 * tt:2 * tt + 1],
                            mstats[:, 2 * tt + 1:2 * tt + 2], ALU.subtract, ALU.mult))
                    xn_m = sv.n
                    xstat_marks[tt] = xn_m

                    q_m = fp4_chain(1024, xtok[:], xtok[:], [(sv, xn_m)])

                    xq_marks[tt] = tsplit(
                        xtok, 8,
                        lambda b, tt=tt: xn_hi[:, b * T + tt * 128: b * T + (tt + 1) * 128],
                        lambda b, tt=tt: xn_lo[:, b * T + tt * 128: b * T + (tt + 1) * 128],
                        [(sv, q_m)], 0)
                xq_done = sv.n

                wd_marks = {}
                for icd in range(NCK):
                    @block.sync
                    def _(sync):
                        if icd >= 2:
                            sync.wait_ge(sv.h, wd_marks[icd - 2])
                        swg.issue(sync.dma_start(
                            out=wstg2[:, (icd % 2) * FI:(icd % 2) * FI + FI],
                            in_=wd_d[icd * 128:(icd + 1) * 128, :]), icd)

                    @block.vector
                    def _(vector):
                        swg.wait(vector, icd)
                        sv.inc(vector.engine_nop())
                    wl_m = sv.n

                    for t0, tw in ((0, 1024), (1024, 1024), (2048, 768)):
                        b = (icd % 2) * FI + t0
                        q_m = fp4_chain(tw, wstg2[:, b:b + tw],
                                        wstg2[:, b:b + tw], [(sv, wl_m)])

                    def dhi(b, icd=icd):
                        return wgT[:, (b % 8) * 512 + (icd % 4) * 128:
                                   (b % 8) * 512 + (icd % 4) * 128 + 128] if False else                                wgT[:, b * 128: b * 128 + 128]

                    def dlo(b, icd=icd):
                        return wuT[:, b * 128: b * 128 + 128]

                    wm = [(sv, q_m)]
                    if icd >= 1:
                        for hl in range(2):
                            pidx = 100 + (icd - 1) * 2 + hl
                            wm.append((swd.sems[pidx % swd.nbuf], swd.targets[pidx]))
                    t_m = tsplit(wstg2, 22, dhi, dlo, wm, 0,
                                 src_off=(icd % 2) * FI)

                    @block.sync
                    def _(sync):
                        sync.wait_ge(sv.h, t_m)
                        for hl, tsrc in ((0, wgT), (1, wuT)):
                            swd.issue(sync.dma_start(
                                out=ap2(wdt_scr.tensor, hl * NFI * 128 * 1024 + icd * 128,
                                        [[1024, 128], [128 * 1024, 22], [1, 128]]),
                                in_=ap2(tsrc, 0, [[2 * 2 * 8 * 512, 128], [128, 22], [1, 128]])),
                                100 + icd * 2 + hl)
                    wd_marks[icd] = sv.n
                wd_done = sv.n

                sgu_guard = [0]
                spans = [(i * 512, min(512, FI - i * 512)) for i in range((FI + 511) // 512)]
                gw_marks = {}
                wcount = [8]
                SPW = 2 * 8 * 512

                def emit_wchain(spi, wi, j):
                    f0, fw = spans[spi]
                    wsrc, tgtT = ((wg_d, wgT), (wu_d, wuT))[wi]
                    po = (spi % 2) * SPW
                    ic2 = f0 // 128 + j
                    widx = wcount[0]
                    wcount[0] += 1

                    @block.sync
                    def _(sync):
                        if widx - 4 in gw_marks:
                            sync.wait_ge(sv.h, gw_marks[widx - 4])
                        elif widx in (8, 9, 10, 11):
                            sync.wait_ge(sv.h, wd_done)
                        swg.issue(sync.dma_start(
                            out=wstg2[:, (widx % 4) * 1024:(widx % 4) * 1024 + 1024],
                            in_=wsrc[ic2 * 128:(ic2 + 1) * 128, :]), widx)

                    @block.vector
                    def _(vector):
                        swg.wait(vector, widx)
                        sv.inc(vector.engine_nop())
                    wl_m = sv.n

                    q_m = fp4_chain(
                        1024,
                        wstg2[:, (widx % 4) * 1024:(widx % 4) * 1024 + 1024],
                        wstg2[:, (widx % 4) * 1024:(widx % 4) * 1024 + 1024],
                        [(sv, wl_m)])
                    return (widx, tgtT, ic2 * 128 - f0, po, q_m)

                def emit_wtsplit(task):
                    widx, tgtT, off, po, q_m = task

                    def dhi2(b, tgtT=tgtT, off=off, po=po):
                        return tgtT[:, po + b * 512 + off: po + b * 512 + off + 128]

                    def dlo2(b, tgtT=tgtT, off=off, po=po):
                        return tgtT[:, po + 8 * 512 + b * 512 + off:
                                    po + 8 * 512 + b * 512 + off + 128]

                    wm2 = [(sv, q_m)]
                    if widx == 8:
                        for k2, tgt in sorted(swd.targets.items()):
                            wm2.append((swd.sems[k2 % swd.nbuf], tgt))
                    gw_marks[widx] = tsplit(wstg2, 8, dhi2, dlo2, wm2, 2,
                                            src_off=(widx % 4) * 1024)
                    return gw_marks[widx]

                span_tasks = [[(wi, j) for wi in range(2)
                               for j in range(spans[s][1] // 128)]
                              for s in range(len(spans))]
                span_ready = {}
                for wi, j in span_tasks[0]:
                    emit_wtsplit(emit_wchain(0, wi, j))
                span_ready[0] = sv.n

                for spi, (f0, fw) in enumerate(spans):
                    nxt = span_tasks[spi + 1] if spi + 1 < len(spans) else []
                    po = (spi % 2) * SPW
                    pend = [emit_wchain(spi + 1, wi, j) for wi, j in nxt[0:4]]
                    nchained = len(pend)
                    for tt in range(NT):
                        p = tt % 2

                        @block.tensor
                        def _(tensor):
                            tensor.wait_ge(sv.h, max(span_ready[spi], xq_done))
                            ps_wait(tensor, p)
                            for kc in range(NCK):
                                sh = xn_hi[:, kc * T + tt * 128: kc * T + (tt + 1) * 128]
                                sl = xn_lo[:, kc * T + tt * 128: kc * T + (tt + 1) * 128]
                                whg = wgT[:, po + kc * 512: po + kc * 512 + fw]
                                wlg = wgT[:, po + 8 * 512 + kc * 512:
                                          po + 8 * 512 + kc * 512 + fw]
                                whu = wuT[:, po + kc * 512: po + kc * 512 + fw]
                                wlu = wuT[:, po + 8 * 512 + kc * 512:
                                          po + 8 * 512 + kc * 512 + fw]
                                seq = [(sh, whg, 0, kc == 0, False),
                                       (sh, wlg, 0, False, False),
                                       (sh, whu, 1, kc == 0, False),
                                       (sh, wlu, 1, False, False),
                                       (sl, whg, 0, False, kc == NCK - 1),
                                       (sl, whu, 1, False, kc == NCK - 1)]
                                for sop, mop, gi, st_, sp_ in seq:
                                    sp.inc(mm(tensor,
                                        PS[p][:, gi * 512: gi * 512 + fw], sop, mop,
                                        start=st_, stop=sp_,
                                        key=("gu", spi, tt, kc, sop is sl)))
                        gu_m = sp.n

                        @block.scalar
                        def _(scalar):
                            scalar.wait_ge(sp.h, gu_m)
                            if sgu_guard[0]:
                                scalar.wait_ge(sv.h, sgu_guard[0])
                            sa.inc(scalar.activation(sgu[:, 0:fw], PS[p][:, 0:fw],
                                                     ACTF.Silu, bias=0.0, scale=1.0))
                        si_m = sa.n

                        kpr = spi * NT + tt
                        hslot = kpr % 4

                        @block.vector
                        def _(vector):
                            vector.wait_ge(sa.h, si_m)
                            if kpr >= 4:
                                shw.wait(vector, kpr - 4)
                            sv.inc(vector.tensor_tensor(
                                hstg[:, hslot * 512: hslot * 512 + fw],
                                sgu[:, 0:fw], PS[p][:, 512:512 + fw], ALU.mult))
                        hv_m = sv.n
                        sgu_guard[0] = hv_m
                        ps_free[p] = (sv, hv_m)

                        @block.sync
                        def _(sync):
                            sync.wait_ge(sv.h, hv_m)
                            shw.issue(sync.dma_start(
                                out=ap2(h_scr.tensor, tt * 128 * FI + f0,
                                        [[FI, 128], [1, fw]]),
                                in_=hstg[:, hslot * 512: hslot * 512 + fw]),
                                spi * NT + tt)

                        if 0 <= tt - 2 < len(nxt):
                            emit_wtsplit(pend.pop(0))
                            if nchained < len(nxt):
                                wi, j = nxt[nchained]
                                pend.append(emit_wchain(spi + 1, wi, j))
                                nchained += 1
                    while pend:
                        emit_wtsplit(pend.pop(0))
                        if nchained < len(nxt):
                            wi, j = nxt[nchained]
                            pend.append(emit_wchain(spi + 1, wi, j))
                            nchained += 1
                    if nxt:
                        span_ready[spi + 1] = sv.n
                gu_done = sv.n
                gu_mm_final = sp.n
                gs.close()

                NG = min(4, NT)
                if NCK * T >= NG * NFI * 128:
                    hqh, hql = xn_hi, xn_lo
                else:
                    hqh = ph.enter_context(nc.sbuf_tensor("hqh", [128, NG * NFI * 128], BF16))
                    hql = ph.enter_context(nc.sbuf_tensor("hql", [128, NG * NFI * 128], BF16))
                hstg2 = ph.enter_context(nc.sbuf_tensor("hstg2", [128, 2 * FI], F32))
                wdk = ph.enter_context(nc.sbuf_tensor("wdk", [128, 4 * 1024], BF16))
                fstg = ph.enter_context(nc.sbuf_tensor("fstg", [128, 4 * 1024], F32))
                grp_marks = {}
                hq_ev = {}
                x2_marks = {}
                NGRP = NT // NG

                def emit_hchain(grp, tg, extra=()):
                    tt = grp * NG + tg
                    hsl = tg % 2

                    @block.sync
                    def _(sync):
                        if grp == 0 and tg == 0:
                            for k in sorted(shw.targets):
                                shw.wait(sync, k)
                            sync.wait_ge(sp.h, gu_mm_final)
                            sync.wait_ge(sv.h, gu_done)
                        for s, v in extra:
                            sync.wait_ge(s.h, v)
                        shr.issue(sync.dma_start(
                            out=hstg2[:, hsl * FI: hsl * FI + FI],
                            in_=h_scr[tt * 128:(tt + 1) * 128, :]), tt)

                    @block.vector
                    def _(vector):
                        shr.wait(vector, tt)
                        if grp == 0 and tg == 0:
                            vector.wait_ge(sv.h, gu_done)
                        sv.inc(vector.engine_nop())
                    hl_m = sv.n
                    q_m = hl_m
                    for t0, tw in ((0, 1024), (1024, 1024), (2048, 768)):
                        b = hsl * FI + t0
                        q_m = fp4_chain(tw, hstg2[:, b:b + tw],
                                        hstg2[:, b:b + tw], [(sv, hl_m)])
                    return q_m

                def emit_htsplit(grp, tg, q_m, aidx, guard_mm):
                    hsl = tg % 2

                    def hqhi(b, tg=tg):
                        return hqh[:, tg * NFI * 128 + b * 128:
                                   tg * NFI * 128 + (b + 1) * 128]

                    def hqlo(b, tg=tg):
                        return hql[:, tg * NFI * 128 + b * 128:
                                   tg * NFI * 128 + (b + 1) * 128]

                    hq_ev[(grp, tg)] = tsplit(hstg2, NFI, hqhi, hqlo,
                                              [(sv, q_m)] + list(guard_mm), aidx,
                                              src_off=hsl * FI)
                    return hq_ev[(grp, tg)]

                def serial_group(grp, guard_mm):
                    for tg in range(NG):
                        extra = []
                        if tg >= 2:
                            extra.append((sv, hq_ev[(grp, tg - 2)]))
                        elif grp > 0:
                            extra.append((sv, hq_ev[(grp - 1, tg + NG - 2)]))
                        q = emit_hchain(grp, tg, extra)
                        emit_htsplit(grp, tg, q, (tg % 2) * 2, guard_mm)

                serial_group(0, [(sp, gu_mm_final)])

                for grp in range(NGRP):
                    nxtg = grp + 1 if grp + 1 < NGRP else None
                    pendc = {}
                    if nxtg is not None and NG == 4:
                        pendc[0] = emit_hchain(nxtg, 0, [(sv, hq_ev[(grp, 2)])])
                        pendc[1] = emit_hchain(nxtg, 1, [(sv, hq_ev[(grp, 3)])])
                    hq_m = max(hq_ev[(grp, t)] for t in range(NG))

                    dn_m = 0
                    for kc in range(NFI):
                        ksl = kc % 2

                        @block.sync
                        def _(sync):
                            if grp == 0 and kc == 0:
                                for k in sorted(swd.targets):
                                    swd.wait(sync, k)
                                sync.wait_ge(sp.h, gu_mm_final)
                                sync.wait_ge(sv.h, gu_done)
                            if kc >= 2:
                                sync.wait_ge(sp.h, dn_kc_marks[kc - 2])
                            elif grp > 0:
                                sync.wait_ge(sp.h, grp_mm_marks[grp - 1])
                            swd.issue(sync.dma_start(
                                out=wdk[:, ksl * 2 * 1024:
                                        ksl * 2 * 1024 + 1024],
                                in_=ap2(wdt_scr.tensor, kc * 128 * 1024,
                                        [[1024, 128], [1, 1024]])),
                                200 + grp * NFI + kc)

                        @block.scalar
                        def _(scalar):
                            if grp == 0 and kc == 0:
                                for k in sorted(swd.targets):
                                    if k < 200:
                                        swd.wait(scalar, k)
                                scalar.wait_ge(sp.h, gu_mm_final)
                                scalar.wait_ge(sv.h, gu_done)
                            if kc >= 2:
                                scalar.wait_ge(sp.h, dn_kc_marks[kc - 2])
                            elif grp > 0:
                                scalar.wait_ge(sp.h, grp_mm_marks[grp - 1])
                            swd2.issue(scalar.dma_start(
                                out=wdk[:, (ksl * 2 + 1) * 1024:
                                        (ksl * 2 + 1) * 1024 + 1024],
                                in_=ap2(wdt_scr.tensor,
                                        NFI * 128 * 1024 + kc * 128 * 1024,
                                        [[1024, 128], [1, 1024]])),
                                grp * NFI + kc)

                        @block.tensor
                        def _(tensor):
                            swd.wait(tensor, 200 + grp * NFI + kc)
                            swd2.wait(tensor, grp * NFI + kc)
                            tensor.wait_ge(sv.h, hq_m)
                            if kc == 0:
                                for j in range(4):
                                    ps_wait(tensor, j)
                            wh = wdk[:, ksl * 2 * 1024: ksl * 2 * 1024 + 1024]
                            wl = wdk[:, (ksl * 2 + 1) * 1024: (ksl * 2 + 1) * 1024 + 1024]
                            for tg in range(NG):
                                sh = hqh[:, tg * NFI * 128 + kc * 128:
                                         tg * NFI * 128 + (kc + 1) * 128]
                                sl = hql[:, tg * NFI * 128 + kc * 128:
                                         tg * NFI * 128 + (kc + 1) * 128]
                                seq = [(sh, wh, kc == 0, False), (sh, wl, False, False),
                                       (sl, wh, False, kc == NFI - 1)]
                                for sop, mop, st_, sp_ in seq:
                                    for j in range(2):
                                        sp.inc(mm(tensor,
                                            PS[tg][:, j * 512:(j + 1) * 512],
                                            sop, mop[:, j * 512:(j + 1) * 512],
                                            start=st_, stop=sp_,
                                            key=("dn", grp, kc, tg, sop is sl)))
                        dn_kc_marks[kc] = sp.n
                    grp_mm_marks[grp] = sp.n
                    dn_m = sp.n

                    for tg in range(NG):
                        tt = grp * NG + tg
                        csl = tt % 2

                        @block.scalar
                        def _(scalar):
                            scalar.wait_ge(sp.h, dn_m)
                            sa.inc(scalar.activation(fstg[:, tg * 1024:(tg + 1) * 1024],
                                                     PS[tg][:], ACTF.Silu,
                                                     bias=0.0, scale=1.0))
                        g2_m = sa.n
                        ps_free[tg] = (sa, g2_m)

                        @block.sync
                        def _(sync):
                            sync.wait_ge(sp.h, dn_m)
                            if tt < 2:
                                for k in sorted(shw.targets):
                                    shw.wait(sync, k)
                                sync.wait_ge(sv.h, gu_done)
                            if tt >= 2:
                                syf.wait(sync, tt - 2)
                            if tt >= 1:
                                sync.wait_ge(sv.h, x2_marks[tt - 1])
                            ss2r.issue(sync.dma_start(
                                out=hstg[:, csl * 1024: csl * 1024 + 1024],
                                in_=s2_scr[tt * 128:(tt + 1) * 128, :]), tt)
                            sx2r.issue(sync.dma_start(
                                out=xtok[:, 0:1024],
                                in_=y_d[tt * 128:(tt + 1) * 128, :]), tt)

                        @block.vector
                        def _(vector):
                            ss2r.wait(vector, tt)
                            vector.wait_ge(sa.h, g2_m)
                            vector.wait_ge(sv.h, sob_m)
                            vector.tensor_mul(
                                hstg[:, csl * 1024: csl * 1024 + 1024],
                                hstg[:, csl * 1024: csl * 1024 + 1024], sob[:])
                            sv.inc(vector.engine_nop())
                        c1_m = sv.n

                        @block.vector
                        def _(vector):
                            vector.wait_ge(sv.h, c1_m)
                            sv.inc(vector.tensor_mul(
                                hstg[:, csl * 1024: csl * 1024 + 1024],
                                hstg[:, csl * 1024: csl * 1024 + 1024],
                                fstg[:, tg * 1024:(tg + 1) * 1024]))
                        c2_m = sv.n

                        @block.vector
                        def _(vector):
                            sx2r.wait(vector, tt)
                            vector.wait_ge(sv.h, c2_m)
                            vector.wait_ge(sv.h, arms_m)
                            sv.inc(vector.scalar_tensor_tensor(
                                hstg[:, csl * 1024: csl * 1024 + 1024],
                                hstg[:, csl * 1024: csl * 1024 + 1024],
                                arms[:, tt:tt + 1],
                                xtok[:, 0:1024],
                                ALU.mult, ALU.add))
                        x2_marks[tt] = sv.n
                        x2_m = sv.n

                        @block.sync
                        def _(sync):
                            sync.wait_ge(sv.h, x2_m)
                            syf.issue(sync.dma_start(
                                out=y_d[tt * 128:(tt + 1) * 128, :],
                                in_=hstg[:, csl * 1024: csl * 1024 + 1024]), tt)

                        if nxtg is not None and NG == 4:
                            if tg == 1:
                                emit_htsplit(nxtg, 0, pendc[0], 0, [(sp, dn_m)])
                            elif tg == 2:
                                pendc[2] = emit_hchain(
                                    nxtg, 2, [(sv, hq_ev[(nxtg, 0)])])
                            elif tg == 3:
                                emit_htsplit(nxtg, 1, pendc[1], 2, [(sp, dn_m)])
                                pendc[3] = emit_hchain(
                                    nxtg, 3, [(sv, hq_ev[(nxtg, 1)])])
                                emit_htsplit(nxtg, 2, pendc[2], 0, [(sp, dn_m)])
                                emit_htsplit(nxtg, 3, pendc[3], 2, [(sp, dn_m)])
                    if nxtg is not None and NG != 4:
                        serial_group(nxtg, [(sp, dn_m)])
                    grp_marks[grp] = sv.n

                @block.sync
                def _(sync):
                    for k in sorted(syf.targets):
                        syf.wait(sync, k)
            return sv.n

        if upto == "full":
            dn_kc_marks = {}
            grp_mm_marks = {}
            l1f_done = emit_l1_ffn(l1a_done, arms_m, sob_m)

        if upto == "ln1":
            pass

        @block.sync
        def _(sync):
            sync.wait_ge(so_.h, so_.n)
            for strm in all_streams:
                for s in strm.sems:
                    if s.n:
                        sync.wait_ge(s.h, s.n)

    return nc



def _host_prep(inputs):
    f32 = np.float32
    x = np.ascontiguousarray(np.asarray(inputs["L0_attn_x"], f32))
    wqkv = np.asarray(inputs["L0_attn_w_qkv"], f32)
    wproj = np.asarray(inputs["L0_attn_w_proj"], f32)
    k0 = np.asarray(inputs["L0_ffn_k0"]).view(np.uint8)
    k1 = np.asarray(inputs["L0_ffn_k1"]).view(np.uint8)
    k2 = np.asarray(inputs["L0_ffn_k2"]).view(np.uint8)
    com = {
        "wqkv_t": np.ascontiguousarray(wqkv.reshape(8, 128, 24, 128).transpose(2, 0, 1, 3)),
        "wproj_t": np.ascontiguousarray(wproj.reshape(8, 128, 8, 128).transpose(2, 0, 1, 3)),
        "k0_t": np.ascontiguousarray(k0.reshape(16, 128, 32, 128).transpose(2, 1, 0, 3)),
        "k1_t": np.ascontiguousarray(k1.reshape(32, 128, 32, 128).transpose(2, 1, 0, 3)),
        "k2_t": np.ascontiguousarray(k2.reshape(32, 128, 8, 128).transpose(2, 1, 0, 3)),
        "s0": np.asarray(inputs["L0_ffn_s0"], f32).reshape(1, 1),
        "s1": np.asarray(inputs["L0_ffn_s1"], f32).reshape(1, 1),
        "s2": np.asarray(inputs["L0_ffn_s2"], f32).reshape(1, 1),
        "wr": np.ascontiguousarray(np.asarray(inputs["L1_attn_Wr"], f32)),
        "wk": np.ascontiguousarray(np.asarray(inputs["L1_attn_Wk"], f32)),
        "wv": np.ascontiguousarray(np.asarray(inputs["L1_attn_Wv"], f32)),
        "wo": np.ascontiguousarray(np.asarray(inputs["L1_attn_Wo"], f32)),
        "decay8": np.ascontiguousarray(np.asarray(inputs["L1_attn_decay"], f32).reshape(8, 128)),
        "lnw8": np.ascontiguousarray(np.asarray(inputs["L1_attn_ln_w"], f32).reshape(8, 128)),
        "wg": np.ascontiguousarray(np.asarray(inputs["L1_ffn_w_gate"], f32)),
        "wu": np.ascontiguousarray(np.asarray(inputs["L1_ffn_w_up"], f32)),
        "wd": np.ascontiguousarray(np.asarray(inputs["L1_ffn_w_down"], f32)),
        "identf": np.eye(128, dtype=f32),
        "identb": np.eye(128, dtype=f32).astype(ml_dtypes.bfloat16),
        "iota": np.ascontiguousarray(np.broadcast_to(np.arange(2048, dtype=f32), (128, 2048))),
        "masku": np.triu(np.ones((128, 128), f32)),
        "blk2": np.ascontiguousarray(
            np.stack([np.r_[np.ones(64), np.zeros(64)],
                      np.r_[np.zeros(64), np.ones(64)]], axis=1).astype(ml_dtypes.bfloat16)),
        "sel2": np.ascontiguousarray(
            np.stack([np.r_[np.ones(64), np.zeros(64)],
                      np.r_[np.zeros(64), np.ones(64)]], axis=0).astype(f32)),
    }
    return x, com


def kernel(**inputs):
    x, com = _host_prep(inputs)
    nc = build()
    in_maps = [dict(com, x=np.ascontiguousarray(x[c])) for c in range(8)]
    res = run_bass_kernel_spmd(nc, in_maps, core_ids=list(range(8)))
    return np.stack([np.asarray(r["y"], np.float32) for r in res.results])
